# version 02
# baseline (speedup 1.0000x reference)
import numpy as np
import ml_dtypes
from contextlib import ExitStack
import concourse.bass as bass
import concourse.mybir as mybir
from concourse.bass_utils import run_bass_kernel_spmd

F32 = mybir.dt.float32
BF16 = mybir.dt.bfloat16
AF = mybir.ActivationFunctionType
ALU = mybir.AluOpType
BFDT = ml_dtypes.bfloat16

NCORES = 8
BLOC = 4
DBG_ROUTE = False
T = 2048
NTOK = BLOC * T
D = 1024
NTT = T // 128
KC = D // 128
ES = 512
NCH = T // ES


class Prog:
    def __init__(self):
        self.nc = bass.Bass("TRN2", target_bir_lowering=False, debug=False)
        self._sems = {}
        self.extra_inputs = {}
        self.extra_aps = {}

    def sem(self, name):
        if name not in self._sems:
            self._sems[name] = self.nc.alloc_semaphore(name)
        return self._sems[name]

    def reg_const(self, tag, vals):
        h = self.nc.alloc_sbuf_tensor(tag + "_cpoolP", [128, 16], F32)
        arr = np.zeros((128, 16), np.float32)
        for i, v in enumerate(vals):
            arr[:, i] = v
            self.nc.const_aps.register(F32, float(v), h.ap()[:, i:i + 1])
        ap_in = self.const_input(tag + "_cpoolin", arr)
        return h.ap(), ap_in

    def const_input(self, name, arr):
        if name in self.extra_inputs:
            return self.extra_aps[name]
        arr = np.ascontiguousarray(arr)
        self.extra_inputs[name] = arr
        dt = {np.dtype(np.float32): F32, np.dtype(BFDT): BF16}[arr.dtype]
        ap = self.nc.dram_tensor(name, list(arr.shape), dt, kind="ExternalInput").ap()
        self.extra_aps[name] = ap
        return ap


class SelfSync:

    def __init__(self, eng, sem):
        self._e = eng
        self._s = sem
        self._n = 0
        self._last = None
        self._edge = None

    def wait_ge(self, sem, val):
        return self._e.wait_ge(sem, val)

    def sig(self, sem, val):
        assert self._last is not None
        self._last.then_inc(sem, 1)
        self._edge = (sem, val)
        self._last = None

    def _advance(self):
        if self._last is not None:
            self._last.then_inc(self._s, 1)
            self._n += 1
            self._edge = (self._s, self._n)
            self._last = None
        if self._edge is not None:
            self._e.wait_ge(*self._edge)
            self._edge = None

    def __getattr__(self, name):
        f = getattr(self._e, name)

        def wrap(*a, **k):
            self._advance()
            ins = f(*a, **k)
            self._last = ins
            return ins
        return wrap


def emit_l0_attn_a(p, xin, xout, w_qkv, b_qkv, w_out, b_out, tag="l0a"):
    nc = p.nc
    M_STAB = 32.0
    SCALE = 0.125

    consts = np.zeros((128, 8), np.float32)
    consts[:, 0] = -M_STAB
    cpool_in = p.const_input(tag + "_cpool", consts)
    ones_row = p.const_input(tag + "_ones", np.ones((64, 128), np.float32))
    eyeb_in = p.const_input(tag + "_eyeb", np.eye(128, dtype=BFDT))
    eyef_in = p.const_input(tag + "_eyef", np.eye(128, dtype=np.float32))
    masks = np.zeros((4, 128, ES), BFDT)
    sl = np.arange(128)[:, None]
    tl = np.arange(ES)[None, :]
    for j in range(4):
        masks[j] = (128 * j + sl >= tl).astype(BFDT)
    mask_in = p.const_input(tag + "_masks", np.ascontiguousarray(masks.transpose(1, 0, 2).reshape(128, 4 * ES)))

    wconv = nc.dram_tensor(tag + "_wconv", [128, KC * 3072], BF16).ap()
    oconv = nc.dram_tensor(tag + "_oconv", [128, KC * 1024], BF16).ap()

    sm = {n: p.sem(tag + n) for n in
          ["w", "wc", "x", "sq", "vb", "sd", "ln", "tp", "xT", "qk", "qe",
           "vn", "ve", "sc", "pf", "pm", "av", "li", "rp", "rpc", "oe",
           "pj", "yd", "st", "pro", "rc", "wm", "cvt", "ye",
           "x0", "x1", "wm0", "wm1", "st0", "st1", "va", "vch", "sch"]}

    fl_tiles = [list(range(4 * tc, NTT)) for tc in range(NCH)]
    fl_off = [sum(len(fl_tiles[c]) for c in range(tc)) for tc in range(NCH)]
    TPB = sum(len(x) for x in fl_tiles)

    NBLK = 20

    with ExitStack() as _es:
        def A(nm, shape, dt):
            return _es.enter_context(nc.sbuf_tensor(tag + nm, shape, dt))

        cpool = A("cpool", [128, 8], F32)
        bcol = A("bcol", [128, 24], F32)
        bvrep = A("bvrep", [128, 1024], F32)
        borep = A("borep", [128, 1024], F32)
        onesr = A("onesr", [64, 128], F32)
        eyeb = A("eyeb", [128, 128], BF16)
        eyef = A("eyef", [128, 128], F32)
        rows = A("rows", [64, 1024], F32)
        maskt = A("maskt", [128, 4 * ES], BF16)
        btmp = A("btmp", [24, 128], F32)
        stg = A("stg", [128, 3072], BF16)
        wstream = A("wstream", [128, 2 * 4096], BF16)
        xtile = A("xtile", [128, 2 * 1024], F32)
        stat = A("stat", [128, 16], F32)
        sjunk = A("sjunk", [128, 1024], F32)
        vjunk = A("vjunk", [128, 1024], F32)
        xln = A("xln", [128, 2 * 1024], BF16)
        xT = A("xT", [128, KC * T], BF16)
        qkT = A("qkT", [128, 16 * T], BF16)
        vb = A("vb", [128, NTT * 1040], BF16)
        ptile = A("pt", [128, 2 * ES], BF16)
        repsb = A("repsb", [64, 2 * ES], F32)
        ytile = A("ytile", [128, 2 * 1024], F32)
        ofm = xT
        linv = rows
        ps_s = _es.enter_context(nc.psum_tensor(tag + "ps_s", [128, 2 * ES], F32))
        ps_o = _es.enter_context(nc.psum_tensor(tag + "ps_o", [128, 2 * ES], F32))
        ps_q = _es.enter_context(nc.psum_tensor(tag + "ps_q", [128, 2 * ES], F32))
        ps_r = _es.enter_context(nc.psum_tensor(tag + "ps_r", [128, ES], F32))
        ps_tp = _es.enter_context(nc.psum_tensor(tag + "ps_tp", [128, 1024], BF16))
        ps_b = ps_r

        nc.const_aps.register(F32, -M_STAB, cpool[:, 0:1])
        wcv = wconv.rearrange("p (kc j) -> p kc j", kc=KC)
        ocv = oconv.rearrange("p (kc j) -> p kc j", kc=KC)

        def wblk_src(bi):
            if bi < 16:
                return wcv[:, :, bi * 128:(bi + 1) * 128]
            if bi < 18:
                n = bi - 16
                return wcv[:, :, 2048 + n * ES:2048 + (n + 1) * ES]
            n = bi - 18
            return ocv[:, :, n * ES:(n + 1) * ES]

        def wslot(bi, parity):
            cols = 1024 if bi < 16 else 4096
            return wstream[:, parity * 4096:parity * 4096 + cols].rearrange(
                "p (kc j) -> p kc j", kc=KC)

        with nc.Block(tag) as blk:
            @blk.sync
            def _(s):
                s.dma_start(out=btmp[:, :], in_=b_qkv.rearrange("(m q) -> m q", q=128)).then_inc(sm["w"], 16)
                s.dma_start(out=onesr[:, :], in_=ones_row[:, :]).then_inc(sm["w"], 16)
                s.dma_start(out=maskt[:, :], in_=mask_in[:, :]).then_inc(sm["w"], 16)
                s.dma_start(out=rows[0:1, :], in_=b_out.rearrange("(a q) -> a q", a=1)).then_inc(sm["w"], 16)
                s.dma_start(out=rows[32:33, :], in_=b_qkv.rearrange("(a q) -> a q", a=3)[2:3, :]).then_inc(sm["w"], 16)
                s.dma_start(out=cpool[:, :], in_=cpool_in[:, :]).then_inc(sm["w"], 16)
                s.dma_start(out=eyeb[:, :], in_=eyeb_in[:, :]).then_inc(sm["w"], 16)
                s.dma_start(out=eyef[:, :], in_=eyef_in[:, :]).then_inc(sm["w"], 16)
                for b in range(BLOC):
                    t0 = b * T
                    for tt in range(NTT):
                        k = b * NTT + tt
                        if k >= 2:
                            s.wait_ge(sm["ln"], k - 1)
                        if b > 0 and tt < 2:
                            s.wait_ge(sm["yd"], b * NTT)
                        s.dma_start(out=xtile[:, (tt % 2) * 1024:(tt % 2 + 1) * 1024],
                                    in_=xin[t0 + tt * 128:t0 + (tt + 1) * 128, :]).then_inc(sm["x%d" % (tt % 2)], 16)
                    for bi in range(NBLK):
                        g = b * NBLK + bi
                        if g >= 2:
                            pb_, pbi = divmod(g - 2, NBLK)
                            if pbi < 16:
                                s.wait_ge(sm["qk"], pb_ * 64 + (pbi + 1) * NCH)
                            elif pbi < 18:
                                s.wait_ge(sm["vn"], pb_ * 32 + (pbi - 16 + 1) * 16)
                            else:
                                s.wait_ge(sm["pj"], pb_ * 32 + 31 + (pbi - 18))
                        if g == 0:
                            s.wait_ge(sm["cvt"], 16 * 16)
                        s.dma_start(out=wslot(bi, g % 2), in_=wblk_src(bi)).then_inc(sm["wm%d" % (g % 2)], 16)
                    for tt in range(NTT):
                        k = b * NTT + tt
                        if tt < 2:
                            s.wait_ge(sm["ln"], (b + 1) * NTT)
                        else:
                            s.wait_ge(sm["yd"], k - 1)
                        s.dma_start(out=xtile[:, (tt % 2) * 1024:(tt % 2 + 1) * 1024],
                                    in_=xin[t0 + tt * 128:t0 + (tt + 1) * 128, :]).then_inc(sm["x%d" % (tt % 2)], 16)
                        if tt >= 2:
                            st_t = tt - 2
                            s.wait_ge(sm["yd"], b * NTT + st_t + 1)
                            s.dma_start(out=xout[t0 + st_t * 128:t0 + (st_t + 1) * 128, :],
                                        in_=ytile[:, (st_t % 2) * 1024:(st_t % 2 + 1) * 1024]).then_inc(sm["st%d" % (st_t % 2)], 16)
                    for st_t in (NTT - 2, NTT - 1):
                        s.wait_ge(sm["yd"], b * NTT + st_t + 1)
                        s.dma_start(out=xout[t0 + st_t * 128:t0 + (st_t + 1) * 128, :],
                                    in_=ytile[:, (st_t % 2) * 1024:(st_t % 2 + 1) * 1024]).then_inc(sm["st%d" % (st_t % 2)], 16)
                s.wait_ge(sm["st0"], 16 * (BLOC * NTT // 2))
                s.wait_ge(sm["st1"], 16 * (BLOC * NTT // 2))

            @blk.gpsimd
            def _(g):
                cv = 0
                for kc in range(KC):
                    if cv > 0:
                        g.wait_ge(sm["cvt"], cv * 16)
                    g.dma_start(out=stg[:, :], in_=w_qkv[kc * 128:(kc + 1) * 128, :]).then_inc(sm["wc"], 16)
                    cv += 1
                    g.wait_ge(sm["wc"], cv * 16)
                    g.dma_start(out=wcv[:, kc, :], in_=stg[:, :]).then_inc(sm["cvt"], 16)
                for kc in range(KC):
                    g.wait_ge(sm["cvt"], cv * 16)
                    g.dma_start(out=stg[:, 0:1024], in_=w_out[kc * 128:(kc + 1) * 128, :]).then_inc(sm["wc"], 16)
                    cv += 1
                    g.wait_ge(sm["wc"], cv * 16)
                    g.dma_start(out=ocv[:, kc, :], in_=stg[:, 0:1024]).then_inc(sm["cvt"], 16)

            @blk.tensor
            def _(t):
                t.wait_ge(sm["w"], 16 * 8)
                t.transpose(ps_b[0:128, 0:24], btmp[:, :], eyef[0:24, 0:24])
                for half in range(2):
                    t.matmul(ps_q[:, half * ES:(half + 1) * ES], onesr[0:1, :],
                             rows[0:1, half * ES:(half + 1) * ES], start=True, stop=True)
                t.matmul(ps_o[:, 0:ES], onesr[32:33, :],
                         rows[32:33, 0:ES], start=True, stop=True)
                t.matmul(ps_o[:, ES:2 * ES], onesr[32:33, :],
                         rows[32:33, ES:2 * ES], start=True, stop=True).then_inc(sm["pro"], 1)
                t.wait_ge(sm["rc"], 1)
                for b in range(BLOC):
                    for tt in range(NTT):
                        k = b * NTT + tt
                        t.wait_ge(sm["ln"], k + 1)
                        if k >= 1:
                            t.wait_ge(sm["xT"], k)
                        for kc in range(KC):
                            ins = t.transpose(ps_tp[:, kc * 128:(kc + 1) * 128],
                                              xln[:, (tt % 2) * 1024 + kc * 128:(tt % 2) * 1024 + (kc + 1) * 128],
                                              eyeb[:, :])
                        ins.then_inc(sm["tp"], 1)
                    t.wait_ge(sm["xT"], (b + 1) * NTT)
                    for m in range(16):
                        g = b * NBLK + m
                        t.wait_ge(sm["wm%d" % (g % 2)], 16 * (g // 2 + 1))
                        ws = wslot(m, g % 2)
                        for tc in range(NCH):
                            mmcnt = m * NCH + tc
                            if mmcnt >= 2:
                                t.wait_ge(sm["qe"], b * 64 + mmcnt - 1)
                            for kc in range(KC):
                                ins = t.matmul(ps_q[:, (mmcnt % 2) * ES:(mmcnt % 2 + 1) * ES],
                                               ws[:, kc, :],
                                               xT[:, kc * T + tc * ES:kc * T + (tc + 1) * ES],
                                               start=(kc == 0), stop=(kc == KC - 1))
                            ins.then_inc(sm["qk"], 1)
                    for nblk in range(2):
                        g = b * NBLK + 16 + nblk
                        t.wait_ge(sm["wm%d" % (g % 2)], 16 * (g // 2 + 1))
                        ws = wslot(16 + nblk, g % 2)
                        for tt in range(NTT):
                            vcnt = nblk * 16 + tt
                            if vcnt >= 2:
                                t.wait_ge(sm["ve"], b * 32 + vcnt - 1)
                            else:
                                if b > 0:
                                    t.wait_ge(sm["ve"], b * 32)
                                    t.wait_ge(sm["oe"], b * 64)
                            for kc in range(KC):
                                ins = t.matmul(ps_o[:, (vcnt % 2) * ES:(vcnt % 2 + 1) * ES],
                                               xT[:, kc * T + tt * 128:kc * T + (tt + 1) * 128],
                                               ws[:, kc, :],
                                               start=(kc == 0), stop=(kc == KC - 1))
                            ins.then_inc(sm["vn"], 1)
                    t.wait_ge(sm["ve"], (b + 1) * 32)
                    t.wait_ge(sm["qe"], (b + 1) * 64)
                    for h in range(16):
                        pb = (h % 2) * 64
                        mq = h // 2
                        mk = 8 + h // 2
                        for tc in range(NCH):
                            tiles = fl_tiles[tc]
                            c_idx = b * 64 + h * NCH + tc
                            opar = c_idx % 2
                            for i, tt in enumerate(tiles):
                                gi = (b * 16 + h) * TPB + fl_off[tc] + i
                                if gi >= 2:
                                    t.wait_ge(sm["pf"], gi - 1)
                                t.matmul(ps_s[:, (gi % 2) * ES:(gi % 2 + 1) * ES],
                                         qkT[pb:pb + 64, mk * T + tt * 128:mk * T + (tt + 1) * 128],
                                         qkT[pb:pb + 64, mq * T + tc * ES:mq * T + (tc + 1) * ES],
                                         start=True, stop=True).then_inc(sm["sc"], 1)
                                t.wait_ge(sm["pm"], gi + 1)
                                if i == 0 and c_idx >= 2:
                                    t.wait_ge(sm["oe"], c_idx - 1)
                                t.matmul(ps_o[0:65, opar * ES:opar * ES + ES],
                                         vb[:, tt * 1040 + h * 65:tt * 1040 + h * 65 + 65],
                                         ptile[:, (gi % 2) * ES:(gi % 2 + 1) * ES],
                                         start=(i == 0), stop=(i == len(tiles) - 1)).then_inc(sm["av"], 1)
                            t.wait_ge(sm["li"], c_idx + 1)
                            if c_idx >= 1:
                                t.wait_ge(sm["rpc"], c_idx)
                            t.matmul(ps_r[0:64, :], onesr[0:1, 0:64],
                                     linv[0:1, opar * ES:(opar + 1) * ES],
                                     start=True, stop=True).then_inc(sm["rp"], 1)
                    t.wait_ge(sm["oe"], (b + 1) * 64)
                    wsE = []
                    for nblk in range(2):
                        g = b * NBLK + 18 + nblk
                        t.wait_ge(sm["wm%d" % (g % 2)], 16 * (g // 2 + 1))
                        wsE.append(wslot(18 + nblk, g % 2))
                    for tt in range(NTT):
                        for nblk in range(2):
                            pjc = tt * 2 + nblk
                            if pjc >= 2:
                                t.wait_ge(sm["ye"], b * 32 + pjc - 1)
                            for kc in range(KC):
                                ins = t.matmul(ps_q[:, (pjc % 2) * ES:(pjc % 2 + 1) * ES],
                                               ofm[:, kc * T + tt * 128:kc * T + (tt + 1) * 128],
                                               wsE[nblk][:, kc, :],
                                               start=(kc == 0), stop=(kc == KC - 1))
                            ins.then_inc(sm["pj"], 1)

            @blk.scalar
            def _(sc_raw):
                sc = SelfSync(sc_raw, sm["sch"])
                for b in range(BLOC):
                    for tt in range(NTT):
                        k = b * NTT + tt
                        sc.wait_ge(sm["x%d" % (tt % 2)], 16 * (b * 16 + tt // 2 + 1))
                        xa = xtile[:, (tt % 2) * 1024:(tt % 2 + 1) * 1024]
                        st8 = (tt % 2) * 8
                        sc.activation(sjunk[:, :], xa, AF.Square,
                                      accum_out=stat[:, st8 + 1:st8 + 2])
                        sc.wait_ge(sm["vb"], k + 1)
                        sc.activation(stat[:, st8 + 3:st8 + 4],
                                      stat[:, st8 + 1:st8 + 2], AF.Sqrt,
                                      bias=stat[:, st8 + 2:st8 + 3],
                                      scale=1.0 / 1024.0)
                        sc.sig(sm["sd"], k + 1)
                    for m in range(16):
                        for tc in range(NCH):
                            i = m * NCH + tc
                            sc.wait_ge(sm["qk"], b * 64 + i + 1)
                            sc.activation(qkT[:, m * T + tc * ES:m * T + (tc + 1) * ES],
                                          ps_q[:, (i % 2) * ES:(i % 2 + 1) * ES], AF.Identity,
                                          bias=bcol[:, m:m + 1])
                            sc.sig(sm["qe"], b * 64 + i + 1)
                    for h in range(16):
                        for tc in range(NCH):
                            tiles = fl_tiles[tc]
                            for i, tt in enumerate(tiles):
                                gi = (b * 16 + h) * TPB + fl_off[tc] + i
                                if gi >= 2:
                                    sc.wait_ge(sm["av"], gi - 1)
                                sc.wait_ge(sm["sc"], gi + 1)
                                sc.activation(ptile[:, (gi % 2) * ES:(gi % 2 + 1) * ES],
                                              ps_s[:, (gi % 2) * ES:(gi % 2 + 1) * ES], AF.Exp,
                                              bias=-M_STAB, scale=SCALE)
                                sc.sig(sm["pf"], gi + 1)

            @blk.vector
            def _(v_raw):
                v = SelfSync(v_raw, sm["vch"])
                v.wait_ge(sm["pro"], 1)
                v.tensor_copy(bcol[:, :], ps_b[0:128, 0:24])
                v.tensor_copy(borep[:, 0:ES], ps_q[:, 0:ES])
                v.tensor_copy(borep[:, ES:2 * ES], ps_q[:, ES:2 * ES])
                v.tensor_copy(bvrep[:, 0:ES], ps_o[:, 0:ES])
                v.tensor_copy(bvrep[:, ES:2 * ES], ps_o[:, ES:2 * ES])
                v.memset(vb.rearrange("p (u h j) -> p u h j", u=NTT, h=16)[:, :, :, 64:65], 1.0)
                v.sig(sm["rc"], 1)
                for b in range(BLOC):
                    for tt in range(NTT):
                        k = b * NTT + tt
                        xa = xtile[:, (tt % 2) * 1024:(tt % 2 + 1) * 1024]
                        st8 = (tt % 2) * 8
                        v.wait_ge(sm["x%d" % (tt % 2)], 16 * (b * 16 + tt // 2 + 1))
                        v.tensor_scalar(vjunk[:, :], xa, 1.0, None, ALU.mult, ALU.add,
                                        accum_out=stat[:, st8 + 0:st8 + 1])
                        v.tensor_scalar(stat[:, st8 + 4:st8 + 5], stat[:, st8 + 0:st8 + 1],
                                        1.0 / 1024.0, None, ALU.mult)
                        v.tensor_tensor(stat[:, st8 + 5:st8 + 6], stat[:, st8 + 4:st8 + 5],
                                        stat[:, st8 + 4:st8 + 5], ALU.mult)
                        v.tensor_scalar(stat[:, st8 + 2:st8 + 3], stat[:, st8 + 5:st8 + 6],
                                        -1.0, 1e-5, ALU.mult, ALU.add)
                        v.sig(sm["vb"], k + 1)
                        v.wait_ge(sm["sd"], k + 1)
                        v.reciprocal(stat[:, st8 + 6:st8 + 7], stat[:, st8 + 3:st8 + 4])
                        v.tensor_scalar(xln[:, (tt % 2) * 1024:(tt % 2 + 1) * 1024], xa,
                                        stat[:, st8 + 4:st8 + 5], stat[:, st8 + 6:st8 + 7],
                                        ALU.subtract, ALU.mult)
                        v.sig(sm["ln"], k + 1)
                        v.wait_ge(sm["tp"], k + 1)
                        v.tensor_copy(
                            xT.rearrange("p (kc u) -> p kc u", kc=KC)[:, :, tt * 128:(tt + 1) * 128],
                            ps_tp[:, :].rearrange("p (c u) -> p c u", c=KC))
                        v.sig(sm["xT"], k + 1)
                    for nblk in range(2):
                        for tt in range(NTT):
                            i = nblk * 16 + tt
                            v.wait_ge(sm["vn"], b * 32 + i + 1)
                            v.tensor_tensor(
                                vb.rearrange("p (u h j) -> p u h j", u=NTT, h=16)[:, tt, nblk * 8:(nblk + 1) * 8, 0:64],
                                ps_o[:, (i % 2) * ES:(i % 2 + 1) * ES].rearrange("p (h j) -> p h j", h=8),
                                bvrep[:, nblk * ES:(nblk + 1) * ES].rearrange("p (h j) -> p h j", h=8),
                                ALU.add)
                            v.sig(sm["ve"], b * 32 + i + 1)
                    for h in range(16):
                        pb = (h % 2) * 64
                        for tc in range(NCH):
                            tiles = fl_tiles[tc]
                            c_idx = b * 64 + h * NCH + tc
                            opar = c_idx % 2
                            for i, tt in enumerate(tiles):
                                gi = (b * 16 + h) * TPB + fl_off[tc] + i
                                v.wait_ge(sm["pf"], gi + 1)
                                j = tt - 4 * tc
                                if 0 <= j < 4:
                                    v.tensor_tensor(ptile[:, (gi % 2) * ES:(gi % 2 + 1) * ES],
                                                    ptile[:, (gi % 2) * ES:(gi % 2 + 1) * ES],
                                                    maskt[:, j * ES:(j + 1) * ES], ALU.mult)
                                else:
                                    v.nop()
                                v.sig(sm["pm"], gi + 1)
                            gi_last = (b * 16 + h) * TPB + fl_off[tc] + len(tiles) - 1
                            v.wait_ge(sm["av"], gi_last + 1)
                            v.reciprocal(linv[0:1, opar * ES:(opar + 1) * ES],
                                         ps_o[64:65, opar * ES:(opar + 1) * ES])
                            v.sig(sm["li"], c_idx + 1)
                            v.wait_ge(sm["rp"], c_idx + 1)
                            v.tensor_copy(repsb[0:64, opar * ES:(opar + 1) * ES],
                                          ps_r[0:64, :])
                            v.sig(sm["rpc"], c_idx + 1)
                            v.tensor_tensor(ofm[pb:pb + 64, (h // 2) * T + tc * ES:(h // 2) * T + (tc + 1) * ES],
                                            ps_o[0:64, opar * ES:(opar + 1) * ES],
                                            repsb[0:64, opar * ES:(opar + 1) * ES],
                                            ALU.mult)
                            v.sig(sm["oe"], c_idx + 1)
                    for tt in range(NTT):
                        k = b * NTT + tt
                        v.wait_ge(sm["x%d" % (tt % 2)], 16 * (b * 16 + 8 + tt // 2 + 1))
                        if tt >= 2:
                            v.wait_ge(sm["st%d" % (tt % 2)], 16 * (b * 8 + tt // 2))
                        elif b > 0:
                            v.wait_ge(sm["st%d" % (tt % 2)], 16 * (b * 8))
                        for nblk in range(2):
                            i = tt * 2 + nblk
                            v.wait_ge(sm["pj"], b * 32 + i + 1)
                            v.tensor_tensor(vjunk[:, 0:ES],
                                            ps_q[:, (nblk % 2) * ES:(nblk % 2 + 1) * ES],
                                            borep[:, nblk * ES:(nblk + 1) * ES],
                                            ALU.add)
                            v.sig(sm["ye"], b * 32 + i + 1)
                            v.tensor_tensor(ytile[:, (tt % 2) * 1024 + nblk * ES:(tt % 2) * 1024 + (nblk + 1) * ES],
                                            vjunk[:, 0:ES],
                                            xtile[:, (tt % 2) * 1024 + nblk * ES:(tt % 2) * 1024 + (nblk + 1) * ES],
                                            ALU.add)
                            if nblk == 1:
                                v.sig(sm["yd"], k + 1)


def emit_l1_ffn(p, xin, xout, wg, w1T, w3T, w2T, tag="f1a"):
    nc = p.nc
    NT = NTOK // 128
    NT2 = NTOK // ES
    IC = 2048 // 128
    E = 8
    BIG = 30000.0

    iota_in = p.const_input(tag + "_iota", np.broadcast_to(np.arange(8, dtype=np.float32), (128, 8)).copy())
    eyeb_in = p.const_input(tag + "_eyeb", np.eye(128, dtype=BFDT))
    xT_h = nc.dram_tensor(tag + "_xT", [128, KC * NTOK], BF16).ap()
    w1c = nc.dram_tensor(tag + "_w1c", [E, 128, KC * 2048], BF16).ap()
    w3c = nc.dram_tensor(tag + "_w3c", [E, 128, KC * 2048], BF16).ap()
    w2c = nc.dram_tensor(tag + "_w2c", [E, 128, IC * 1024], BF16).ap()

    sm = {n: p.sem(tag + n) for n in
          ["w", "wc", "cvt", "x0", "x1", "vb", "sd", "lo", "tp", "xTc",
           "xw0", "xw1", "rle", "rtm", "rt", "sg", "ce", "xs0", "xs1",
           "wl0", "wl1", "wl2", "h1", "sl", "he", "yp", "ye", "yd",
           "st0", "st1", "vch", "sch"]}

    with ExitStack() as _es:
        def A(nm, shape, dt):
            return _es.enter_context(nc.sbuf_tensor(tag + nm, shape, dt))

        wgsb = A("wgsb", [128, KC * 24], F32)
        wgbf = A("wgbf", [128, KC * 24], BF16)
        iota = A("iota", [128, 8], F32)
        eyeb = A("eyeb", [128, 128], BF16)
        stg = A("stg", [128, 2048], BF16)
        xtile = A("xtile", [128, 2 * 1024], F32)
        stat = A("stat", [128, 16], F32)
        sjunk = A("sjunk", [128, 1024], F32)
        vjunk = A("vjunk", [128, 1024], F32)
        xln = A("xln", [128, 2 * 1024], BF16)
        xlo = A("xlo", [128, 2 * 1024], BF16)
        xTrow = A("xTrow", [128, 2 * 1024], BF16)
        xlorow = A("xlorow", [128, 1024], BF16)
        rtt = A("rtt", [128, 72], F32)
        ce_all = A("ce", [128, NT * 8], F32)
        xs = A("xs", [128, 2 * KC * ES], BF16)
        wb1 = A("wb1", [128, KC * 2048], BF16)
        wb3 = A("wb3", [128, KC * 2048], BF16)
        wb2 = A("wb2", [128, IC * 1024], BF16)
        hti = A("hti", [128, IC * ES], BF16)
        hs = A("hs", [128, 2 * ES], F32)
        yacc = A("yacc", [128, 4 * 1024], F32)
        ytile = A("ytile", [128, 2 * 1024], F32)
        grow2 = ytile
        ps_a = _es.enter_context(nc.psum_tensor(tag + "ps_a", [128, 2 * ES], F32))
        ps_b2 = _es.enter_context(nc.psum_tensor(tag + "ps_b", [128, 2 * ES], F32))
        ps_y = _es.enter_context(nc.psum_tensor(tag + "ps_y", [128, 2 * ES], F32))
        ps_tp = _es.enter_context(nc.psum_tensor(tag + "ps_tp", [128, 1024], BF16))
        ps_r = _es.enter_context(nc.psum_tensor(tag + "ps_r", [128, 512], F32))

        LG, V0, EQ0, INC, F0, L2, F1, V1, DD, W0, W1C, CEA, CEB = (
            0, 8, 9, 17, 25, 33, 41, 49, 50, 51, 52, 53, 61)

        with nc.Block(tag) as blk:
            @blk.sync
            def _(s):
                s.dma_start(out=iota[:, :], in_=iota_in[:, :]).then_inc(sm["w"], 16)
                s.dma_start(out=eyeb[:, :], in_=eyeb_in[:, :]).then_inc(sm["w"], 16)
                s.dma_start(out=wgsb.rearrange("q (kc j) -> q kc j", kc=KC)[:, :, 0:8],
                            in_=wg.rearrange("(kc q) e -> q kc e", q=128)).then_inc(sm["w"], 16)
                for tt in range(NT):
                    if tt >= 2:
                        s.wait_ge(sm["lo"], tt - 1)
                    s.dma_start(out=xtile[:, (tt % 2) * 1024:(tt % 2 + 1) * 1024],
                                in_=xin[tt * 128:(tt + 1) * 128, :]).then_inc(sm["x%d" % (tt % 2)], 16)
                    s.wait_ge(sm["xTc"], 2 * tt + 1)
                    s.dma_start(out=xT_h.rearrange("q (kc u) -> q kc u", kc=KC)[:, :, tt * 128:(tt + 1) * 128],
                                in_=xTrow[:, (tt % 2) * 1024:(tt % 2 + 1) * 1024].rearrange("q (c u) -> q c u", c=KC)
                                ).then_inc(sm["xw%d" % (tt % 2)], 16)
                s.wait_ge(sm["cvt"], 16 * 256)
                for t2 in range(NT2 + 1):
                    if t2 >= 1:
                        c2 = t2 - 1
                        for j in range(4):
                            tt = c2 * 4 + j
                            if tt < 2:
                                s.wait_ge(sm["lo"], NT)
                            else:
                                s.wait_ge(sm["yd"], tt - 1)
                            s.dma_start(out=xtile[:, (tt % 2) * 1024:(tt % 2 + 1) * 1024],
                                        in_=xin[tt * 128:(tt + 1) * 128, :]).then_inc(sm["x%d" % (tt % 2)], 16)
                            s.wait_ge(sm["yd"], tt + 1)
                            s.dma_start(out=xout[tt * 128:(tt + 1) * 128, :],
                                        in_=ytile[:, (tt % 2) * 1024:(tt % 2 + 1) * 1024]).then_inc(sm["st%d" % (tt % 2)], 16)
                    if t2 < NT2:
                        s.wait_ge(sm["xw0"], 16 * (NT // 2))
                        s.wait_ge(sm["xw1"], 16 * (NT // 2))
                        if t2 >= 2:
                            s.wait_ge(sm["he"], ((t2 - 2) * 8 + 8) * IC)
                        s.dma_start(out=xs[:, (t2 % 2) * KC * ES:(t2 % 2 + 1) * KC * ES].rearrange("q (kc j) -> q kc j", kc=KC),
                                    in_=xT_h.rearrange("q (kc u) -> q kc u", kc=KC)[:, :, t2 * ES:(t2 + 1) * ES]
                                    ).then_inc(sm["xs%d" % (t2 % 2)], 16)
                        for e in range(E):
                            g = t2 * 8 + e
                            if g >= 1:
                                s.wait_ge(sm["he"], g * IC)
                            s.dma_start(out=wb1[:, :], in_=w1c[e, :, :]).then_inc(sm["wl0"], 16)
                            s.dma_start(out=wb3[:, :], in_=w3c[e, :, :]).then_inc(sm["wl1"], 16)
                            if g >= 1:
                                s.wait_ge(sm["ye"], g * 8)
                            s.dma_start(out=wb2[:, :], in_=w2c[e, :, :]).then_inc(sm["wl2"], 16)
                s.wait_ge(sm["st0"], 16 * (NT // 2))
                s.wait_ge(sm["st1"], 16 * (NT // 2))

            @blk.gpsimd
            def _(g):
                cv = 0
                for e in range(E):
                    for (dst, srcT) in ((w1c, w1T), (w3c, w3T)):
                        for kc in range(KC):
                            if cv > 0:
                                g.wait_ge(sm["cvt"], cv * 16)
                            g.dma_start(out=stg[:, :],
                                        in_=srcT[e, kc * 128:(kc + 1) * 128, :]).then_inc(sm["wc"], 16)
                            cv += 1
                            g.wait_ge(sm["wc"], cv * 16)
                            g.dma_start(out=dst[e, :, kc * 2048:(kc + 1) * 2048],
                                        in_=stg[:, :]).then_inc(sm["cvt"], 16)
                    for ic in range(IC):
                        g.wait_ge(sm["cvt"], cv * 16)
                        g.dma_start(out=stg[:, 0:1024],
                                    in_=w2T[e, ic * 128:(ic + 1) * 128, :]).then_inc(sm["wc"], 16)
                        cv += 1
                        g.wait_ge(sm["wc"], cv * 16)
                        g.dma_start(out=w2c[e, :, ic * 1024:(ic + 1) * 1024],
                                    in_=stg[:, 0:1024]).then_inc(sm["cvt"], 16)
                g.wait_ge(sm["cvt"], cv * 16)

            @blk.tensor
            def _(t):
                t.wait_ge(sm["w"], 16 * 3)
                t.wait_ge(sm["rt"], 1)
                for tt in range(NT):
                    t.wait_ge(sm["lo"], tt + 1)
                    if tt >= 1:
                        t.wait_ge(sm["xTc"], 2 * tt)
                    for kc in range(KC):
                        ins = t.transpose(ps_tp[:, kc * 128:(kc + 1) * 128],
                                          xln[:, (tt % 2) * 1024 + kc * 128:(tt % 2) * 1024 + (kc + 1) * 128],
                                          eyeb[:, :])
                    ins.then_inc(sm["tp"], 1)
                    t.wait_ge(sm["xTc"], 2 * tt + 1)
                    for kc in range(KC):
                        ins = t.transpose(ps_tp[:, kc * 128:(kc + 1) * 128],
                                          xlo[:, (tt % 2) * 1024 + kc * 128:(tt % 2) * 1024 + (kc + 1) * 128],
                                          eyeb[:, :])
                    ins.then_inc(sm["tp"], 1)
                    t.wait_ge(sm["xTc"], 2 * tt + 2)
                    if tt >= 1:
                        t.wait_ge(sm["rle"], tt)
                    xr = xTrow[:, (tt % 2) * 1024:(tt % 2 + 1) * 1024]
                    for kc in range(KC):
                        t.matmul(ps_r[:, 0:8], xr[:, kc * 128:(kc + 1) * 128],
                                 wgbf[:, kc * 24 + 0:kc * 24 + 8], start=(kc == 0), stop=False)
                        t.matmul(ps_r[:, 0:8], xr[:, kc * 128:(kc + 1) * 128],
                                 wgbf[:, kc * 24 + 8:kc * 24 + 16], start=False, stop=False)
                        ins = t.matmul(ps_r[:, 0:8], xlorow[:, kc * 128:(kc + 1) * 128],
                                       wgbf[:, kc * 24 + 16:kc * 24 + 24], start=False,
                                       stop=(kc == KC - 1))
                    ins.then_inc(sm["rtm"], 1)
                for t2 in range(NT2):
                    t.wait_ge(sm["xs%d" % (t2 % 2)], 16 * (t2 // 2 + 1))
                    xsl = xs[:, (t2 % 2) * KC * ES:(t2 % 2 + 1) * KC * ES].rearrange("q (kc j) -> q kc j", kc=KC)
                    for e in range(E):
                        g = t2 * 8 + e
                        t.wait_ge(sm["wl0"], 16 * (g + 1))
                        t.wait_ge(sm["wl1"], 16 * (g + 1))
                        for it in range(IC):
                            hcnt = g * IC + it
                            if hcnt >= 2:
                                t.wait_ge(sm["he"], hcnt - 1)
                            for kc in range(KC):
                                t.matmul(ps_a[:, (hcnt % 2) * ES:(hcnt % 2 + 1) * ES],
                                         wb1[:, kc * 2048 + it * 128:kc * 2048 + (it + 1) * 128],
                                         xsl[:, kc, :], start=(kc == 0), stop=(kc == KC - 1))
                            for kc in range(KC):
                                ins = t.matmul(ps_b2[:, (hcnt % 2) * ES:(hcnt % 2 + 1) * ES],
                                               wb3[:, kc * 2048 + it * 128:kc * 2048 + (it + 1) * 128],
                                               xsl[:, kc, :], start=(kc == 0), stop=(kc == KC - 1))
                            ins.then_inc(sm["h1"], 1)
                        t.wait_ge(sm["wl2"], 16 * (g + 1))
                        t.wait_ge(sm["he"], (g + 1) * IC)
                        for j in range(4):
                            for hc in range(2):
                                ycnt = g * 8 + j * 2 + hc
                                if ycnt >= 2:
                                    t.wait_ge(sm["ye"], ycnt - 1)
                                for ic in range(IC):
                                    ins = t.matmul(ps_y[:, (ycnt % 2) * ES:(ycnt % 2 + 1) * ES],
                                                   hti[:, ic * ES + j * 128:ic * ES + (j + 1) * 128],
                                                   wb2[:, ic * 1024 + hc * ES:ic * 1024 + (hc + 1) * ES],
                                                   start=(ic == 0), stop=(ic == IC - 1))
                                ins.then_inc(sm["yp"], 1)

            @blk.scalar
            def _(sc_raw):
                sc = SelfSync(sc_raw, sm["sch"])
                for tt in range(NT):
                    sc.wait_ge(sm["x%d" % (tt % 2)], 16 * (tt // 2 + 1))
                    xa = xtile[:, (tt % 2) * 1024:(tt % 2 + 1) * 1024]
                    st8 = (tt % 2) * 8
                    sc.activation(sjunk[:, :], xa, AF.Square,
                                  accum_out=stat[:, st8 + 1:st8 + 2])
                    sc.wait_ge(sm["vb"], tt + 1)
                    sc.activation(stat[:, st8 + 3:st8 + 4],
                                  stat[:, st8 + 1:st8 + 2], AF.Sqrt,
                                  bias=stat[:, st8 + 2:st8 + 3], scale=1.0 / 1024.0)
                    sc.sig(sm["sd"], tt + 1)
                    sc.wait_ge(sm["rt"], tt + 2)
                    sc.activation(rtt[:, W0:W0 + 1], rtt[:, DD:DD + 1], AF.Sigmoid,
                                  bias=0.0, scale=1.0)
                    sc.sig(sm["sg"], tt + 1)
                for t2 in range(NT2):
                    for e in range(E):
                        g = t2 * 8 + e
                        for it in range(IC):
                            hcnt = g * IC + it
                            sc.wait_ge(sm["h1"], hcnt + 1)
                            if hcnt >= 2:
                                sc.wait_ge(sm["he"], hcnt - 1)
                            sc.activation(hs[:, (hcnt % 2) * ES:(hcnt % 2 + 1) * ES],
                                          ps_a[:, (hcnt % 2) * ES:(hcnt % 2 + 1) * ES],
                                          AF.Silu, bias=0.0, scale=1.0)
                            sc.sig(sm["sl"], hcnt + 1)

            @blk.vector
            def _(v_raw):
                v = SelfSync(v_raw, sm["vch"])
                v.wait_ge(sm["w"], 16 * 3)
                wgv = wgsb.rearrange("q (kc j) -> q kc j", kc=KC)
                wgbv = wgbf.rearrange("q (kc j) -> q kc j", kc=KC)
                v.tensor_copy(wgbv[:, :, 0:8], wgv[:, :, 0:8])
                v.tensor_copy(wgv[:, :, 8:16], wgbv[:, :, 0:8])
                v.tensor_tensor(xlo[:, 0:KC * 8].rearrange("q (kc j) -> q kc j", kc=KC),
                                wgv[:, :, 0:8], wgv[:, :, 8:16], ALU.subtract)
                v.tensor_copy(wgbv[:, :, 8:16], xlo[:, 0:KC * 8].rearrange("q (kc j) -> q kc j", kc=KC))
                v.tensor_copy(wgbv[:, :, 16:24], wgbv[:, :, 0:8])
                v.sig(sm["rt"], 1)
                for tt in range(NT):
                    xa = xtile[:, (tt % 2) * 1024:(tt % 2 + 1) * 1024]
                    st8 = (tt % 2) * 8
                    v.wait_ge(sm["x%d" % (tt % 2)], 16 * (tt // 2 + 1))
                    v.tensor_scalar(vjunk[:, :], xa, 1.0, None, ALU.mult, ALU.add,
                                    accum_out=stat[:, st8 + 0:st8 + 1])
                    v.tensor_scalar(stat[:, st8 + 4:st8 + 5], stat[:, st8 + 0:st8 + 1],
                                    1.0 / 1024.0, None, ALU.mult)
                    v.tensor_tensor(stat[:, st8 + 5:st8 + 6], stat[:, st8 + 4:st8 + 5],
                                    stat[:, st8 + 4:st8 + 5], ALU.mult)
                    v.tensor_scalar(stat[:, st8 + 2:st8 + 3], stat[:, st8 + 5:st8 + 6],
                                    -1.0, 1e-5, ALU.mult, ALU.add)
                    v.sig(sm["vb"], tt + 1)
                    v.wait_ge(sm["sd"], tt + 1)
                    v.reciprocal(stat[:, st8 + 6:st8 + 7], stat[:, st8 + 3:st8 + 4])
                    v.tensor_scalar(vjunk[:, :], xa, stat[:, st8 + 4:st8 + 5],
                                    stat[:, st8 + 6:st8 + 7], ALU.subtract, ALU.mult)
                    v.tensor_copy(xln[:, (tt % 2) * 1024:(tt % 2 + 1) * 1024], vjunk[:, :])
                    v.tensor_tensor(xlo[:, (tt % 2) * 1024:(tt % 2 + 1) * 1024], vjunk[:, :],
                                    xln[:, (tt % 2) * 1024:(tt % 2 + 1) * 1024], ALU.subtract)
                    v.sig(sm["lo"], tt + 1)
                    v.wait_ge(sm["tp"], 2 * tt + 1)
                    if tt >= 2:
                        v.wait_ge(sm["xw%d" % (tt % 2)], 16 * (tt // 2))
                    v.tensor_copy(xTrow[:, (tt % 2) * 1024:(tt % 2 + 1) * 1024].rearrange("q (c u) -> q c u", c=KC),
                                  ps_tp[:, :].rearrange("q (c u) -> q c u", c=KC))
                    v.sig(sm["xTc"], 2 * tt + 1)
                    v.wait_ge(sm["tp"], 2 * tt + 2)
                    v.tensor_copy(xlorow[:, :].rearrange("q (c u) -> q c u", c=KC),
                                  ps_tp[:, :].rearrange("q (c u) -> q c u", c=KC))
                    v.sig(sm["xTc"], 2 * tt + 2)
                    v.wait_ge(sm["rtm"], tt + 1)
                    v.tensor_copy(rtt[:, LG:LG + 8], ps_r[:, 0:8])
                    v.sig(sm["rle"], tt + 1)
                    v.reduce_max(rtt[:, V0:V0 + 1], rtt[:, LG:LG + 8], mybir.AxisListType.X)
                    v.tensor_scalar(rtt[:, EQ0:EQ0 + 8], rtt[:, LG:LG + 8],
                                    rtt[:, V0:V0 + 1], None, ALU.is_equal)
                    v.tensor_tensor_scan(rtt[:, INC:INC + 8], rtt[:, EQ0:EQ0 + 8],
                                         rtt[:, EQ0:EQ0 + 8], 0.0, ALU.add, ALU.max)
                    v.tensor_scalar(rtt[:, F0:F0 + 8], rtt[:, INC:INC + 8], 1.0, None, ALU.is_equal)
                    v.tensor_tensor(rtt[:, F0:F0 + 8], rtt[:, F0:F0 + 8],
                                    rtt[:, EQ0:EQ0 + 8], ALU.mult)
                    v.tensor_scalar(rtt[:, L2:L2 + 8], rtt[:, F0:F0 + 8], -BIG, None, ALU.mult)
                    v.tensor_tensor(rtt[:, L2:L2 + 8], rtt[:, L2:L2 + 8],
                                    rtt[:, LG:LG + 8], ALU.add)
                    v.reduce_max(rtt[:, V1:V1 + 1], rtt[:, L2:L2 + 8], mybir.AxisListType.X)
                    v.tensor_scalar(rtt[:, F1:F1 + 8], rtt[:, L2:L2 + 8],
                                    rtt[:, V1:V1 + 1], None, ALU.is_equal)
                    v.tensor_tensor_scan(rtt[:, INC:INC + 8], rtt[:, F1:F1 + 8],
                                         rtt[:, F1:F1 + 8], 0.0, ALU.add, ALU.max)
                    v.tensor_scalar(rtt[:, EQ0:EQ0 + 8], rtt[:, INC:INC + 8], 1.0, None, ALU.is_equal)
                    v.tensor_tensor(rtt[:, F1:F1 + 8], rtt[:, F1:F1 + 8],
                                    rtt[:, EQ0:EQ0 + 8], ALU.mult)
                    v.tensor_tensor(rtt[:, DD:DD + 1], rtt[:, V0:V0 + 1],
                                    rtt[:, V1:V1 + 1], ALU.subtract)
                    v.sig(sm["rt"], tt + 2)
                    v.wait_ge(sm["sg"], tt + 1)
                    v.tensor_scalar(rtt[:, W1C:W1C + 1], rtt[:, W0:W0 + 1], -1.0, 1.0,
                                    ALU.mult, ALU.add)
                    v.tensor_scalar(rtt[:, CEA:CEA + 8], rtt[:, F0:F0 + 8],
                                    rtt[:, W0:W0 + 1], None, ALU.mult)
                    v.tensor_scalar(rtt[:, CEB:CEB + 8], rtt[:, F1:F1 + 8],
                                    rtt[:, W1C:W1C + 1], None, ALU.mult)
                    v.tensor_tensor(ce_all[:, tt * 8:(tt + 1) * 8], rtt[:, CEA:CEA + 8],
                                    rtt[:, CEB:CEB + 8], ALU.add)
                    v.sig(sm["ce"], tt + 1)
                for t2 in range(NT2):
                    for e in range(E):
                        g = t2 * 8 + e
                        for it in range(IC):
                            hcnt = g * IC + it
                            v.wait_ge(sm["sl"], hcnt + 1)
                            if it == 0 and g >= 1:
                                v.wait_ge(sm["yp"], g * 8)
                            v.tensor_tensor(hti[:, it * ES:(it + 1) * ES],
                                            hs[:, (hcnt % 2) * ES:(hcnt % 2 + 1) * ES],
                                            ps_b2[:, (hcnt % 2) * ES:(hcnt % 2 + 1) * ES],
                                            ALU.mult)
                            v.sig(sm["he"], hcnt + 1)
                        for j in range(4):
                            tt = t2 * 4 + j
                            for hc in range(2):
                                ycnt = g * 8 + j * 2 + hc
                                v.wait_ge(sm["yp"], ycnt + 1)
                                v.wait_ge(sm["ce"], tt + 1)
                                cecol = ce_all[:, tt * 8 + e:tt * 8 + e + 1]
                                ya = yacc[:, j * 1024 + hc * ES:j * 1024 + (hc + 1) * ES]
                                if e == 0:
                                    v.tensor_scalar(ya, ps_y[:, (ycnt % 2) * ES:(ycnt % 2 + 1) * ES],
                                                    cecol, None, ALU.mult)
                                else:
                                    v.scalar_tensor_tensor(ya, ps_y[:, (ycnt % 2) * ES:(ycnt % 2 + 1) * ES],
                                                           cecol, ya, ALU.mult, ALU.add)
                                v.sig(sm["ye"], ycnt + 1)
                    for j in range(4):
                        tt = t2 * 4 + j
                        v.wait_ge(sm["x%d" % (tt % 2)], 16 * (NT // 2 + tt // 2 + 1))
                        if tt >= 2:
                            v.wait_ge(sm["st%d" % (tt % 2)], 16 * (tt // 2))
                        for hc in range(2):
                            v.tensor_tensor(ytile[:, (tt % 2) * 1024 + hc * ES:(tt % 2) * 1024 + (hc + 1) * ES],
                                            yacc[:, j * 1024 + hc * ES:j * 1024 + (hc + 1) * ES],
                                            xtile[:, (tt % 2) * 1024 + hc * ES:(tt % 2) * 1024 + (hc + 1) * ES],
                                            ALU.add)
                        v.sig(sm["yd"], tt + 1)


def emit_l0_ffn(p, xin, xout, g1, g2, wq, wk, wv, wo, router_w, router_b,
                w1, w2, w3, cosr, sinr, cmask_row, pmask_col, tag="f0a",
                dbg_h=None, cc_mock=False):
    nc = p.nc
    M_STAB = 32.0
    SCALE = 0.125
    NT = NTOK // 128
    NT2 = NTOK // ES
    IC = 2048 // 128
    E = 8
    BIG = 30000.0
    CAP = 8192.0

    iota_in = p.const_input(tag + "_iota", np.broadcast_to(np.arange(8, dtype=np.float32), (128, 8)).copy())
    eyeb_in = p.const_input(tag + "_eyeb", np.eye(128, dtype=BFDT))
    onesc_in = p.const_input(tag + "_onesc", np.ones((128, 1), np.float32))
    trilm_in = p.const_input(tag + "_trilm", np.tril(np.ones((128, 128), BFDT)))
    masks = np.zeros((4, 128, ES), BFDT)
    sl_ = np.arange(128)[:, None]
    tl_ = np.arange(ES)[None, :]
    for j in range(4):
        masks[j] = (128 * j + sl_ <= tl_).astype(BFDT)
    mask_in = p.const_input(tag + "_masks", np.ascontiguousarray(masks.transpose(1, 0, 2).reshape(128, 4 * ES)))

    ones64_in = p.const_input(tag + "_ones", np.ones((64, 128), np.float32))
    eyef_in = p.const_input(tag + "_eyef", np.eye(128, dtype=np.float32))
    xln_h = nc.dram_tensor(tag + "_xlnh", [NTOK, D], F32).ap()
    h_h = dbg_h if dbg_h is not None else nc.dram_tensor(tag + "_hh", [NTOK, D], F32).ap()
    xf_h = nc.dram_tensor(tag + "_xfh", [NTOK, D], F32).ap()
    xT2_h = nc.dram_tensor(tag + "_xT2", [128, KC * NTOK], BF16).ap()
    w1c = nc.dram_tensor(tag + "_w1c", [E, 128, KC * 2048], BF16).ap()
    w2c = nc.dram_tensor(tag + "_w2c", [E, 128, KC * 2048], BF16).ap()
    w3c = nc.dram_tensor(tag + "_w3c", [E, 128, IC * 1024], BF16).ap()
    cc_in = nc.dram_tensor(tag + "_ccin", [8, 16], F32).ap()
    cc_out = nc.dram_tensor(tag + "_ccout", [8, 16], F32).ap()

    sm = {n: p.sem(tag + n) for n in
          ["w", "wc", "cvt", "x0", "x1", "vb", "sd", "ln", "lnw0", "lnw1",
           "tp", "xT", "qk", "qe", "vn", "ve", "rot", "sc", "pf", "pm",
           "av", "li", "rp", "rpc", "oe", "pj", "yd", "ye", "st0", "st1",
           "hx0", "hx1", "hvb", "hsd", "hlo", "htp", "hxc", "hxw0", "hxw1",
           "rs", "cc", "ccd", "co",
           "xs0", "xs1", "wl0", "wl1", "wl2", "h1", "slh", "he", "yp",
           "fy", "fx0", "fx1", "fh0", "fh1", "ff0", "ff1", "vch", "sch",
           "vch2", "sch2", "rrow", "gh",
           "Msc", "Mpf", "Mpm", "Mav", "Mli", "Mrp", "Moe", "Mce", "Mve",
           "Mhtp", "Mhxc", "Myd", "Mye", "Mrle", "Mrt", "Msg", "Mrtm"]}

    fl_tiles = [list(range(0, 4 * tc + 4)) for tc in range(NCH)]
    fl_off = [sum(len(fl_tiles[c]) for c in range(tc)) for tc in range(NCH)]
    TPB = sum(len(x) for x in fl_tiles)
    cpool_ap, cpool_in = p.reg_const(tag, [-M_STAB, 1e-6])

    with ExitStack() as _es:
        def A(nm, shape, dt):
            return _es.enter_context(nc.sbuf_tensor(tag + nm, shape, dt))

        eyeb = A("eyeb", [128, 128], BF16)
        g1rep = A("g1rep", [128, 1024], F32)
        grow = A("grow", [64, 1024], F32)
        onesr = A("onesr", [64, 128], F32)
        wqkv = A("wqkv", [128, KC * 1536], BF16)
        wosb = A("wosb", [128, KC * 1024], BF16)
        stg = A("stgA", [128, 2048], BF16)
        xtile = A("xtileA", [128, 2 * 1024], F32)
        stat = A("statA", [128, 16], F32)
        sjunk = A("sjunkA", [128, 1024], F32)
        vjunk = A("vjunkA", [128, 1024], F32)
        vrot = A("vrotA", [128, 1024], F32)
        xlnrow = A("xlnrow", [128, 2 * 1024], F32)
        xrm = A("xrm", [128, 2 * 1024], BF16)
        rote = A("rote", [128, 2 * 2048], BF16)
        qn = A("qn", [128, 2 * 1536], BF16)
        qr = A("qr", [128, 2 * 1536], BF16)
        xT = A("xTA", [128, KC * T], BF16)
        qkT = A("qkTA", [128, 10 * T], BF16)
        vb = A("vbA", [128, NTT * 260], BF16)
        ptile = A("ptA", [128, 2 * ES], BF16)
        repsb = A("repsbA", [64, 2 * ES], F32)
        linv = grow
        maskt = A("masktA", [128, 4 * ES], BF16)
        ytile = A("ytileA", [128, 2 * 1024], F32)
        ofm = xT
        ps_s = _es.enter_context(nc.psum_tensor(tag + "ps_s", [128, 2 * ES], F32))
        ps_o = _es.enter_context(nc.psum_tensor(tag + "ps_o", [128, 2 * ES], F32))
        ps_q = _es.enter_context(nc.psum_tensor(tag + "ps_q", [128, 2 * ES], F32))
        ps_r = _es.enter_context(nc.psum_tensor(tag + "ps_r", [128, ES], F32))
        ps_tp = _es.enter_context(nc.psum_tensor(tag + "ps_tp", [128, 1024], BF16))

        with nc.Block(tag + "A") as blk:
            @blk.sync
            def _(s):
                s.dma_start(out=eyeb[:, :], in_=eyeb_in[:, :]).then_inc(sm["w"], 16)
                s.dma_start(out=maskt[:, :], in_=mask_in[:, :]).then_inc(sm["w"], 16)
                s.dma_start(out=grow[0:1, :], in_=g1.rearrange("(a q) -> a q", a=1)).then_inc(sm["w"], 16)
                s.dma_start(out=onesr[:, :], in_=ones64_in[:, :]).then_inc(sm["w"], 16)
                s.dma_start(out=cpool_ap[:, :], in_=cpool_in[:, :]).then_inc(sm["w"], 16)
                for b in range(BLOC):
                    t0 = b * T
                    for tt in range(NTT):
                        k = b * NTT + tt
                        if k >= 2:
                            s.wait_ge(sm["ln"], k - 1)
                        if b > 0 and tt < 2:
                            s.wait_ge(sm["yd"], b * NTT)
                        s.dma_start(out=xtile[:, (tt % 2) * 1024:(tt % 2 + 1) * 1024],
                                    in_=xin[t0 + tt * 128:t0 + (tt + 1) * 128, :]).then_inc(sm["x%d" % (tt % 2)], 16)
                        if k >= 2:
                            s.wait_ge(sm["rot"], k)
                        s.dma_start(out=rote[:, (tt % 2) * 2048:(tt % 2) * 2048 + 1024],
                                    in_=cosr[tt * 128:(tt + 1) * 128, :]).then_inc(sm["x%d" % (tt % 2)], 16)
                        s.dma_start(out=rote[:, (tt % 2) * 2048 + 1024:(tt % 2 + 1) * 2048],
                                    in_=sinr[tt * 128:(tt + 1) * 128, :]).then_inc(sm["x%d" % (tt % 2)], 16)
                        s.wait_ge(sm["ln"], k + 1)
                        s.dma_start(out=xln_h[t0 + tt * 128:t0 + (tt + 1) * 128, :],
                                    in_=xlnrow[:, (tt % 2) * 1024:(tt % 2 + 1) * 1024]).then_inc(sm["lnw%d" % (tt % 2)], 16)
                    for tt in range(NTT):
                        k = b * NTT + tt
                        if tt < 2:
                            s.wait_ge(sm["ln"], (b + 1) * NTT)
                        else:
                            s.wait_ge(sm["yd"], k - 1)
                        s.wait_ge(sm["lnw%d" % (tt % 2)], 16 * (b * 8 + tt // 2 + 1))
                        s.dma_start(out=xtile[:, (tt % 2) * 1024:(tt % 2 + 1) * 1024],
                                    in_=xln_h[t0 + tt * 128:t0 + (tt + 1) * 128, :]).then_inc(sm["x%d" % (tt % 2)], 16)
                        if tt >= 2:
                            st_t = tt - 2
                            s.wait_ge(sm["yd"], b * NTT + st_t + 1)
                            s.dma_start(out=h_h[t0 + st_t * 128:t0 + (st_t + 1) * 128, :],
                                        in_=ytile[:, (st_t % 2) * 1024:(st_t % 2 + 1) * 1024]).then_inc(sm["st%d" % (st_t % 2)], 16)
                    for st_t in (NTT - 2, NTT - 1):
                        s.wait_ge(sm["yd"], b * NTT + st_t + 1)
                        s.dma_start(out=h_h[t0 + st_t * 128:t0 + (st_t + 1) * 128, :],
                                    in_=ytile[:, (st_t % 2) * 1024:(st_t % 2 + 1) * 1024]).then_inc(sm["st%d" % (st_t % 2)], 16)
                s.wait_ge(sm["st0"], 16 * (BLOC * NTT // 2))
                s.wait_ge(sm["st1"], 16 * (BLOC * NTT // 2))

            @blk.gpsimd
            def _(g):
                cv = 0
                for kc in range(KC):
                    if cv:
                        g.wait_ge(sm["cvt"], cv * 16)
                    g.dma_start(out=wqkv[:, kc * 1536:kc * 1536 + 1024],
                                in_=wq[kc * 128:(kc + 1) * 128, :]).then_inc(sm["cvt"], 16)
                    cv += 1
                    g.dma_start(out=wqkv[:, kc * 1536 + 1024:kc * 1536 + 1280],
                                in_=wk[kc * 128:(kc + 1) * 128, :]).then_inc(sm["cvt"], 16)
                    cv += 1
                    g.dma_start(out=wqkv[:, kc * 1536 + 1280:kc * 1536 + 1536],
                                in_=wv[kc * 128:(kc + 1) * 128, :]).then_inc(sm["cvt"], 16)
                    cv += 1
                    g.dma_start(out=wosb[:, kc * 1024:(kc + 1) * 1024],
                                in_=wo[kc * 128:(kc + 1) * 128, :]).then_inc(sm["cvt"], 16)
                    cv += 1
                g.wait_ge(sm["cvt"], cv * 16)

            @blk.tensor
            def _(t):
                t.wait_ge(sm["w"], 16 * 5)
                t.matmul(ps_q[:, 0:ES], onesr[0:1, :], grow[0:1, 0:ES], start=True, stop=True)
                t.matmul(ps_q[:, ES:2 * ES], onesr[0:1, :], grow[0:1, ES:2 * ES], start=True, stop=True).then_inc(sm["rot"], 1)
                t.wait_ge(sm["rp"], 1)
                t.wait_ge(sm["cvt"], 16 * 32)
                for b in range(BLOC):
                    for tt in range(NTT):
                        k = b * NTT + tt
                        t.wait_ge(sm["vb"], 2 * k + 2)
                        if k >= 1:
                            t.wait_ge(sm["hxc"], 2 * k)
                        for kc in range(KC):
                            ins = t.transpose(ps_tp[:, kc * 128:(kc + 1) * 128],
                                              xrm[:, (tt % 2) * 1024 + kc * 128:(tt % 2) * 1024 + (kc + 1) * 128],
                                              eyeb[:, :])
                        ins.then_inc(sm["tp"], 1)
                        t.wait_ge(sm["xT"], k + 1)
                        for nc3 in range(3):
                            i3 = k * 3 + nc3
                            if i3 >= 2:
                                t.wait_ge(sm["qe"], i3 - 1)
                            for kc in range(KC):
                                ins = t.matmul(ps_q[:, (i3 % 2) * ES:(i3 % 2 + 1) * ES],
                                               xT[:, kc * T + tt * 128:kc * T + (tt + 1) * 128],
                                               wqkv[:, kc * 1536 + nc3 * ES:kc * 1536 + (nc3 + 1) * ES],
                                               start=(kc == 0), stop=(kc == KC - 1))
                            ins.then_inc(sm["qk"], 1)
                        t.wait_ge(sm["rot"], k + 2)
                        t.wait_ge(sm["xT"], k + 1)
                        for mi in range(8):
                            ins = t.transpose(ps_tp[:, mi * 128:(mi + 1) * 128],
                                              qr[:, (tt % 2) * 1536 + mi * 128:(tt % 2) * 1536 + (mi + 1) * 128],
                                              eyeb[:, :])
                        ins.then_inc(sm["htp"], 1)
                        t.wait_ge(sm["hxc"], 2 * k + 1)
                        for mi in range(8, 10):
                            ins = t.transpose(ps_tp[:, (mi - 8) * 128:(mi - 7) * 128],
                                              qr[:, (tt % 2) * 1536 + mi * 128:(tt % 2) * 1536 + (mi + 1) * 128],
                                              eyeb[:, :])
                        ins.then_inc(sm["htp"], 1)
                    t.wait_ge(sm["ve"], (b + 1) * NTT)
                    t.wait_ge(sm["hxc"], 2 * (b + 1) * NTT)
                    for h in range(16):
                        pb = (h % 2) * 64
                        mq = h // 2
                        mk = 8 + (h % 4) // 2
                        pk = ((h % 4) % 2) * 64
                        for tc in range(NCH):
                            tiles = fl_tiles[tc]
                            c_idx = b * 64 + h * NCH + tc
                            opar = c_idx % 2
                            for i, tt in enumerate(tiles):
                                gi = (b * 16 + h) * TPB + fl_off[tc] + i
                                if gi >= 2:
                                    t.wait_ge(sm["pf"], gi - 1)
                                t.matmul(ps_s[:, (gi % 2) * ES:(gi % 2 + 1) * ES],
                                         qkT[pk:pk + 64, mk * T + tt * 128:mk * T + (tt + 1) * 128],
                                         qkT[pb:pb + 64, mq * T + tc * ES:mq * T + (tc + 1) * ES],
                                         start=True, stop=True).then_inc(sm["sc"], 1)
                                t.wait_ge(sm["pm"], gi + 1)
                                if i == 0 and c_idx >= 2:
                                    t.wait_ge(sm["oe"], c_idx - 1)
                                t.matmul(ps_o[0:65, opar * ES:opar * ES + ES],
                                         vb[:, tt * 260 + (h % 4) * 65:tt * 260 + (h % 4) * 65 + 65],
                                         ptile[:, (gi % 2) * ES:(gi % 2 + 1) * ES],
                                         start=(i == 0), stop=(i == len(tiles) - 1)).then_inc(sm["av"], 1)
                            t.wait_ge(sm["li"], c_idx + 1)
                            if c_idx >= 1:
                                t.wait_ge(sm["rpc"], c_idx)
                            t.matmul(ps_r[0:64, :], onesr[0:1, 0:64],
                                     linv[0:1, opar * ES:(opar + 1) * ES],
                                     start=True, stop=True).then_inc(sm["rp"], 1)
                    t.wait_ge(sm["oe"], (b + 1) * 64)
                    for tt in range(NTT):
                        for nchunk in range(2):
                            pjc = tt * 2 + nchunk
                            if pjc >= 2:
                                t.wait_ge(sm["ye" if nchunk == 0 else "yd"], b * NTT + tt)
                            for kc in range(KC):
                                ins = t.matmul(ps_q[:, (pjc % 2) * ES:(pjc % 2 + 1) * ES],
                                               ofm[:, kc * T + tt * 128:kc * T + (tt + 1) * 128],
                                               wosb[:, kc * 1024 + nchunk * ES:kc * 1024 + (nchunk + 1) * ES],
                                               start=(kc == 0), stop=(kc == KC - 1))
                            ins.then_inc(sm["pj"], 1)

            @blk.scalar
            def _(sc_raw):
                sc = SelfSync(sc_raw, sm["sch"])
                sc.wait_ge(sm["w"], 16 * 5)
                for b in range(BLOC):
                    for tt in range(NTT):
                        k = b * NTT + tt
                        sc.wait_ge(sm["x%d" % (tt % 2)], 16 * (32 * b + 3 * (tt // 2) + 3))
                        xa = xtile[:, (tt % 2) * 1024:(tt % 2 + 1) * 1024]
                        st8 = (tt % 2) * 8
                        sc.activation(sjunk[:, :], xa, AF.Square,
                                      accum_out=stat[:, st8 + 1:st8 + 2])
                        sc.wait_ge(sm["vb"], 2 * k + 1)
                        sc.activation(stat[:, st8 + 3:st8 + 4],
                                      stat[:, st8 + 1:st8 + 2], AF.Sqrt,
                                      bias=stat[:, st8 + 2:st8 + 3], scale=1.0 / 1024.0)
                        sc.sig(sm["sd"], 2 * k + 1)
                        sc.wait_ge(sm["ln"], k + 1)
                        sc.activation(sjunk[:, :], xlnrow[:, (tt % 2) * 1024:(tt % 2 + 1) * 1024],
                                      AF.Square, accum_out=stat[:, st8 + 5:st8 + 6])
                        sc.activation(stat[:, st8 + 6:st8 + 7],
                                      stat[:, st8 + 5:st8 + 6], AF.Sqrt,
                                      bias=1e-6, scale=1.0 / 1024.0)
                        sc.sig(sm["sd"], 2 * k + 2)
                    for h in range(16):
                        for tc in range(NCH):
                            tiles = fl_tiles[tc]
                            for i, tt in enumerate(tiles):
                                gi = (b * 16 + h) * TPB + fl_off[tc] + i
                                if gi >= 2:
                                    sc.wait_ge(sm["av"], gi - 1)
                                sc.wait_ge(sm["sc"], gi + 1)
                                sc.activation(ptile[:, (gi % 2) * ES:(gi % 2 + 1) * ES],
                                              ps_s[:, (gi % 2) * ES:(gi % 2 + 1) * ES], AF.Exp,
                                              bias=-M_STAB, scale=SCALE)
                                sc.sig(sm["pf"], gi + 1)

            @blk.vector
            def _(v_raw):
                v = SelfSync(v_raw, sm["vch"])
                v.wait_ge(sm["rot"], 1)
                v.tensor_copy(g1rep[:, 0:ES], ps_q[:, 0:ES])
                v.tensor_copy(g1rep[:, ES:2 * ES], ps_q[:, ES:2 * ES])
                v.memset(vb.rearrange("p (u h j) -> p u h j", u=NTT, h=4)[:, :, :, 64:65], 1.0)
                v.sig(sm["rp"], 1)
                for b in range(BLOC):
                    for tt in range(NTT):
                        k = b * NTT + tt
                        xa = xtile[:, (tt % 2) * 1024:(tt % 2 + 1) * 1024]
                        st8 = (tt % 2) * 8
                        if tt == 0 and b >= 1:
                            v.wait_ge(sm["av"], b * 16 * TPB)
                        v.wait_ge(sm["x%d" % (tt % 2)], 16 * (32 * b + 3 * (tt // 2) + 3))
                        if k >= 2:
                            v.wait_ge(sm["lnw%d" % (tt % 2)], 16 * (b * 8 + tt // 2))
                        v.tensor_scalar(vjunk[:, :], xa, 1.0, None, ALU.mult, ALU.add,
                                        accum_out=stat[:, st8 + 0:st8 + 1])
                        v.tensor_scalar(stat[:, st8 + 4:st8 + 5], stat[:, st8 + 0:st8 + 1],
                                        1.0 / 1024.0, None, ALU.mult)
                        v.tensor_tensor(stat[:, st8 + 7:st8 + 8], stat[:, st8 + 4:st8 + 5],
                                        stat[:, st8 + 4:st8 + 5], ALU.mult)
                        v.tensor_scalar(stat[:, st8 + 2:st8 + 3], stat[:, st8 + 7:st8 + 8],
                                        -1.0, 1e-5, ALU.mult, ALU.add)
                        v.sig(sm["vb"], 2 * k + 1)
                        v.wait_ge(sm["sd"], 2 * k + 1)
                        v.reciprocal(stat[:, st8 + 7:st8 + 8], stat[:, st8 + 3:st8 + 4])
                        v.tensor_scalar(xlnrow[:, (tt % 2) * 1024:(tt % 2 + 1) * 1024], xa,
                                        stat[:, st8 + 4:st8 + 5], stat[:, st8 + 7:st8 + 8],
                                        ALU.subtract, ALU.mult)
                        v.sig(sm["ln"], k + 1)
                        v.wait_ge(sm["sd"], 2 * k + 2)
                        v.reciprocal(stat[:, st8 + 5:st8 + 6], stat[:, st8 + 6:st8 + 7])
                        v.tensor_scalar(vjunk[:, :], xlnrow[:, (tt % 2) * 1024:(tt % 2 + 1) * 1024],
                                        stat[:, st8 + 5:st8 + 6], None, ALU.mult)
                        v.tensor_tensor(xrm[:, (tt % 2) * 1024:(tt % 2 + 1) * 1024],
                                        vjunk[:, :], g1rep[:, :], ALU.mult)
                        v.sig(sm["vb"], 2 * k + 2)
                        v.wait_ge(sm["tp"], k + 1)
                        v.tensor_copy(xT.rearrange("p (kc u) -> p kc u", kc=KC)[:, :, tt * 128:(tt + 1) * 128],
                                      ps_tp[:, :].rearrange("p (c u) -> p c u", c=KC))
                        v.sig(sm["xT"], k + 1)
                        for nc3 in range(3):
                            i3 = k * 3 + nc3
                            v.wait_ge(sm["qk"], i3 + 1)
                            v.tensor_copy(qn[:, (tt % 2) * 1536 + nc3 * ES:(tt % 2) * 1536 + (nc3 + 1) * ES],
                                          ps_q[:, (i3 % 2) * ES:(i3 % 2 + 1) * ES])
                            v.sig(sm["qe"], i3 + 1)
                        qnv = qn[:, (tt % 2) * 1536:(tt % 2 + 1) * 1536]
                        qrv = qr[:, (tt % 2) * 1536:(tt % 2 + 1) * 1536]
                        cosv = rote[:, (tt % 2) * 2048:(tt % 2) * 2048 + 1024]
                        sinv = rote[:, (tt % 2) * 2048 + 1024:(tt % 2 + 1) * 2048]
                        if k >= 2:
                            v.wait_ge(sm["htp"], 2 * (k - 1))
                        v.tensor_tensor(vrot[:, :], qnv[:, 0:1024], cosv[:, :], ALU.mult)
                        qsw = qnv[:, 0:1024].rearrange("p (h s j) -> p h s j", h=16, s=2)
                        ssw = sinv.rearrange("p (h s j) -> p h s j", h=16, s=2)
                        v.tensor_tensor(vjunk.rearrange("p (h s j) -> p h s j", h=16, s=2)[:, :, 0, :],
                                        qsw[:, :, 1, :], ssw[:, :, 0, :], ALU.mult)
                        v.tensor_tensor(vjunk.rearrange("p (h s j) -> p h s j", h=16, s=2)[:, :, 1, :],
                                        qsw[:, :, 0, :], ssw[:, :, 1, :], ALU.mult)
                        v.tensor_tensor(qrv[:, 0:1024], vrot[:, :], vjunk[:, :], ALU.add)
                        v.tensor_tensor(vrot[:, 0:256], qnv[:, 1024:1280], cosv[:, 0:256], ALU.mult)
                        ksw = qnv[:, 1024:1280].rearrange("p (h s j) -> p h s j", h=4, s=2)
                        ssw4 = sinv[:, 0:256].rearrange("p (h s j) -> p h s j", h=4, s=2)
                        v.tensor_tensor(vjunk[:, 0:256].rearrange("p (h s j) -> p h s j", h=4, s=2)[:, :, 0, :],
                                        ksw[:, :, 1, :], ssw4[:, :, 0, :], ALU.mult)
                        v.tensor_tensor(vjunk[:, 0:256].rearrange("p (h s j) -> p h s j", h=4, s=2)[:, :, 1, :],
                                        ksw[:, :, 0, :], ssw4[:, :, 1, :], ALU.mult)
                        v.tensor_tensor(qrv[:, 1024:1280], vrot[:, 0:256], vjunk[:, 0:256], ALU.add)
                        v.sig(sm["rot"], k + 2)
                        v.tensor_copy(vb.rearrange("p (u h j) -> p u h j", u=NTT, h=4)[:, tt, :, 0:64],
                                      qnv[:, 1280:1536].rearrange("p (h j) -> p h j", h=4))
                        v.sig(sm["ve"], k + 1)
                        v.wait_ge(sm["htp"], 2 * k + 1)
                        v.tensor_copy(qkT.rearrange("p (m u) -> p m u", m=10)[:, 0:8, tt * 128:(tt + 1) * 128],
                                      ps_tp[:, :].rearrange("p (c u) -> p c u", c=8))
                        v.sig(sm["hxc"], 2 * k + 1)
                        v.wait_ge(sm["htp"], 2 * k + 2)
                        v.tensor_copy(qkT.rearrange("p (m u) -> p m u", m=10)[:, 8:10, tt * 128:(tt + 1) * 128],
                                      ps_tp[:, 0:256].rearrange("p (c u) -> p c u", c=2))
                        v.sig(sm["hxc"], 2 * k + 2)
                    for h in range(16):
                        pb = (h % 2) * 64
                        for tc in range(NCH):
                            tiles = fl_tiles[tc]
                            c_idx = b * 64 + h * NCH + tc
                            opar = c_idx % 2
                            for i, tt in enumerate(tiles):
                                gi = (b * 16 + h) * TPB + fl_off[tc] + i
                                v.wait_ge(sm["pf"], gi + 1)
                                j = tt - 4 * tc
                                if 0 <= j < 4:
                                    v.tensor_tensor(ptile[:, (gi % 2) * ES:(gi % 2 + 1) * ES],
                                                    ptile[:, (gi % 2) * ES:(gi % 2 + 1) * ES],
                                                    maskt[:, j * ES:(j + 1) * ES], ALU.mult)
                                else:
                                    v.nop()
                                v.sig(sm["pm"], gi + 1)
                            gi_last = (b * 16 + h) * TPB + fl_off[tc] + len(tiles) - 1
                            v.wait_ge(sm["av"], gi_last + 1)
                            v.reciprocal(linv[0:1, opar * ES:(opar + 1) * ES],
                                         ps_o[64:65, opar * ES:(opar + 1) * ES])
                            v.sig(sm["li"], c_idx + 1)
                            v.wait_ge(sm["rp"], c_idx + 2)
                            v.tensor_copy(repsb[0:64, opar * ES:(opar + 1) * ES], ps_r[0:64, :])
                            v.sig(sm["rpc"], c_idx + 1)
                            v.tensor_tensor(ofm[pb:pb + 64, (h // 2) * T + tc * ES:(h // 2) * T + (tc + 1) * ES],
                                            ps_o[0:64, opar * ES:(opar + 1) * ES],
                                            repsb[0:64, opar * ES:(opar + 1) * ES],
                                            ALU.mult)
                            v.sig(sm["oe"], c_idx + 1)
                    for tt in range(NTT):
                        k = b * NTT + tt
                        v.wait_ge(sm["x%d" % (tt % 2)], 16 * (32 * b + 24 + tt // 2 + 1))
                        if tt >= 2:
                            v.wait_ge(sm["st%d" % (tt % 2)], 16 * (b * 8 + tt // 2))
                        elif b > 0:
                            v.wait_ge(sm["st%d" % (tt % 2)], 16 * (b * 8))
                        for nchunk in range(2):
                            i = tt * 2 + nchunk
                            v.wait_ge(sm["pj"], b * 32 + i + 1)
                            v.tensor_tensor(ytile[:, (tt % 2) * 1024 + nchunk * ES:(tt % 2) * 1024 + (nchunk + 1) * ES],
                                            ps_q[:, (nchunk % 2) * ES:(nchunk % 2 + 1) * ES],
                                            xtile[:, (tt % 2) * 1024 + nchunk * ES:(tt % 2) * 1024 + (nchunk + 1) * ES],
                                            ALU.add)
                            v.sig(sm["ye" if nchunk == 0 else "yd"], b * NTT + tt + 1)

    with ExitStack() as _es:
        def A(nm, shape, dt):
            return _es.enter_context(nc.sbuf_tensor(tag + "M" + nm, shape, dt))

        NT = NTOK // 128
        eyeb2 = A("eyeb", [128, 128], BF16)
        eyef2 = A("eyef", [128, 128], F32)
        iota = A("iota", [128, 8], F32)
        trilm = A("trilm", [128, 128], BF16)
        onesc = A("onesc", [128, 1], F32)
        onesr2 = A("onesr", [64, 128], F32)
        g2rep = A("g2rep", [128, 1024], F32)
        rwsb = A("rwsb", [128, KC * 24], F32)
        rwbf = A("rwbf", [128, KC * 24], BF16)
        brep = A("brep", [128, 8], F32)
        cmrow = A("cmrow", [1, 8], F32)
        pmcol = A("pmcol", [8, 1], F32)
        stg = A("stg", [128, 1024], BF16)
        htile = A("htile", [128, 2 * 1024], F32)
        xtile2 = A("xtile", [128, 2 * 1024], F32)
        stat = A("stat", [128, 16], F32)
        sjunk = A("sjunk", [128, 1024], F32)
        vjunk = A("vjunk", [128, 1024], F32)
        xhi = A("xhi", [128, 2 * 1024], BF16)
        xlo2 = A("xlo2", [128, 2 * 1024], BF16)
        xT2row = A("xT2row", [128, 2 * 1024], BF16)
        xlorow = A("xlorow", [128, 1024], BF16)
        rtt = A("rtt", [128, 72], F32)
        rst = A("rst", [128, NT * 5], F32)
        dbgl = A("dbgl", [128, NT * 8], F32) if DBG_ROUTE else None
        fT = A("fT", [8, 3 * 128], F32)
        ofs = A("ofs", [1, NT * 16], F32)
        orun = A("orun", [1, 16], F32)
        ccsb = A("ccsb", [8, 16], F32)
        coreoff = A("coreoff", [1, 16], F32)
        rstd2 = A("rstd2", [128, NT], F32)
        cxa = A("cxa", [128, NT], F32)
        ce_all = A("Mce", [128, NT * 8], F32)
        xs = A("xs", [128, KC * ES], BF16)
        wbA = A("wbA", [128, KC * 2048], BF16)
        wbB = A("wbB", [128, KC * 2048], BF16)
        wbC = A("wbC", [128, IC * 1024], BF16)
        hti = A("hti", [128, IC * ES], BF16)
        hs = A("hs", [128, ES], F32)
        ghin = A("ghin", [128, ES], F32)
        yacc = A("yacc", [128, 4 * 1024], F32)
        ytile2 = A("ytile", [128, 2 * 1024], F32)
        grow2 = ytile2
        ps_a = _es.enter_context(nc.psum_tensor(tag + "Mps_a", [128, 2 * ES], F32))
        ps_b2 = _es.enter_context(nc.psum_tensor(tag + "Mps_b", [128, 2 * ES], F32))
        ps_y = _es.enter_context(nc.psum_tensor(tag + "Mps_y", [128, 2 * ES], F32))
        ps_tp = _es.enter_context(nc.psum_tensor(tag + "Mps_tp", [128, 1024], BF16))
        ps_r = _es.enter_context(nc.psum_tensor(tag + "Mps_r", [128, 128], F32))

        LG, V0, EQ0, INC, F0, L2, F1, V1, DD, W0, W1C, WN0, WN1, PS0, PS1 = (
            0, 8, 9, 17, 25, 33, 41, 49, 50, 51, 52, 53, 54, 55, 56)

        with nc.Block(tag + "B") as blk:
            @blk.sync
            def _(s):
                s.dma_start(out=eyeb2[:, :], in_=eyeb_in[:, :]).then_inc(sm["fy"], 16)
                s.dma_start(out=eyef2[:, :], in_=eyef_in[:, :]).then_inc(sm["fy"], 16)
                s.dma_start(out=iota[:, :], in_=iota_in[:, :]).then_inc(sm["fy"], 16)
                s.dma_start(out=trilm[:, :], in_=trilm_in[:, :]).then_inc(sm["fy"], 16)
                s.dma_start(out=onesc[:, :], in_=onesc_in[:, :]).then_inc(sm["fy"], 16)
                s.dma_start(out=onesr2[:, :], in_=p.extra_aps[tag + "_ones"][:, :]).then_inc(sm["fy"], 16)
                s.dma_start(out=grow2[0:1, 0:1024], in_=g2.rearrange("(a q) -> a q", a=1)).then_inc(sm["fy"], 16)
                s.dma_start(out=grow2[32:33, 0:8], in_=router_b.rearrange("(a q) -> a q", a=1)).then_inc(sm["fy"], 16)
                s.dma_start(out=rwsb.rearrange("q (kc j) -> q kc j", kc=KC)[:, :, 0:8],
                            in_=router_w.rearrange("(kc q) e -> q kc e", q=128)).then_inc(sm["fy"], 16)
                s.dma_start(out=cmrow[:, :], in_=cmask_row[:, :]).then_inc(sm["fy"], 16)
                s.dma_start(out=pmcol[:, :], in_=pmask_col[:, :]).then_inc(sm["fy"], 16)
                for tt in range(NT):
                    if tt >= 2:
                        s.wait_ge(sm["hlo"], 2 * tt - 3)
                    s.dma_start(out=htile[:, (tt % 2) * 1024:(tt % 2 + 1) * 1024],
                                in_=h_h[tt * 128:(tt + 1) * 128, :]).then_inc(sm["hx%d" % (tt % 2)], 16)
                    s.wait_ge(sm["Mhxc"], 2 * tt + 1)
                    s.dma_start(out=xT2_h.rearrange("q (kc u) -> q kc u", kc=KC)[:, :, tt * 128:(tt + 1) * 128],
                                in_=xT2row[:, (tt % 2) * 1024:(tt % 2 + 1) * 1024].rearrange("q (c u) -> q c u", c=KC)
                                ).then_inc(sm["hxw%d" % (tt % 2)], 16)
                s.wait_ge(sm["cvt"], 16 * (32 + 384))
                for t2 in range(NT2 + 1):
                    if t2 >= 1:
                        c2 = t2 - 1
                        for j in range(4):
                            tt = c2 * 4 + j
                            if tt < 2:
                                s.wait_ge(sm["hlo"], 2 * NT)
                            else:
                                s.wait_ge(sm["Myd"], tt - 1)
                            s.dma_start(out=xtile2[:, (tt % 2) * 1024:(tt % 2 + 1) * 1024],
                                        in_=xin[tt * 128:(tt + 1) * 128, :]).then_inc(sm["fx%d" % (tt % 2)], 16)
                            s.dma_start(out=htile[:, (tt % 2) * 1024:(tt % 2 + 1) * 1024],
                                        in_=h_h[tt * 128:(tt + 1) * 128, :]).then_inc(sm["fh%d" % (tt % 2)], 16)
                            s.wait_ge(sm["Myd"], tt + 1)
                            s.dma_start(out=xout[tt * 128:(tt + 1) * 128, :],
                                        in_=ytile2[:, (tt % 2) * 1024:(tt % 2 + 1) * 1024]).then_inc(sm["ff%d" % (tt % 2)], 16)
                    if t2 < NT2:
                        s.wait_ge(sm["hxw0"], 16 * (NT // 2))
                        s.wait_ge(sm["hxw1"], 16 * (NT // 2))
                        if t2 >= 1:
                            s.wait_ge(sm["he"], ((t2 - 1) * 8 + 8) * IC)
                        s.dma_start(out=xs[:, :].rearrange("q (kc j) -> q kc j", kc=KC),
                                    in_=xT2_h.rearrange("q (kc u) -> q kc u", kc=KC)[:, :, t2 * ES:(t2 + 1) * ES]
                                    ).then_inc(sm["xs0"], 16)
                        for e in range(E):
                            g = t2 * 8 + e
                            if g >= 1:
                                s.wait_ge(sm["he"], g * IC)
                            s.dma_start(out=wbA[:, :], in_=w1c[e, :, :]).then_inc(sm["wl0"], 16)
                            s.dma_start(out=wbB[:, :], in_=w2c[e, :, :]).then_inc(sm["wl1"], 16)
                            if g >= 1:
                                s.wait_ge(sm["Mye"], g * 8)
                            s.dma_start(out=wbC[:, :], in_=w3c[e, :, :]).then_inc(sm["wl2"], 16)
                s.wait_ge(sm["ff0"], 16 * (NT // 2))
                s.wait_ge(sm["ff1"], 16 * (NT // 2))

            @blk.gpsimd
            def _(g):
                cv = 32
                lv = 0
                for e in range(E):
                    for (dst, src) in ((w1c, w1), (w2c, w2)):
                        for kc in range(KC):
                            for hh in range(2):
                                g.wait_ge(sm["cvt"], cv * 16)
                                g.dma_start(out=stg[:, :],
                                            in_=src[e, kc * 128:(kc + 1) * 128, hh * 1024:(hh + 1) * 1024]).then_inc(sm["wc"], 16)
                                cv += 1
                                lv += 1
                                g.wait_ge(sm["wc"], lv * 16)
                                g.dma_start(out=dst[e, :, kc * 2048 + hh * 1024:kc * 2048 + (hh + 1) * 1024],
                                            in_=stg[:, :]).then_inc(sm["cvt"], 16)
                    for ic in range(IC):
                        g.wait_ge(sm["cvt"], cv * 16)
                        g.dma_start(out=stg[:, 0:1024],
                                    in_=w3[e, ic * 128:(ic + 1) * 128, :]).then_inc(sm["wc"], 16)
                        cv += 1
                        lv += 1
                        g.wait_ge(sm["wc"], lv * 16)
                        g.dma_start(out=w3c[e, :, ic * 1024:(ic + 1) * 1024],
                                    in_=stg[:, 0:1024]).then_inc(sm["cvt"], 16)
                g.wait_ge(sm["rs"], 1)
                g.dma_start(out=cc_in[:, :], in_=ccsb[:, :]).then_inc(sm["cc"], 16)
                g.wait_ge(sm["cc"], 16)
                if cc_mock:
                    g.dma_start(out=cc_out[:, :], in_=cc_in[:, :]).then_inc(sm["cc"], 16)
                else:
                    g.collective_compute(
                        "AllReduce", mybir.AluOpType.add,
                        replica_groups=[list(range(8))],
                        ins=[cc_in.opt()], outs=[cc_out.opt()],
                    ).then_inc(sm["cc"])
                g.wait_ge(sm["cc"], 32 if cc_mock else 17)
                g.dma_start(out=ccsb[:, :], in_=cc_out[:, :]).then_inc(sm["ccd"], 16)

            @blk.tensor
            def _(t):
                t.wait_ge(sm["fy"], 16 * 11)
                t.matmul(ps_y[:, 0:ES], onesr2[0:1, :], grow2[0:1, 0:ES], start=True, stop=True)
                t.matmul(ps_y[:, ES:2 * ES], onesr2[0:1, :], grow2[0:1, ES:2 * ES], start=True, stop=True)
                t.matmul(ps_r[:, 0:8], onesr2[32:33, :], grow2[32:33, 0:8], start=True, stop=True).then_inc(sm["rrow"], 1)
                t.wait_ge(sm["Mrle"], 1)
                for tt in range(NT):
                    t.wait_ge(sm["hlo"], 2 * tt + 1)
                    if tt >= 1:
                        t.wait_ge(sm["Mhxc"], 2 * tt)
                    for kc in range(KC):
                        ins = t.transpose(ps_tp[:, kc * 128:(kc + 1) * 128],
                                          xhi[:, (tt % 2) * 1024 + kc * 128:(tt % 2) * 1024 + (kc + 1) * 128],
                                          eyeb2[:, :])
                    ins.then_inc(sm["Mhtp"], 1)
                    t.wait_ge(sm["Mhxc"], 2 * tt + 1)
                    t.wait_ge(sm["hlo"], 2 * tt + 2)
                    for kc in range(KC):
                        ins = t.transpose(ps_tp[:, kc * 128:(kc + 1) * 128],
                                          xlo2[:, (tt % 2) * 1024 + kc * 128:(tt % 2) * 1024 + (kc + 1) * 128],
                                          eyeb2[:, :])
                    ins.then_inc(sm["Mhtp"], 1)
                    t.wait_ge(sm["Mhxc"], 2 * tt + 2)
                    t.wait_ge(sm["Mrle"], tt + 1)
                    xr = xT2row[:, (tt % 2) * 1024:(tt % 2 + 1) * 1024]
                    for kc in range(KC):
                        t.matmul(ps_r[:, 8:16], xr[:, kc * 128:(kc + 1) * 128],
                                 rwbf[:, kc * 24 + 0:kc * 24 + 8], start=(kc == 0), stop=False)
                        t.matmul(ps_r[:, 8:16], xr[:, kc * 128:(kc + 1) * 128],
                                 rwbf[:, kc * 24 + 8:kc * 24 + 16], start=False, stop=False)
                        t.matmul(ps_r[:, 8:16], xlorow[:, kc * 128:(kc + 1) * 128],
                                 rwbf[:, kc * 24 + 16:kc * 24 + 24], start=False, stop=False)
                        ins = t.matmul(ps_r[:, 8:16], xlorow[:, kc * 128:(kc + 1) * 128],
                                       rwbf[:, kc * 24 + 8:kc * 24 + 16], start=False,
                                       stop=(kc == KC - 1))
                    ins.then_inc(sm["Mrtm"], 1)
                    t.wait_ge(sm["Mrt"], tt + 1)
                    t.matmul(ps_r[0:1, 16:24], onesc[:, :], rtt[:, F0:F0 + 8], start=True, stop=True)
                    t.matmul(ps_r[0:1, 24:32], onesc[:, :], rtt[:, F1:F1 + 8], start=True, stop=True).then_inc(sm["Msc"], 1)
                    t.transpose(ps_a[0:8, 0:128], rtt[:, F0:F0 + 8], eyef2[:, :])
                    t.transpose(ps_a[0:8, 128:256], rtt[:, F1:F1 + 8], eyef2[:, :]).then_inc(sm["Msc"], 1)
                    t.wait_ge(sm["Mpf"], tt + 1)
                    if tt >= 1:
                        t.wait_ge(sm["Moe"], tt)
                    t.matmul(ps_b2[:, 0:128], fT[0:8, 0:128], fT[0:8, 0:128], start=True, stop=True)
                    t.matmul(ps_b2[:, 128:256], fT[0:8, 128:256], fT[0:8, 256:384], start=True, stop=True).then_inc(sm["Mpm"], 1)
                t.wait_ge(sm["co"], 1)
                t.matmul(ps_r[0:8, 32:48], cmrow[:, :], orun[0:1, :], start=True, stop=True).then_inc(sm["Mav"], 1)
                t.wait_ge(sm["ccd"], 16)
                t.wait_ge(sm["Mli"], 1)
                t.matmul(ps_r[0:1, 48:64], pmcol[:, :], ccsb[:, :], start=True, stop=True).then_inc(sm["Mav"], 1)
                for tt in range(NT):
                    t.wait_ge(sm["Moe"], NT + tt + 1)
                    if tt >= 1:
                        t.wait_ge(sm["Mce"], tt)
                    t.matmul(ps_r[:, 64:72], onesr2[0:1, :],
                             ofs[0:1, tt * 16:tt * 16 + 8], start=True, stop=True)
                    t.matmul(ps_r[:, 72:80], onesr2[0:1, :],
                             ofs[0:1, tt * 16 + 8:tt * 16 + 16], start=True, stop=True).then_inc(sm["Mrp"], 1)
                for t2 in range(NT2):
                    t.wait_ge(sm["xs0"], 16 * (t2 + 1))
                    xsl = xs[:, :].rearrange("q (kc j) -> q kc j", kc=KC)
                    for e in range(E):
                        g = t2 * 8 + e
                        t.wait_ge(sm["wl0"], 16 * (g + 1))
                        t.wait_ge(sm["wl1"], 16 * (g + 1))
                        for it in range(IC):
                            hcnt = g * IC + it
                            if hcnt >= 2:
                                t.wait_ge(sm["he"], hcnt - 1)
                            for kc in range(KC):
                                t.matmul(ps_a[:, (hcnt % 2) * ES:(hcnt % 2 + 1) * ES],
                                         wbA[:, kc * 2048 + it * 128:kc * 2048 + (it + 1) * 128],
                                         xsl[:, kc, :], start=(kc == 0), stop=(kc == KC - 1))
                            for kc in range(KC):
                                ins = t.matmul(ps_b2[:, (hcnt % 2) * ES:(hcnt % 2 + 1) * ES],
                                               wbB[:, kc * 2048 + it * 128:kc * 2048 + (it + 1) * 128],
                                               xsl[:, kc, :], start=(kc == 0), stop=(kc == KC - 1))
                            ins.then_inc(sm["h1"], 1)
                        t.wait_ge(sm["wl2"], 16 * (g + 1))
                        t.wait_ge(sm["he"], (g + 1) * IC)
                        for j in range(4):
                            for hc in range(2):
                                ycnt = g * 8 + j * 2 + hc
                                if ycnt >= 2:
                                    t.wait_ge(sm["Mye"], ycnt - 1)
                                elif g == 0:
                                    t.wait_ge(sm["Mye"], 0)
                                for ic in range(IC):
                                    ins = t.matmul(ps_y[:, (ycnt % 2) * ES:(ycnt % 2 + 1) * ES],
                                                   hti[:, ic * ES + j * 128:ic * ES + (j + 1) * 128],
                                                   wbC[:, ic * 1024 + hc * ES:ic * 1024 + (hc + 1) * ES],
                                                   start=(ic == 0), stop=(ic == IC - 1))
                                ins.then_inc(sm["yp"], 1)

            @blk.scalar
            def _(sc_raw):
                sc = SelfSync(sc_raw, sm["sch2"])
                for tt in range(NT):
                    sc.wait_ge(sm["hx%d" % (tt % 2)], 16 * (tt // 2 + 1))
                    ha = htile[:, (tt % 2) * 1024:(tt % 2 + 1) * 1024]
                    st8 = (tt % 2) * 8
                    sc.activation(sjunk[:, :], ha, AF.Square,
                                  accum_out=stat[:, st8 + 1:st8 + 2])
                    sc.activation(stat[:, st8 + 3:st8 + 4],
                                  stat[:, st8 + 1:st8 + 2], AF.Sqrt,
                                  bias=1e-6, scale=1.0 / 1024.0)
                    sc.sig(sm["hsd"], tt + 1)
                    sc.wait_ge(sm["hvb"], tt + 1)
                    sc.activation(rtt[:, W0:W0 + 1], rtt[:, DD:DD + 1], AF.Sigmoid,
                                  bias=0.0, scale=1.0)
                    sc.sig(sm["Msg"], tt + 1)
                for t2 in range(NT2):
                    for e in range(E):
                        g = t2 * 8 + e
                        for it in range(IC):
                            hcnt = g * IC + it
                            sc.wait_ge(sm["h1"], hcnt + 1)
                            if hcnt >= 1:
                                sc.wait_ge(sm["gh"], hcnt)
                            sc.activation(hs[:, 0:ES],
                                          ps_b2[:, (hcnt % 2) * ES:(hcnt % 2 + 1) * ES],
                                          AF.Identity, bias=0.0, scale=1.0)
                            sc.sig(sm["slh"], hcnt + 1)
                            sc.wait_ge(sm["gh"], hcnt + 1)
                            sc.activation(hti[:, it * ES:(it + 1) * ES], ghin[:, :],
                                          AF.Gelu, bias=0.0, scale=1.0)
                            sc.sig(sm["he"], hcnt + 1)

            @blk.vector
            def _(v_raw):
                v = SelfSync(v_raw, sm["vch2"])
                v.wait_ge(sm["rrow"], 1)
                v.tensor_copy(g2rep[:, 0:ES], ps_y[:, 0:ES])
                v.tensor_copy(g2rep[:, ES:2 * ES], ps_y[:, ES:2 * ES])
                v.tensor_copy(brep[:, :], ps_r[:, 0:8])
                v.wait_ge(sm["fy"], 16 * 11)
                rv = rwsb.rearrange("q (kc j) -> q kc j", kc=KC)
                rb = rwbf.rearrange("q (kc j) -> q kc j", kc=KC)
                v.tensor_copy(rb[:, :, 0:8], rv[:, :, 0:8])
                v.tensor_copy(rv[:, :, 8:16], rb[:, :, 0:8])
                v.tensor_tensor(rv[:, :, 16:24], rv[:, :, 0:8], rv[:, :, 8:16], ALU.subtract)
                v.tensor_copy(rb[:, :, 8:16], rv[:, :, 16:24])
                v.tensor_copy(rb[:, :, 16:24], rb[:, :, 0:8])
                v.memset(orun[:, :], 0.0)
                v.sig(sm["Mrle"], 1)
                for tt in range(NT):
                    ha = htile[:, (tt % 2) * 1024:(tt % 2 + 1) * 1024]
                    st8 = (tt % 2) * 8
                    v.wait_ge(sm["hx%d" % (tt % 2)], 16 * (tt // 2 + 1))
                    v.wait_ge(sm["hsd"], tt + 1)
                    v.reciprocal(stat[:, st8 + 5:st8 + 6], stat[:, st8 + 3:st8 + 4])
                    v.tensor_copy(rstd2[:, tt:tt + 1], stat[:, st8 + 5:st8 + 6])
                    v.tensor_scalar(vjunk[:, :], ha, stat[:, st8 + 5:st8 + 6], None, ALU.mult)
                    v.tensor_tensor(vjunk[:, :], vjunk[:, :], g2rep[:, :], ALU.mult)
                    if tt >= 2:
                        v.wait_ge(sm["Mhtp"], 2 * tt - 2)
                    v.tensor_copy(xhi[:, (tt % 2) * 1024:(tt % 2 + 1) * 1024], vjunk[:, :])
                    v.sig(sm["hlo"], 2 * tt + 1)
                    v.tensor_tensor(xlo2[:, (tt % 2) * 1024:(tt % 2 + 1) * 1024], vjunk[:, :],
                                    xhi[:, (tt % 2) * 1024:(tt % 2 + 1) * 1024], ALU.subtract)
                    v.sig(sm["hlo"], 2 * tt + 2)
                    v.wait_ge(sm["Mhtp"], 2 * tt + 1)
                    if tt >= 2:
                        v.wait_ge(sm["hxw%d" % (tt % 2)], 16 * (tt // 2))
                    v.tensor_copy(xT2row[:, (tt % 2) * 1024:(tt % 2 + 1) * 1024].rearrange("q (c u) -> q c u", c=KC),
                                  ps_tp[:, :].rearrange("q (c u) -> q c u", c=KC))
                    v.sig(sm["Mhxc"], 2 * tt + 1)
                    v.wait_ge(sm["Mhtp"], 2 * tt + 2)
                    v.tensor_copy(xlorow[:, :].rearrange("q (c u) -> q c u", c=KC),
                                  ps_tp[:, :].rearrange("q (c u) -> q c u", c=KC))
                    v.sig(sm["Mhxc"], 2 * tt + 2)
                    v.wait_ge(sm["Mrtm"], tt + 1)
                    v.tensor_tensor(rtt[:, LG:LG + 8], ps_r[:, 8:16], brep[:, :], ALU.add)
                    if DBG_ROUTE:
                        v.tensor_copy(dbgl[:, tt * 8:(tt + 1) * 8], rtt[:, LG:LG + 8])
                    v.reduce_max(rtt[:, V0:V0 + 1], rtt[:, LG:LG + 8], mybir.AxisListType.X)
                    v.tensor_scalar(rtt[:, EQ0:EQ0 + 8], rtt[:, LG:LG + 8],
                                    rtt[:, V0:V0 + 1], None, ALU.is_equal)
                    v.tensor_tensor_scan(rtt[:, INC:INC + 8], rtt[:, EQ0:EQ0 + 8],
                                         rtt[:, EQ0:EQ0 + 8], 0.0, ALU.add, ALU.max)
                    v.tensor_scalar(rtt[:, F0:F0 + 8], rtt[:, INC:INC + 8], 1.0, None, ALU.is_equal)
                    v.tensor_tensor(rtt[:, F0:F0 + 8], rtt[:, F0:F0 + 8],
                                    rtt[:, EQ0:EQ0 + 8], ALU.mult)
                    v.tensor_scalar(rtt[:, L2:L2 + 8], rtt[:, F0:F0 + 8], -BIG, None, ALU.mult)
                    v.tensor_tensor(rtt[:, L2:L2 + 8], rtt[:, L2:L2 + 8],
                                    rtt[:, LG:LG + 8], ALU.add)
                    v.reduce_max(rtt[:, V1:V1 + 1], rtt[:, L2:L2 + 8], mybir.AxisListType.X)
                    v.tensor_scalar(rtt[:, F1:F1 + 8], rtt[:, L2:L2 + 8],
                                    rtt[:, V1:V1 + 1], None, ALU.is_equal)
                    v.tensor_tensor_scan(rtt[:, INC:INC + 8], rtt[:, F1:F1 + 8],
                                         rtt[:, F1:F1 + 8], 0.0, ALU.add, ALU.max)
                    v.tensor_scalar(rtt[:, EQ0:EQ0 + 8], rtt[:, INC:INC + 8], 1.0, None, ALU.is_equal)
                    v.tensor_tensor(rtt[:, F1:F1 + 8], rtt[:, F1:F1 + 8],
                                    rtt[:, EQ0:EQ0 + 8], ALU.mult)
                    v.tensor_tensor(rtt[:, DD:DD + 1], rtt[:, V0:V0 + 1],
                                    rtt[:, V1:V1 + 1], ALU.subtract)
                    v.sig(sm["hvb"], tt + 1)
                    v.tensor_tensor(rtt[:, L2:L2 + 8], rtt[:, F0:F0 + 8], iota[:, :], ALU.mult)
                    v.reduce_sum(rst[:, tt * 5 + 0:tt * 5 + 1], rtt[:, L2:L2 + 8], mybir.AxisListType.X)
                    v.tensor_tensor(rtt[:, L2:L2 + 8], rtt[:, F1:F1 + 8], iota[:, :], ALU.mult)
                    v.reduce_sum(rst[:, tt * 5 + 1:tt * 5 + 2], rtt[:, L2:L2 + 8], mybir.AxisListType.X)
                    v.sig(sm["Mrt"], tt + 1)
                    v.wait_ge(sm["Msc"], 2 * tt + 2)
                    v.tensor_copy(fT[0:8, 0:128], ps_a[0:8, 0:128])
                    v.tensor_copy(fT[0:8, 128:256], ps_a[0:8, 128:256])
                    v.tensor_tensor(fT[0:8, 256:384], fT[0:8, 0:128], fT[0:8, 128:256], ALU.add)
                    v.sig(sm["Mpf"], tt + 1)
                    v.tensor_copy(ofs[0:1, tt * 16:(tt + 1) * 16], orun[0:1, :])
                    v.tensor_tensor(orun[0:1, 0:8], orun[0:1, 0:8], ps_r[0:1, 16:24], ALU.add)
                    v.tensor_tensor(orun[0:1, 8:16], orun[0:1, 8:16], ps_r[0:1, 16:24], ALU.add)
                    v.tensor_tensor(orun[0:1, 8:16], orun[0:1, 8:16], ps_r[0:1, 24:32], ALU.add)
                    v.sig(sm["Mrle"], tt + 2)
                    v.wait_ge(sm["Mpm"], tt + 1)
                    v.tensor_tensor(vjunk[:, 0:128], ps_b2[:, 0:128], trilm[:, :], ALU.mult)
                    v.reduce_sum(rtt[:, WN0:WN0 + 1], vjunk[:, 0:128], mybir.AxisListType.X)
                    v.tensor_tensor(vjunk[:, 0:128], ps_b2[:, 128:256], trilm[:, :], ALU.mult)
                    v.reduce_sum(rtt[:, WN1:WN1 + 1], vjunk[:, 0:128], mybir.AxisListType.X)
                    v.tensor_copy(rst[:, tt * 5 + 2:tt * 5 + 3], rtt[:, WN0:WN0 + 1])
                    v.tensor_copy(rst[:, tt * 5 + 3:tt * 5 + 4], rtt[:, WN1:WN1 + 1])
                    v.sig(sm["Moe"], tt + 1)
                    v.wait_ge(sm["Msg"], tt + 1)
                    v.tensor_copy(rst[:, tt * 5 + 4:tt * 5 + 5], rtt[:, W0:W0 + 1])
                v.sig(sm["co"], 1)
                v.wait_ge(sm["Mav"], 1)
                v.tensor_copy(ccsb[:, :], ps_r[0:8, 32:48])
                v.sig(sm["rs"], 1)
                v.wait_ge(sm["ccd"], 16)
                v.nop()
                v.sig(sm["Mli"], 1)
                v.wait_ge(sm["Mav"], 2)
                v.tensor_copy(coreoff[0:1, :], ps_r[0:1, 48:64])
                for tt in range(NT):
                    v.tensor_tensor(ofs[0:1, tt * 16:(tt + 1) * 16], ofs[0:1, tt * 16:(tt + 1) * 16],
                                    coreoff[0:1, :], ALU.add)
                    v.sig(sm["Moe"], NT + tt + 1)
                    v.wait_ge(sm["Mrp"], tt + 1)
                    f0s = rtt[:, F0:F0 + 8]
                    f1s = rtt[:, F1:F1 + 8]
                    v.tensor_scalar(f0s, iota[:, :], rst[:, tt * 5 + 0:tt * 5 + 1], None, ALU.is_equal)
                    v.tensor_scalar(f1s, iota[:, :], rst[:, tt * 5 + 1:tt * 5 + 2], None, ALU.is_equal)
                    v.tensor_tensor(rtt[:, LG:LG + 8], f0s, ps_r[:, 64:72], ALU.mult)
                    v.reduce_sum(rtt[:, PS0:PS0 + 1], rtt[:, LG:LG + 8], mybir.AxisListType.X)
                    v.tensor_tensor(rtt[:, LG:LG + 8], f1s, ps_r[:, 72:80], ALU.mult)
                    v.reduce_sum(rtt[:, PS1:PS1 + 1], rtt[:, LG:LG + 8], mybir.AxisListType.X)
                    v.sig(sm["Mce"], tt + 1)
                    v.tensor_tensor(rtt[:, PS0:PS0 + 1], rtt[:, PS0:PS0 + 1],
                                    rst[:, tt * 5 + 2:tt * 5 + 3], ALU.add)
                    v.tensor_tensor(rtt[:, PS1:PS1 + 1], rtt[:, PS1:PS1 + 1],
                                    rst[:, tt * 5 + 3:tt * 5 + 4], ALU.add)
                    v.tensor_scalar(rtt[:, V0:V0 + 1], rtt[:, PS0:PS0 + 1], CAP, None, ALU.is_lt)
                    v.tensor_scalar(rtt[:, V1:V1 + 1], rtt[:, PS1:PS1 + 1], CAP, None, ALU.is_lt)
                    w0s = rst[:, tt * 5 + 4:tt * 5 + 5]
                    v.tensor_tensor(rtt[:, DD:DD + 1], rtt[:, V0:V0 + 1], w0s, ALU.mult)
                    v.tensor_scalar(rtt[:, W1C:W1C + 1], w0s, -1.0, 1.0, ALU.mult, ALU.add)
                    v.tensor_tensor(rtt[:, W1C:W1C + 1], rtt[:, W1C:W1C + 1],
                                    rtt[:, V1:V1 + 1], ALU.mult)
                    v.tensor_tensor(rtt[:, INC:INC + 1], rtt[:, DD:DD + 1],
                                    rtt[:, W1C:W1C + 1], ALU.add)
                    v.tensor_scalar(cxa[:, tt:tt + 1], rtt[:, INC:INC + 1], -1.0, 1.0,
                                    ALU.mult, ALU.add)
                    v.tensor_tensor(rtt[:, PS0:PS0 + 1], rst[:, tt * 5 + 0:tt * 5 + 1],
                                    rtt[:, V0:V0 + 1], ALU.mult)
                    v.tensor_tensor(rtt[:, PS1:PS1 + 1], rst[:, tt * 5 + 1:tt * 5 + 2],
                                    rtt[:, V1:V1 + 1], ALU.mult)
                    v.tensor_tensor(rtt[:, EQ0:EQ0 + 1], rtt[:, PS0:PS0 + 1],
                                    rtt[:, PS1:PS1 + 1], ALU.max)
                    v.tensor_scalar(rtt[:, LG:LG + 8], iota[:, :], rtt[:, EQ0:EQ0 + 1],
                                    None, ALU.is_equal)
                    v.tensor_scalar(ce_all[:, tt * 8:(tt + 1) * 8], rtt[:, LG:LG + 8],
                                    rtt[:, INC:INC + 1], None, ALU.mult)
                    v.sig(sm["Mve"], tt + 1)
                for t2 in range(NT2):
                    for e in range(E):
                        g = t2 * 8 + e
                        for it in range(IC):
                            hcnt = g * IC + it
                            v.wait_ge(sm["slh"], hcnt + 1)
                            if it == 0 and g >= 1:
                                v.wait_ge(sm["yp"], g * 8)
                            v.tensor_tensor(ghin[:, :],
                                            ps_a[:, (hcnt % 2) * ES:(hcnt % 2 + 1) * ES],
                                            hs[:, 0:ES],
                                            ALU.mult)
                            v.sig(sm["gh"], hcnt + 1)
                        for j in range(4):
                            tt = t2 * 4 + j
                            for hc in range(2):
                                ycnt = g * 8 + j * 2 + hc
                                v.wait_ge(sm["yp"], ycnt + 1)
                                v.wait_ge(sm["Mve"], tt + 1)
                                cecol = ce_all[:, tt * 8 + e:tt * 8 + e + 1]
                                ya = yacc[:, j * 1024 + hc * ES:j * 1024 + (hc + 1) * ES]
                                if e == 0:
                                    v.tensor_scalar(ya, ps_y[:, (ycnt % 2) * ES:(ycnt % 2 + 1) * ES],
                                                    cecol, None, ALU.mult)
                                else:
                                    v.scalar_tensor_tensor(ya, ps_y[:, (ycnt % 2) * ES:(ycnt % 2 + 1) * ES],
                                                           cecol, ya, ALU.mult, ALU.add)
                                v.sig(sm["Mye"], ycnt + 1)
                    for j in range(4):
                        tt = t2 * 4 + j
                        v.wait_ge(sm["fx%d" % (tt % 2)], 16 * (tt // 2 + 1))
                        v.wait_ge(sm["fh%d" % (tt % 2)], 16 * (tt // 2 + 1))
                        if tt >= 2:
                            v.wait_ge(sm["ff%d" % (tt % 2)], 16 * (tt // 2))
                        v.tensor_tensor(rtt[:, DD:DD + 1], cxa[:, tt:tt + 1],
                                        rstd2[:, tt:tt + 1], ALU.mult)
                        for hc in range(2):
                            sl2 = slice((tt % 2) * 1024 + hc * ES, (tt % 2) * 1024 + (hc + 1) * ES)
                            v.scalar_tensor_tensor(vjunk[:, hc * ES:(hc + 1) * ES],
                                                   htile[:, sl2], rtt[:, DD:DD + 1],
                                                   g2rep[:, hc * ES:(hc + 1) * ES],
                                                   ALU.mult, ALU.mult)
                            v.tensor_tensor(vjunk[:, hc * ES:(hc + 1) * ES],
                                            vjunk[:, hc * ES:(hc + 1) * ES],
                                            yacc[:, j * 1024 + hc * ES:j * 1024 + (hc + 1) * ES],
                                            ALU.add)
                            v.tensor_tensor(vjunk[:, hc * ES:(hc + 1) * ES],
                                            vjunk[:, hc * ES:(hc + 1) * ES],
                                            htile[:, sl2], ALU.add)
                            v.tensor_tensor(ytile2[:, sl2], vjunk[:, hc * ES:(hc + 1) * ES],
                                            xtile2[:, sl2], ALU.add)
                        v.sig(sm["Myd"], tt + 1)


def emit_l0_attn_b(p, xin, xout, wjT, wvT, woT, tag="ab"):
    nc = p.nc
    eyeb_in = p.const_input(tag + "_eyeb", np.eye(128, dtype=BFDT))
    eyef_in = p.const_input(tag + "_eyef", np.eye(128, dtype=np.float32))
    sl = np.arange(128)[:, None]
    tl = np.arange(128)[None, :]
    tril_in = p.const_input(tag + "_tril", (sl <= tl).astype(BFDT))

    scrow_h = nc.dram_tensor(tag + "_scrow", [T], F32).ap()
    mrow_h = nc.dram_tensor(tag + "_mrow", [T], F32).ap()

    sm = {n: p.sem(tag + n) for n in
          ["w", "x0", "x1", "vb", "sd", "ln", "tp", "xT", "ye", "yv", "ve",
           "scv", "srt", "src", "srd", "srl", "rmx", "sc", "pf", "pm",
           "dn", "av", "oe", "ot", "oc", "pj", "yo", "yd0", "yd1", "rc", "rc2", "vch", "sch"]}

    with ExitStack() as _es:
        def A(nm, shape, dt):
            return _es.enter_context(nc.sbuf_tensor(tag + nm, shape, dt))

        xtile = A("xtile", [128, 2 * 1024], F32)
        sjunk = A("sjunk", [128, 1024], F32)
        vjunk = A("vjunk", [128, 1024], F32)
        stat = A("stat", [128, 16], F32)
        xlnb = A("xlnb", [128, 2 * 1024], BF16)
        xT = A("xT", [128, KC * T], BF16)
        valc = A("valc", [128, NTT * 1024], BF16)
        wjsb = A("wjsb", [128, KC * 1024], BF16)
        wvsb = A("wvsb", [128, KC * 1024], BF16)
        wosb = A("wosb", [128, KC * 1024], BF16)
        scc = A("scc", [128, 16], F32)
        scr = A("scr", [16, 128], F32)
        lhsb = A("lhsb", [2, T], F32)
        rhs2 = A("rhs2", [2, T], F32)
        rmax = A("rmax", [1, T], F32)
        mrow = A("mrow", [1, T], F32)
        pch = A("pch", [128, NTT * 128], BF16)
        onesb = A("onesb", [128, 1], BF16)
        trilb = A("trilb", [128, 128], BF16)
        eyeb = A("eyeb", [128, 128], BF16)
        eyef = A("eyef", [128, 128], F32)
        ob = A("ob", [128, 1024], BF16)
        oT = A("oT", [128, KC * 128], BF16)
        otile = A("otile", [128, 2 * 1024], F32)
        dcol = A("dcol", [128, 2], F32)
        ps_tp = _es.enter_context(nc.psum_tensor(tag + "ps_tp", [128, 1024], BF16))
        ps_sc = _es.enter_context(nc.psum_tensor(tag + "ps_sc", [128, 1024], F32))
        ps_d = _es.enter_context(nc.psum_tensor(tag + "ps_d", [128, 128], F32))
        ps_av = _es.enter_context(nc.psum_tensor(tag + "ps_av", [128, 1024], F32))
        ps_p = _es.enter_context(nc.psum_tensor(tag + "ps_p", [128, 1024], F32))

        with nc.Block() as blk:
            @blk.sync
            def _(s):
                s.dma_start(out=eyeb[:, :], in_=eyeb_in[:, :]).then_inc(sm["rc"], 16)
                s.dma_start(out=eyef[:, :], in_=eyef_in[:, :]).then_inc(sm["rc"], 16)
                s.dma_start(out=trilb[:, :], in_=tril_in[:, :]).then_inc(sm["rc"], 16)
                for b in range(BLOC):
                    for tt in range(NTT):
                        k = b * NTT + tt
                        if k >= 2:
                            s.wait_ge(sm["ln"], k - 1)
                        s.dma_start(out=xtile[:, (tt % 2) * 1024:(tt % 2 + 1) * 1024],
                                    in_=xin[k * 128:(k + 1) * 128, :]).then_inc(sm["x%d" % (tt % 2)], 16)
                    s.wait_ge(sm["src"], b + 1)
                    s.dma_start(out=scrow_h.rearrange("(u j) -> u j", u=16),
                                in_=scr[:, :]).then_inc(sm["srd"], 16)
                    s.wait_ge(sm["srd"], 48 * b + 16)
                    s.dma_start(out=lhsb[0:1, :],
                                in_=scrow_h.rearrange("(a j) -> a j", a=1)).then_inc(sm["srd"], 16)
                    s.dma_start(out=rhs2[0:1, :],
                                in_=scrow_h.rearrange("(a j) -> a j", a=1)).then_inc(sm["srd"], 16)
                    s.wait_ge(sm["rmx"], b + 1)
                    s.dma_start(out=mrow_h.rearrange("(a j) -> a j", a=1),
                                in_=mrow[0:1, :]).then_inc(sm["srl"], 16)
                    s.wait_ge(sm["srl"], 32 * b + 16)
                    s.dma_start(out=rhs2[1:2, :],
                                in_=mrow_h.rearrange("(a j) -> a j", a=1)).then_inc(sm["srl"], 16)
                    for tt in range(NTT):
                        k = b * NTT + tt
                        if k >= 2:
                            s.wait_ge(sm["yd%d" % (tt % 2)], 16 * (k // 2))
                        s.dma_start(out=xtile[:, (tt % 2) * 1024:(tt % 2 + 1) * 1024],
                                    in_=xin[k * 128:(k + 1) * 128, :]).then_inc(sm["x%d" % (tt % 2)], 16)
                        s.wait_ge(sm["yo"], k + 1)
                        s.dma_start(out=xout[k * 128:(k + 1) * 128, :],
                                    in_=otile[:, (tt % 2) * 1024:(tt % 2 + 1) * 1024]).then_inc(sm["yd%d" % (tt % 2)], 16)
                s.wait_ge(sm["yd0"], 16 * (BLOC * NTT // 2))
                s.wait_ge(sm["yd1"], 16 * (BLOC * NTT // 2))

            @blk.gpsimd
            def _(g):
                for i, (dst, src) in enumerate(((wjsb, wjT), (wvsb, wvT), (wosb, woT))):
                    g.dma_start(out=dst.rearrange("q (kc d) -> q kc d", kc=KC),
                                in_=src.rearrange("(kc q) d -> q kc d", q=128)).then_inc(sm["w"], 16)
                g.wait_ge(sm["w"], 48)

            @blk.tensor
            def _(t):
                t.wait_ge(sm["rc"], 48)
                t.wait_ge(sm["rc2"], 1)
                t.wait_ge(sm["w"], 48)
                for b in range(BLOC):
                    if b >= 1:
                        t.wait_ge(sm["yo"], b * NTT)
                    for tt in range(NTT):
                        k = b * NTT + tt
                        t.wait_ge(sm["ln"], k + 1)
                        if k >= 1:
                            t.wait_ge(sm["xT"], k)
                        for kc in range(KC):
                            ins = t.transpose(ps_tp[:, kc * 128:(kc + 1) * 128],
                                              xlnb[:, (tt % 2) * 1024 + kc * 128:(tt % 2) * 1024 + (kc + 1) * 128],
                                              eyeb[:, :])
                        ins.then_inc(sm["tp"], 1)
                        t.wait_ge(sm["xT"], k + 1)
                        if k >= 1:
                            t.wait_ge(sm["ye"], 2 * k)
                        for half in range(2):
                            for kc in range(KC):
                                ins = t.matmul(ps_av[:, half * ES:(half + 1) * ES],
                                               xT[:, kc * T + tt * 128:kc * T + (tt + 1) * 128],
                                               wjsb[:, kc * 1024 + half * ES:kc * 1024 + (half + 1) * ES],
                                               start=(kc == 0), stop=(kc == KC - 1))
                            ins.then_inc(sm["sc"], 1)
                        if k >= 1:
                            t.wait_ge(sm["ve"], 2 * k)
                        for half in range(2):
                            for kc in range(KC):
                                ins = t.matmul(ps_p[:, half * ES:(half + 1) * ES],
                                               xT[:, kc * T + tt * 128:kc * T + (tt + 1) * 128],
                                               wvsb[:, kc * 1024 + half * ES:kc * 1024 + (half + 1) * ES],
                                               start=(kc == 0), stop=(kc == KC - 1))
                            ins.then_inc(sm["yv"], 1)
                    t.wait_ge(sm["scv"], b * NTT + NTT)
                    if b >= 1:
                        t.wait_ge(sm["dn"], b * NTT)
                    t.transpose(ps_d[0:16, 0:128], scc[:, 0:16], eyef[:, :]).then_inc(sm["srt"], 1)
                    t.wait_ge(sm["srd"], 48 * b + 48)
                    t.wait_ge(sm["srl"], 32 * b + 32)
                    for tt in range(NTT):
                        k = b * NTT + tt
                        for st in range(tt + 1):
                            i2 = k * NTT + st
                            scnt = b * 136 + tt * (tt + 1) // 2 + st
                            if scnt >= 2:
                                t.wait_ge(sm["pf"], scnt - 1)
                            t.matmul(ps_sc[:, (scnt % 2) * 512:(scnt % 2) * 512 + 128],
                                     lhsb[0:2, st * 128:(st + 1) * 128],
                                     rhs2[0:2, tt * 128:(tt + 1) * 128],
                                     start=True, stop=True).then_inc(sm["sc"], 1)
                        if k >= 1:
                            t.wait_ge(sm["oe"], 2 * k)
                        for st in range(tt + 1):
                            scnt = b * 136 + tt * (tt + 1) // 2 + st
                            if st == tt:
                                t.wait_ge(sm["pm"], k + 1)
                            else:
                                t.wait_ge(sm["pf"], scnt + 1)
                            t.matmul(ps_d[:, 0:1], pch[:, st * 128:(st + 1) * 128],
                                     onesb[:, :], start=(st == 0), stop=(st == tt))
                            for half in range(2):
                                ins = t.matmul(ps_av[:, half * ES:(half + 1) * ES],
                                               pch[:, st * 128:(st + 1) * 128],
                                               valc[:, st * 1024 + half * ES:st * 1024 + (half + 1) * ES],
                                               start=(st == 0), stop=(st == tt))
                            if st == tt:
                                ins.then_inc(sm["dn"], 1)
                        t.wait_ge(sm["oe"], 2 * k + 2)
                        for kc in range(KC):
                            ins = t.transpose(ps_tp[:, kc * 128:(kc + 1) * 128],
                                              ob[:, kc * 128:(kc + 1) * 128],
                                              eyeb[:, :])
                        ins.then_inc(sm["ot"], 1)
                        t.wait_ge(sm["oc"], k + 1)
                        if k >= 1:
                            t.wait_ge(sm["yo"], k)
                        for half in range(2):
                            for kc in range(KC):
                                ins = t.matmul(ps_p[:, half * ES:(half + 1) * ES],
                                               oT[:, kc * 128:(kc + 1) * 128],
                                               wosb[:, kc * 1024 + half * ES:kc * 1024 + (half + 1) * ES],
                                               start=(kc == 0), stop=(kc == KC - 1))
                            ins.then_inc(sm["pj"], 1)

            @blk.scalar
            def _(sc_raw):
                sc = SelfSync(sc_raw, sm["sch"])
                for b in range(BLOC):
                    for tt in range(NTT):
                        k = b * NTT + tt
                        sc.wait_ge(sm["x%d" % (tt % 2)], 16 * (b * 16 + tt // 2 + 1))
                        xa = xtile[:, (tt % 2) * 1024:(tt % 2 + 1) * 1024]
                        sc.activation(sjunk[:, :], xa, AF.Square,
                                      accum_out=stat[:, 1:2])
                        sc.wait_ge(sm["vb"], k + 1)
                        sc.activation(stat[:, 3:4], stat[:, 1:2], AF.Sqrt,
                                      bias=stat[:, 2:3], scale=1.0 / 1024.0)
                        sc.sig(sm["sd"], k + 1)
                        for half in range(2):
                            sc.wait_ge(sm["sc"], b * 168 + tt * 2 + half + 1)
                            sc.activation(sjunk[:, half * ES:(half + 1) * ES],
                                          ps_av[:, half * ES:(half + 1) * ES], AF.Square,
                                          accum_out=stat[:, 8 + half:9 + half])
                            sc.sig(sm["ye"], 2 * k + half + 1)
                    for tt in range(NTT):
                        k = b * NTT + tt
                        for st in range(tt + 1):
                            local = tt * (tt + 1) // 2 + st
                            scnt = b * 136 + local
                            sc.wait_ge(sm["sc"], 32 * (b + 1) + b * 136 + local + 1)
                            sc.activation(pch[:, st * 128:(st + 1) * 128],
                                          ps_sc[:, (scnt % 2) * 512:(scnt % 2) * 512 + 128],
                                          AF.Exp, bias=0.0, scale=1.0)
                            sc.sig(sm["pf"], scnt + 1)

            @blk.vector
            def _(v_raw):
                v = SelfSync(v_raw, sm["vch"])
                v.memset(onesb[:, :], 1.0)
                v.memset(lhsb[0:2, :], -1.0)
                v.sig(sm["rc2"], 1)
                for b in range(BLOC):
                    for tt in range(NTT):
                        k = b * NTT + tt
                        xa = xtile[:, (tt % 2) * 1024:(tt % 2 + 1) * 1024]
                        v.wait_ge(sm["x%d" % (tt % 2)], 16 * (b * 16 + tt // 2 + 1))
                        v.tensor_scalar(vjunk[:, :], xa, 1.0, None, ALU.mult, ALU.add,
                                        accum_out=stat[:, 0:1])
                        v.tensor_scalar(stat[:, 4:5], stat[:, 0:1], 1.0 / 1024.0, None, ALU.mult)
                        v.tensor_tensor(stat[:, 5:6], stat[:, 4:5], stat[:, 4:5], ALU.mult)
                        v.tensor_scalar(stat[:, 2:3], stat[:, 5:6], -1.0, 1e-5, ALU.mult, ALU.add)
                        v.sig(sm["vb"], k + 1)
                        v.wait_ge(sm["sd"], k + 1)
                        v.reciprocal(stat[:, 6:7], stat[:, 3:4])
                        v.tensor_scalar(xlnb[:, (tt % 2) * 1024:(tt % 2 + 1) * 1024], xa,
                                        stat[:, 4:5], stat[:, 6:7], ALU.subtract, ALU.mult)
                        v.sig(sm["ln"], k + 1)
                        v.wait_ge(sm["tp"], k + 1)
                        v.tensor_copy(
                            xT.rearrange("p (kc u) -> p kc u", kc=KC)[:, :, tt * 128:(tt + 1) * 128],
                            ps_tp[:, :].rearrange("p (c u) -> p c u", c=KC))
                        v.sig(sm["xT"], k + 1)
                        v.wait_ge(sm["ye"], 2 * k + 2)
                        v.tensor_tensor(stat[:, 10:11], stat[:, 8:9], stat[:, 9:10], ALU.add)
                        v.tensor_scalar(scc[:, tt:tt + 1], stat[:, 10:11], 1.0 / 32.0, None, ALU.mult)
                        v.sig(sm["scv"], k + 1)
                        for half in range(2):
                            v.wait_ge(sm["yv"], 2 * k + half + 1)
                            v.tensor_copy(valc[:, tt * 1024 + half * ES:tt * 1024 + (half + 1) * ES],
                                          ps_p[:, half * ES:(half + 1) * ES])
                            v.sig(sm["ve"], 2 * k + half + 1)
                    v.wait_ge(sm["srt"], b + 1)
                    v.tensor_copy(scr[:, :], ps_d[0:16, 0:128])
                    v.sig(sm["src"], b + 1)
                    v.wait_ge(sm["srd"], 48 * b + 48)
                    v.tensor_tensor_scan(rmax[0:1, :], rhs2[0:1, :], rhs2[0:1, :], 0.0,
                                         ALU.max, ALU.max)
                    v.tensor_tensor(mrow[0:1, :], rmax[0:1, :], rhs2[0:1, :], ALU.mult)
                    v.sig(sm["rmx"], b + 1)
                    for tt in range(NTT):
                        k = b * NTT + tt
                        scnt_d = b * 136 + tt * (tt + 1) // 2 + tt
                        v.wait_ge(sm["pf"], scnt_d + 1)
                        v.tensor_tensor(pch[:, tt * 128:(tt + 1) * 128],
                                        pch[:, tt * 128:(tt + 1) * 128], trilb[:, :], ALU.mult)
                        v.sig(sm["pm"], k + 1)
                        v.wait_ge(sm["dn"], k + 1)
                        v.reciprocal(dcol[:, 0:1], ps_d[:, 0:1])
                        for half in range(2):
                            v.tensor_scalar(ob[:, half * ES:(half + 1) * ES],
                                            ps_av[:, half * ES:(half + 1) * ES],
                                            dcol[:, 0:1], None, ALU.mult)
                            v.sig(sm["oe"], 2 * k + half + 1)
                        v.wait_ge(sm["ot"], k + 1)
                        v.tensor_copy(oT[:, :].rearrange("p (c u) -> p c u", c=KC),
                                      ps_tp[:, :].rearrange("p (c u) -> p c u", c=KC))
                        v.sig(sm["oc"], k + 1)
                        if k >= 2:
                            v.wait_ge(sm["yd%d" % (tt % 2)], 16 * (k // 2))
                        v.wait_ge(sm["x%d" % (tt % 2)], 16 * (b * 16 + 8 + tt // 2 + 1))
                        for half in range(2):
                            v.wait_ge(sm["pj"], 2 * k + half + 1)
                            v.tensor_tensor(otile[:, (tt % 2) * 1024 + half * ES:(tt % 2) * 1024 + (half + 1) * ES],
                                            ps_p[:, half * ES:(half + 1) * ES],
                                            xtile[:, (tt % 2) * 1024 + half * ES:(tt % 2) * 1024 + (half + 1) * ES],
                                            ALU.add)
                        v.sig(sm["yo"], k + 1)


def _window_indices(seq_len, window=7):
    half = window // 2
    idx = np.zeros((seq_len, window), dtype=np.int32)
    for i in range(seq_len):
        start = max(0, i - half)
        end = min(seq_len, i + half + 1)
        win = np.arange(start, end, dtype=np.int32)
        pad = window - len(win)
        if pad > 0:
            if start == 0:
                win = np.concatenate([win, np.full(pad, seq_len - 1, dtype=np.int32)])
            else:
                win = np.concatenate([np.zeros(pad, dtype=np.int32), win])
        idx[i] = win
    return idx


def emit_l1_attn_a(p, xin, xout, WqkvT, b_qkv, WlatT, b_lat, WprojT, b_proj, tag="la"):
    nc = p.nc
    M_STAB = 8.0
    SCALE = 0.125
    H = 16

    eyeb_in = p.const_input(tag + "_eyeb", np.eye(128, dtype=BFDT))
    onesr_in = p.const_input(tag + "_onesr", np.ones((1, 128), np.float32))
    cpool_ap, cpool_in = p.reg_const(tag, [-M_STAB])

    widx = _window_indices(T)
    mlist = []
    mslot = {}
    for tt in range(NTT):
        for dlt in (-1, 0, 1):
            jt = tt + dlt
            if jt < 0 or jt >= NTT:
                continue
            M = np.zeros((128, 128), np.float32)
            for sl in range(128):
                for j in widx[tt * 128 + sl]:
                    jl = j - jt * 128
                    if 0 <= jl < 128:
                        M[jl, sl] += 1.0
            mslot[(tt, dlt)] = len(mlist)
            mlist.append(M.astype(BFDT))
    NM = len(mlist)
    masks_in = p.const_input(tag + "_masks", np.concatenate(mlist, axis=1))
    ex0 = np.array([[float((widx[sl] == T - 1).sum()) for sl in range(128)]], np.float32)
    ex15 = np.array([[float((widx[15 * 128 + sl] == 0).sum()) for sl in range(128)]], np.float32)
    ex0_in = p.const_input(tag + "_ex0", ex0.astype(BFDT))
    ex15_in = p.const_input(tag + "_ex15", ex15.astype(BFDT))

    def slot_of(t):
        return 3 if t == 0 else (4 if t == 1 else t % 3)

    def qslot_of(t):
        return 2 if t == 0 else t % 2

    attn_order = list(range(1, NTT)) + [0]

    sm = {n: p.sem(tag + n) for n in
          ["w", "x0", "x1", "vb", "sd", "ln", "tp", "xT", "qm", "qe",
           "at", "ae", "am", "dn", "oe", "ot", "oc",
           "pj", "yo", "yd0", "yd1", "rx0", "rx1", "vx0", "vx1", "pro",
           "rc", "rc2", "vch", "sch"]}

    with ExitStack() as _es:
        def A(nm, shape, dt):
            return _es.enter_context(nc.sbuf_tensor(tag + nm, shape, dt))

        xtile = A("xtile", [128, 2 * 1024], F32)
        xres = A("xres", [128, 2 * 1024], F32)
        sjunk = A("sjunk", [128, 1024], F32)
        vjunk = A("vjunk", [128, 1024], F32)
        stat = A("stat", [128, 16], F32)
        xlnb = A("xlnb", [128, 2 * 1024], BF16)
        xTt = A("xTt", [128, 2 * KC * 128], BF16)
        wqsb = A("wqsb", [128, KC * 3072], BF16)
        wlsb = A("wlsb", [128, 64], BF16)
        wpsb = A("wpsb", [128, KC * 1024], BF16)
        bqcol = A("bqcol", [128, 24], F32)
        blcol2 = A("blcol2", [128, 1], F32)
        blrow = A("blrow", [1, 64], F32)
        bprow = A("bprow", [1, 1024], F32)
        blrep = A("blrep", [128, 64], F32)
        bprep = A("bprep", [128, 1024], F32)
        onesr = A("onesr", [1, 128], F32)
        onescol = A("onescol", [128, 1], BF16)
        kTt = A("kTt", [128, KC * 128], BF16)
        vTt = A("vTt", [128, KC * 128], BF16)
        qTs = A("qTs", [128, KC * 3 * 128], BF16)
        klr = A("klr", [128, KC * 5 * 128], BF16)
        vlr = A("vlr", [128, 5 * 1024], BF16)
        vex0 = A("vex0", [1, 1024], BF16)
        vex1 = A("vex1", [1, 1024], BF16)
        masks = A("masks", [128, NM * 128], BF16)
        exm0 = A("exm0", [1, 128], BF16)
        exm15 = A("exm15", [1, 128], BF16)
        pt = A("pt", [128, 2 * 512], BF16)
        ptx = A("ptx", [1, 2 * 128], BF16)
        dcol = A("dcol", [128, 4], F32)
        otile = A("otile", [128, 1024], BF16)
        oT = A("oT", [128, KC * 128], BF16)
        ytile = A("ytile", [128, 2 * 1024], F32)
        eyeb = A("eyeb", [128, 128], BF16)
        ps_q = _es.enter_context(nc.psum_tensor(tag + "ps_q", [128, 1024], F32))
        ps_at = _es.enter_context(nc.psum_tensor(tag + "ps_at", [128, 1024], F32))
        ps_o = _es.enter_context(nc.psum_tensor(tag + "ps_o", [128, 512], F32))
        ps_dn = _es.enter_context(nc.psum_tensor(tag + "ps_dn", [128, 256], F32))
        ps_tp = _es.enter_context(nc.psum_tensor(tag + "ps_tp", [128, 1024], BF16))

        def attn_tiles(ta):
            jts = [jt for jt in (ta - 1, ta, ta + 1) if 0 <= jt < NTT]
            return jts, ta == 0, ta == NTT - 1

        with nc.Block() as blk:
            @blk.sync
            def _(s):
                nrc = [0]

                def cdma(dst, src):
                    s.dma_start(out=dst, in_=src).then_inc(sm["rc"], 16)
                    nrc[0] += 16
                cdma(eyeb[:, :], eyeb_in[:, :])
                cdma(masks[:, :], masks_in[:, :])
                cdma(exm0[0:1, :], ex0_in[:, :])
                cdma(exm15[0:1, :], ex15_in[:, :])
                cdma(bqcol[:, :], b_qkv[:, :])
                cdma(blcol2[0:64, 0], b_lat[:])
                cdma(blcol2[64:128, 0], b_lat[:])
                cdma(blrow[0:1, :], b_lat.rearrange("(a j) -> a j", a=1))
                cdma(bprow[0:1, :], b_proj.rearrange("(a j) -> a j", a=1))
                cdma(onesr[0:1, :], onesr_in[:, :])
                cdma(cpool_ap[:, :], cpool_in[:, :])
                RC_TOTAL = nrc[0]
                for b in range(BLOC):
                    for step in range(NTT + 2):
                        if step < NTT:
                            tt = step
                            k = b * NTT + tt
                            if k >= 2:
                                s.wait_ge(sm["ln"], k - 1)
                            s.dma_start(out=xtile[:, (tt % 2) * 1024:(tt % 2 + 1) * 1024],
                                        in_=xin[k * 128:(k + 1) * 128, :]).then_inc(sm["x%d" % (tt % 2)], 16)
                        if step >= 2:
                            ai = step - 2
                            ta = attn_order[ai]
                            if ai == 13:
                                if b >= 1:
                                    s.wait_ge(sm["dn"], b * NTT * H)
                                s.wait_ge(sm["qe"], (b * NTT + 1) * 56)
                                s.dma_start(out=vex1[0:1, :],
                                            in_=vlr[0:1, 3 * 1024:4 * 1024]).then_inc(sm["vx1"], 16)
                            if ai == 15:
                                s.wait_ge(sm["qe"], (b * NTT + NTT) * 56)
                                s.dma_start(out=vex0[0:1, :],
                                            in_=vlr[127:128, slot_of(NTT - 1) * 1024:(slot_of(NTT - 1) + 1) * 1024]).then_inc(sm["vx0"], 16)
                            kat = b * NTT + ai
                            kr = b * NTT + ta
                            if ai >= 2:
                                s.wait_ge(sm["yo"], kat - 1)
                            s.dma_start(out=xres[:, (ai % 2) * 1024:(ai % 2 + 1) * 1024],
                                        in_=xin[kr * 128:(kr + 1) * 128, :]).then_inc(sm["rx%d" % (ai % 2)], 16)
                            s.wait_ge(sm["yo"], kat + 1)
                            s.dma_start(out=xout[kr * 128:(kr + 1) * 128, :],
                                        in_=ytile[:, (ai % 2) * 1024:(ai % 2 + 1) * 1024]).then_inc(sm["yd%d" % (ai % 2)], 16)
                s.wait_ge(sm["yd0"], 16 * (BLOC * NTT // 2))
                s.wait_ge(sm["yd1"], 16 * (BLOC * NTT // 2))

            @blk.gpsimd
            def _(g):
                g.dma_start(out=wqsb.rearrange("q (kc c) -> q kc c", kc=KC),
                            in_=WqkvT.rearrange("(kc q) c -> q kc c", q=128)).then_inc(sm["w"], 16)
                g.dma_start(out=wlsb[0:64, :], in_=WlatT[:, :]).then_inc(sm["w"], 16)
                g.dma_start(out=wlsb[64:128, :], in_=WlatT[:, :]).then_inc(sm["w"], 16)
                g.dma_start(out=wpsb.rearrange("q (kc c) -> q kc c", kc=KC),
                            in_=WprojT.rearrange("(kc q) c -> q kc c", q=128)).then_inc(sm["w"], 16)
                g.wait_ge(sm["w"], 64)

            @blk.tensor
            def _(t):
                t.wait_ge(sm["rc"], 176)
                t.wait_ge(sm["rc2"], 1)
                t.wait_ge(sm["w"], 64)
                t.matmul(ps_q[:, 0:64], onesr[0:1, :], blrow[0:1, :], start=True, stop=True)
                t.matmul(ps_q[:, 512:1024], onesr[0:1, :], bprow[0:1, 0:512], start=True, stop=True)
                t.matmul(ps_at[:, 0:512], onesr[0:1, :], bprow[0:1, 512:1024], start=True, stop=True).then_inc(sm["pro"], 1)
                t.wait_ge(sm["pro"], 2)
                for b in range(BLOC):
                    for step in range(NTT + 2):
                        if step < NTT:
                            tt = step
                            k = b * NTT + tt
                            t.wait_ge(sm["ln"], k + 1)
                            if k >= 2:
                                t.wait_ge(sm["xT"], k - 1)
                            for kc in range(KC):
                                ins = t.transpose(ps_tp[:, kc * 128:(kc + 1) * 128],
                                                  xlnb[:, (tt % 2) * 1024 + kc * 128:(tt % 2) * 1024 + (kc + 1) * 128],
                                                  eyeb[:, :])
                            ins.then_inc(sm["tp"], 1)
                            t.wait_ge(sm["xT"], k + 1)
                            for cch in range(24):
                                it56 = k * 56 + cch
                                if it56 >= 2:
                                    t.wait_ge(sm["qe"], it56 - 1)
                                for kc in range(KC):
                                    ins = t.matmul(ps_q[:, (it56 % 2) * 512:(it56 % 2) * 512 + 128],
                                                   wqsb[:, kc * 3072 + cch * 128:kc * 3072 + (cch + 1) * 128],
                                                   xTt[:, (tt % 2) * KC * 128 + kc * 128:(tt % 2) * KC * 128 + (kc + 1) * 128],
                                                   start=(kc == 0), stop=(kc == KC - 1))
                                ins.then_inc(sm["qm"], 1)
                            t.wait_ge(sm["qe"], k * 56 + 16)
                            for h in range(H):
                                it56 = k * 56 + 24 + h
                                t.wait_ge(sm["qe"], it56 - 1)
                                pb = (h % 2) * 64
                                t.matmul(ps_q[0:64, (it56 % 2) * 512:(it56 % 2) * 512 + 128],
                                         wlsb[pb:pb + 64, :],
                                         kTt[pb:pb + 64, (h // 2) * 128:(h // 2 + 1) * 128],
                                         start=True, stop=True).then_inc(sm["qm"], 1)
                            t.wait_ge(sm["qe"], k * 56 + 24)
                            for h in range(H):
                                it56 = k * 56 + 40 + h
                                t.wait_ge(sm["qe"], it56 - 1)
                                pb = (h % 2) * 64
                                t.matmul(ps_q[:, (it56 % 2) * 512:(it56 % 2) * 512 + 64],
                                         vTt[pb:pb + 64, (h // 2) * 128:(h // 2 + 1) * 128],
                                         wlsb[pb:pb + 64, :],
                                         start=True, stop=True).then_inc(sm["qm"], 1)
                        if step >= 2:
                            ai = step - 2
                            ta = attn_order[ai]
                            kat = b * NTT + ai
                            jts, hx0, hx15 = attn_tiles(ta)
                            qsl = qslot_of(ta)
                            need = NTT if ta == 0 else max(jts) + 1
                            t.wait_ge(sm["qe"], (b * NTT + need) * 56)
                            if hx0:
                                t.wait_ge(sm["vx0"], 16 * (b + 1))
                            if hx15:
                                t.wait_ge(sm["vx1"], 16 * (b + 1))
                            if kat >= 1:
                                t.wait_ge(sm["yo"], kat)
                            for h in range(H):
                                hg = kat * H + h
                                pb = (h % 2) * 64
                                if hg >= 2:
                                    t.wait_ge(sm["ae"], hg - 1)
                                for ji, jt in enumerate(jts):
                                    sl = slot_of(jt)
                                    ins = t.matmul(ps_at[:, (hg % 2) * 512 + ji * 128:(hg % 2) * 512 + (ji + 1) * 128],
                                                   klr[pb:pb + 64, (h // 2) * 640 + sl * 128:(h // 2) * 640 + (sl + 1) * 128],
                                                   qTs[pb:pb + 64, (h // 2) * 384 + qsl * 128:(h // 2) * 384 + (qsl + 1) * 128],
                                                   start=True, stop=True)
                                    ins.then_inc(sm["at"], 1)
                                if hx0 or hx15:
                                    exsl = slot_of(NTT - 1) if hx0 else slot_of(0)
                                    excol = 127 if hx0 else 0
                                    c0 = (h // 2) * 640 + exsl * 128 + excol
                                    t.matmul(ps_at[0:1, (hg % 2) * 512 + 384:(hg % 2) * 512 + 512],
                                             klr[pb:pb + 64, c0:c0 + 1],
                                             qTs[pb:pb + 64, (h // 2) * 384 + qsl * 128:(h // 2) * 384 + (qsl + 1) * 128],
                                             start=True, stop=True).then_inc(sm["at"], 1)
                                t.wait_ge(sm["am"], hg + 1)
                                if hg >= 2:
                                    t.wait_ge(sm["oe"], hg - 1)
                                nj = len(jts)
                                hasex = hx0 or hx15
                                for ji, jt in enumerate(jts):
                                    sl = slot_of(jt)
                                    last = (ji == nj - 1) and not hasex
                                    t.matmul(ps_o[:, (hg % 2) * 256:(hg % 2) * 256 + 64],
                                             pt[:, (hg % 2) * 512 + ji * 128:(hg % 2) * 512 + (ji + 1) * 128],
                                             vlr[:, sl * 1024 + h * 64:sl * 1024 + (h + 1) * 64],
                                             start=(ji == 0), stop=last)
                                    ins = t.matmul(ps_dn[:, (hg % 2) * 128:(hg % 2) * 128 + 1],
                                                   pt[:, (hg % 2) * 512 + ji * 128:(hg % 2) * 512 + (ji + 1) * 128],
                                                   onescol[:, :],
                                                   start=(ji == 0), stop=last)
                                if hasex:
                                    vexb = vex0 if hx0 else vex1
                                    t.matmul(ps_o[:, (hg % 2) * 256:(hg % 2) * 256 + 64],
                                             ptx[0:1, (hg % 2) * 128:(hg % 2 + 1) * 128],
                                             vexb[0:1, h * 64:(h + 1) * 64],
                                             start=False, stop=True)
                                    ins = t.matmul(ps_dn[:, (hg % 2) * 128:(hg % 2) * 128 + 1],
                                                   ptx[0:1, (hg % 2) * 128:(hg % 2 + 1) * 128],
                                                   onescol[0:1, :],
                                                   start=False, stop=True)
                                ins.then_inc(sm["dn"], 1)
                            t.wait_ge(sm["oe"], kat * H + H)
                            for kc in range(KC):
                                ins = t.transpose(ps_tp[:, kc * 128:(kc + 1) * 128],
                                                  otile[:, kc * 128:(kc + 1) * 128],
                                                  eyeb[:, :])
                            ins.then_inc(sm["ot"], 1)
                            t.wait_ge(sm["oc"], kat + 1)
                            for half in range(2):
                                for kc in range(KC):
                                    ins = t.matmul(ps_at[:, half * 512:(half + 1) * 512],
                                                   oT[:, kc * 128:(kc + 1) * 128],
                                                   wpsb[:, kc * 1024 + half * 512:kc * 1024 + (half + 1) * 512],
                                                   start=(kc == 0), stop=(kc == KC - 1))
                                ins.then_inc(sm["pj"], 1)

            @blk.scalar
            def _(sc_raw):
                sc = SelfSync(sc_raw, sm["sch"])
                atcum = [0]
                for b in range(BLOC):
                    for step in range(NTT + 2):
                        if step < NTT:
                            tt = step
                            k = b * NTT + tt
                            sc.wait_ge(sm["x%d" % (tt % 2)], 16 * (b * 8 + tt // 2 + 1))
                            xa = xtile[:, (tt % 2) * 1024:(tt % 2 + 1) * 1024]
                            sc.activation(sjunk[:, :], xa, AF.Square,
                                          accum_out=stat[:, 1:2])
                            sc.wait_ge(sm["vb"], k + 1)
                            sc.activation(stat[:, 3:4], stat[:, 1:2], AF.Sqrt,
                                          bias=stat[:, 2:3], scale=1.0 / 1024.0)
                            sc.sig(sm["sd"], k + 1)
                            qsl = qslot_of(tt)
                            sc.wait_ge(sm["qe"], k * 56)
                            for cch in range(24):
                                it56 = k * 56 + cch
                                sc.wait_ge(sm["qm"], it56 + 1)
                                src56 = ps_q[:, (it56 % 2) * 512:(it56 % 2) * 512 + 128]
                                if cch < KC:
                                    dst = qTs[:, cch * 384 + qsl * 128:cch * 384 + (qsl + 1) * 128]
                                elif cch < 16:
                                    dst = kTt[:, (cch - 8) * 128:(cch - 7) * 128]
                                else:
                                    dst = vTt[:, (cch - 16) * 128:(cch - 15) * 128]
                                sc.activation(dst, src56, AF.Identity, bias=bqcol[:, cch:cch + 1])
                                sc.sig(sm["qe"], it56 + 1)
                            sl = slot_of(tt)
                            for h in range(H):
                                it56 = k * 56 + 24 + h
                                sc.wait_ge(sm["qm"], it56 + 1)
                                pb = (h % 2) * 64
                                sc.activation(klr[pb:pb + 64, (h // 2) * 640 + sl * 128:(h // 2) * 640 + (sl + 1) * 128],
                                              ps_q[0:64, (it56 % 2) * 512:(it56 % 2) * 512 + 128],
                                              AF.Identity, bias=blcol2[pb:pb + 64, 0:1])
                                sc.sig(sm["qe"], it56 + 1)
                        if step >= 2:
                            ai = step - 2
                            ta = attn_order[ai]
                            kat = b * NTT + ai
                            jts, hx0, hx15 = attn_tiles(ta)
                            cnt_h = len(jts) + (1 if (hx0 or hx15) else 0)
                            for h in range(H):
                                hg = kat * H + h
                                atcum[0] += cnt_h
                                sc.wait_ge(sm["at"], atcum[0])
                                nev = len(jts) + (1 if (hx0 or hx15) else 0)
                                for ji in range(len(jts)):
                                    sc.activation(pt[:, (hg % 2) * 512 + ji * 128:(hg % 2) * 512 + (ji + 1) * 128],
                                                  ps_at[:, (hg % 2) * 512 + ji * 128:(hg % 2) * 512 + (ji + 1) * 128],
                                                  AF.Exp, bias=-M_STAB, scale=SCALE)
                                    if ji == len(jts) - 1 and not (hx0 or hx15):
                                        sc.sig(sm["ae"], hg + 1)
                                if hx0 or hx15:
                                    sc.activation(ptx[0:1, (hg % 2) * 128:(hg % 2 + 1) * 128],
                                                  ps_at[0:1, (hg % 2) * 512 + 384:(hg % 2) * 512 + 512],
                                                  AF.Exp, bias=-M_STAB, scale=SCALE)
                                    sc.sig(sm["ae"], hg + 1)

            @blk.vector
            def _(v_raw):
                v = SelfSync(v_raw, sm["vch"])
                v.memset(onescol[:, :], 1.0)
                v.sig(sm["rc2"], 1)
                v.wait_ge(sm["pro"], 1)
                v.tensor_copy(blrep[:, :], ps_q[:, 0:64])
                v.tensor_copy(bprep[:, 0:512], ps_q[:, 512:1024])
                v.tensor_copy(bprep[:, 512:1024], ps_at[:, 0:512])
                v.sig(sm["pro"], 2)
                for b in range(BLOC):
                    for step in range(NTT + 2):
                        if step < NTT:
                            tt = step
                            k = b * NTT + tt
                            xa = xtile[:, (tt % 2) * 1024:(tt % 2 + 1) * 1024]
                            v.wait_ge(sm["x%d" % (tt % 2)], 16 * (b * 8 + tt // 2 + 1))
                            v.tensor_scalar(vjunk[:, :], xa, 1.0, None, ALU.mult, ALU.add,
                                            accum_out=stat[:, 0:1])
                            v.tensor_scalar(stat[:, 4:5], stat[:, 0:1], 1.0 / 1024.0, None, ALU.mult)
                            v.tensor_tensor(stat[:, 5:6], stat[:, 4:5], stat[:, 4:5], ALU.mult)
                            v.tensor_scalar(stat[:, 2:3], stat[:, 5:6], -1.0, 1e-5, ALU.mult, ALU.add)
                            v.sig(sm["vb"], k + 1)
                            v.wait_ge(sm["sd"], k + 1)
                            v.reciprocal(stat[:, 6:7], stat[:, 3:4])
                            v.tensor_scalar(xlnb[:, (tt % 2) * 1024:(tt % 2 + 1) * 1024], xa,
                                            stat[:, 4:5], stat[:, 6:7], ALU.subtract, ALU.mult)
                            v.sig(sm["ln"], k + 1)
                            v.wait_ge(sm["tp"], k + 1)
                            v.tensor_copy(
                                xTt[:, (tt % 2) * KC * 128:(tt % 2 + 1) * KC * 128].rearrange("p (kc u) -> p kc u", kc=KC),
                                ps_tp[:, :].rearrange("p (c u) -> p c u", c=KC))
                            v.sig(sm["xT"], k + 1)
                            sl = slot_of(tt)
                            v.wait_ge(sm["qe"], k * 56 + 40)
                            for h in range(H):
                                it56 = k * 56 + 40 + h
                                v.wait_ge(sm["qm"], it56 + 1)
                                v.tensor_tensor(vlr[:, sl * 1024 + h * 64:sl * 1024 + (h + 1) * 64],
                                                ps_q[:, (it56 % 2) * 512:(it56 % 2) * 512 + 64],
                                                blrep[:, :], ALU.add)
                                v.sig(sm["qe"], it56 + 1)
                        if step >= 2:
                            ai = step - 2
                            ta = attn_order[ai]
                            kat = b * NTT + ai
                            jts, hx0, hx15 = attn_tiles(ta)
                            for h in range(H):
                                hg = kat * H + h
                                v.wait_ge(sm["ae"], hg + 1)
                                for ji, jt in enumerate(jts):
                                    dlt = jt - ta
                                    msl = mslot[(ta, dlt)]
                                    v.tensor_tensor(pt[:, (hg % 2) * 512 + ji * 128:(hg % 2) * 512 + (ji + 1) * 128],
                                                    pt[:, (hg % 2) * 512 + ji * 128:(hg % 2) * 512 + (ji + 1) * 128],
                                                    masks[:, msl * 128:(msl + 1) * 128], ALU.mult)
                                if hx0 or hx15:
                                    exb = exm0 if hx0 else exm15
                                    v.tensor_tensor(ptx[0:1, (hg % 2) * 128:(hg % 2 + 1) * 128],
                                                    ptx[0:1, (hg % 2) * 128:(hg % 2 + 1) * 128],
                                                    exb[0:1, :], ALU.mult)
                                v.sig(sm["am"], hg + 1)
                                v.wait_ge(sm["dn"], hg + 1)
                                v.reciprocal(dcol[:, (hg % 2):(hg % 2) + 1],
                                             ps_dn[:, (hg % 2) * 128:(hg % 2) * 128 + 1])
                                v.tensor_scalar(otile[:, h * 64:(h + 1) * 64],
                                                ps_o[:, (hg % 2) * 256:(hg % 2) * 256 + 64],
                                                dcol[:, (hg % 2):(hg % 2) + 1], None, ALU.mult)
                                v.sig(sm["oe"], hg + 1)
                            v.wait_ge(sm["ot"], kat + 1)
                            v.tensor_copy(oT[:, :].rearrange("p (c u) -> p c u", c=KC),
                                          ps_tp[:, :].rearrange("p (c u) -> p c u", c=KC))
                            v.sig(sm["oc"], kat + 1)
                            v.wait_ge(sm["rx%d" % (ai % 2)], 16 * (b * 8 + ai // 2 + 1))
                            if kat >= 2:
                                v.wait_ge(sm["yd%d" % (ai % 2)], 16 * (b * 8 + ai // 2))
                            for half in range(2):
                                v.wait_ge(sm["pj"], 2 * kat + half + 1)
                                v.tensor_tensor(vjunk[:, half * 512:(half + 1) * 512],
                                                ps_at[:, half * 512:(half + 1) * 512],
                                                bprep[:, half * 512:(half + 1) * 512], ALU.add)
                                v.tensor_tensor(ytile[:, (ai % 2) * 1024 + half * 512:(ai % 2) * 1024 + (half + 1) * 512],
                                                vjunk[:, half * 512:(half + 1) * 512],
                                                xres[:, (ai % 2) * 1024 + half * 512:(ai % 2) * 1024 + (half + 1) * 512],
                                                ALU.add)
                            v.sig(sm["yo"], kat + 1)


def emit_l1_attn_b(p, xin, xout, WqkvT, WgT, bg, WbT, bb, WoutT, tag="db"):
    nc = p.nc
    H = 16
    BIG = 30000.0

    eyeb_in = p.const_input(tag + "_eyeb", np.eye(128, dtype=BFDT))
    eyef_in = p.const_input(tag + "_eyef", np.eye(128, dtype=np.float32))
    sl_ = np.arange(128)[:, None]
    tl_ = np.arange(128)[None, :]
    triuI_in = p.const_input(tag + "_triuI", (sl_ <= tl_).astype(np.float32))
    triuS_in = p.const_input(tag + "_triuS", (sl_ < tl_).astype(np.float32))
    mbI_in = p.const_input(tag + "_mbI", np.where(sl_ <= tl_, 0.0, -BIG).astype(np.float32))
    mbS_in = p.const_input(tag + "_mbS", np.where(sl_ < tl_, 0.0, -BIG).astype(np.float32))
    o1x128_in = p.const_input(tag + "_o1x128", np.ones((1, 128), np.float32))
    ocol_in = p.const_input(tag + "_ocol", np.ones((128, 1), np.float32))
    o1x64_in = p.const_input(tag + "_o1x64", np.ones((1, 64), np.float32))

    sm = {n: p.sem(tag + n) for n in
          ["w", "x0", "x1", "hm", "he", "yo", "yd0", "yd1", "rc", "vch", "sch"]}

    with ExitStack() as _es:
        def A(nm, shape, dt):
            return _es.enter_context(nc.sbuf_tensor(tag + nm, shape, dt))

        xtile = A("xtile", [128, 2 * 1024], F32)
        sjunk = A("sjunk", [128, 1024], F32)
        vjunk = A("vjunk", [128, 1024], F32)
        stat = A("stat", [128, 48], F32)
        xlnb = A("xlnb", [128, 1024], BF16)
        xlob = A("xlob", [128, 1024], BF16)
        xThi = A("xThi", [128, KC * 128], BF16)
        xTlo = A("xTlo", [128, KC * 128], BF16)
        wqsb = A("wqsb", [128, KC * 3072], BF16)
        wosb = A("wosb", [128, KC * 1024], BF16)
        wab = A("wab", [128, KC * 64], BF16)
        wabf = A("wabf", [128, KC * 32], F32)
        bgrep = A("bgrep", [128, 32], F32)
        bgrow = A("bgrow", [1, 32], F32)
        qkvf = A("qkvf", [128, 3072], F32)
        qf = A("qf", [128, 1024], F32)
        kf = A("kf", [128, 1024], F32)
        qb = A("qb", [128, 1024], BF16)
        kb = A("kb", [128, 1024], BF16)
        ab_all = A("ab_all", [128, 32], F32)
        la_all = A("la_all", [128, 16], F32)
        st_buf = A("st_buf", [64, 2 * 1024], F32)
        ktb = A("ktb", [64, 128], BF16)
        qtb = A("qtb", [64, 128], BF16)
        qtf = A("qtf", [64, 128], F32)
        ktf = A("ktf", [64, 128], F32)
        bkt = A("bkt", [64, 128], F32)
        btar = A("btar", [1, 128], F32)
        lamc = A("lamc", [128, 1], F32)
        lamr = A("lamr", [1, 128], F32)
        gamc1 = A("gamc1", [1, 1], F32)
        gcc = A("gcc", [64, 1], F32)
        gamcol = A("gamcol", [128, 1], F32)
        ggc = A("ggc", [128, 1], F32)
        dmt = A("dmt", [128, 128], F32)
        git = A("git", [128, 128], F32)
        gst = A("gst", [128, 128], F32)
        nwt = A("nwt", [128, 128], F32)
        nut = A("nut", [128, 128], F32)
        qkmt = A("qkmt", [128, 128], F32)
        qk3 = A("qk3", [128, 128], F32)
        brep = A("brep", [128, 128], F32)
        brepn = A("brepn", [128, 128], F32)
        tmpE = A("tmpE", [128, 256], F32)
        ptw = A("ptw", [128, 2 * 128], F32)
        pnat = A("pnat", [128, 128], F32)
        ybuf = A("ybuf", [128, 2 * 64], F32)
        ubuf = A("ubuf", [128, 2 * 64], F32)
        bv = A("bv", [128, 64], F32)
        utb = A("utb", [128, 64], F32)
        ygb = A("ygb", [128, 64], F32)
        zbuf = A("zbuf", [128, 64], F32)
        ynb = A("ynb", [128, 64], F32)
        t3b = A("t3b", [128, 64], F32)
        otile = A("otile", [128, 1024], BF16)
        oT = A("oT", [128, KC * 128], BF16)
        ytile = A("ytile", [128, 2 * 1024], F32)
        eyeb = A("eyeb", [128, 128], BF16)
        eyef = A("eyef", [128, 128], F32)
        triuI = A("triuI", [128, 128], F32)
        triuS = A("triuS", [128, 128], F32)
        mbI = A("mbI", [128, 128], F32)
        mbS = A("mbS", [128, 128], F32)
        o1x128 = A("o1x128", [1, 128], F32)
        ocol = A("ocol", [128, 1], F32)
        o1x64 = A("o1x64", [1, 64], F32)

        ps_tpb = _es.enter_context(nc.psum_tensor(tag + "ps_tpb", [128, 1024], BF16))
        ps_tpf = _es.enter_context(nc.psum_tensor(tag + "ps_tpf", [128, 512], F32))
        ps_r = _es.enter_context(nc.psum_tensor(tag + "ps_r", [128, 512], F32))
        ps_k = _es.enter_context(nc.psum_tensor(tag + "ps_k", [128, 512], F32))
        ps_a = _es.enter_context(nc.psum_tensor(tag + "ps_a", [128, 512], F32))
        ps_q = _es.enter_context(nc.psum_tensor(tag + "ps_q", [128, 1024], F32))

        STEPS = []

        def T(fn):
            STEPS.append(("T", fn))

        def V(fn):
            STEPS.append(("V", fn))

        def S(fn):
            STEPS.append(("S", fn))

        def g1(t):
            return t.matmul(ps_q[:, 0:32], o1x128[0:1, :], bgrow[0:1, :],
                            start=True, stop=True)
        T(g1)

        def g2(v):
            v.tensor_copy(bgrep[:, :], ps_q[:, 0:32])
            for kc in range(KC):
                fs = wabf[:, kc * 32:(kc + 1) * 32]
                v.tensor_copy(wab[:, kc * 64:kc * 64 + 32], fs)
                v.tensor_copy(sjunk[:, 0:32], wab[:, kc * 64:kc * 64 + 32])
                v.tensor_tensor(sjunk[:, 32:64], fs, sjunk[:, 0:32], ALU.subtract)
                v.tensor_copy(wab[:, kc * 64 + 32:kc * 64 + 64], sjunk[:, 32:64])
            return v.tensor_copy(bgrep[:, :], ps_q[:, 0:32])
        V(g2)

        def chunk_steps(b, c):
            k = b * NTT + c
            rd = (c % 2) * 1024
            wr = ((c + 1) % 2) * 1024

            def lnV1(v):
                if c == 0:
                    v.memset(st_buf[:, rd:rd + 1024], 0.0)
                v.wait_ge(sm["x%d" % (c % 2)], 16 * (k // 2 + 1))
                xa = xtile[:, (c % 2) * 1024:(c % 2 + 1) * 1024]
                v.tensor_scalar(vjunk[:, :], xa, 1.0, None, ALU.mult, ALU.add,
                                accum_out=stat[:, 0:1])
                v.tensor_scalar(stat[:, 4:5], stat[:, 0:1], 1.0 / 1024.0, None, ALU.mult)
                v.tensor_tensor(stat[:, 5:6], stat[:, 4:5], stat[:, 4:5], ALU.mult)
                return v.tensor_scalar(stat[:, 2:3], stat[:, 5:6], -1.0, 1e-5,
                                       ALU.mult, ALU.add)
            V(lnV1)

            def lnS1(s):
                s.wait_ge(sm["x%d" % (c % 2)], 16 * (k // 2 + 1))
                xa = xtile[:, (c % 2) * 1024:(c % 2 + 1) * 1024]
                s.activation(sjunk[:, :], xa, AF.Square, accum_out=stat[:, 1:2])
                return s.activation(stat[:, 3:4], stat[:, 1:2], AF.Sqrt,
                                    bias=stat[:, 2:3], scale=1.0 / 1024.0)
            S(lnS1)

            def lnV2(v):
                xa = xtile[:, (c % 2) * 1024:(c % 2 + 1) * 1024]
                v.reciprocal(stat[:, 6:7], stat[:, 3:4])
                v.tensor_scalar(vjunk[:, :], xa, stat[:, 4:5], stat[:, 6:7],
                                ALU.subtract, ALU.mult)
                v.tensor_copy(xlnb[:, :], vjunk[:, :])
                return v.tensor_tensor(xlob[:, :], vjunk[:, :], xlnb[:, :], ALU.subtract)
            V(lnV2)

            def tp1(t):
                for kc in range(KC):
                    ins = t.transpose(ps_tpb[:, kc * 128:(kc + 1) * 128],
                                      xlnb[:, kc * 128:(kc + 1) * 128], eyeb[:, :])
                return ins
            T(tp1)

            def tp1c(v):
                return v.tensor_copy(xThi[:, :].rearrange("p (kc u) -> p kc u", kc=KC),
                                     ps_tpb[:, :].rearrange("p (c2 u) -> p c2 u", c2=KC))
            V(tp1c)

            def tp2(t):
                for kc in range(KC):
                    ins = t.transpose(ps_tpb[:, kc * 128:(kc + 1) * 128],
                                      xlob[:, kc * 128:(kc + 1) * 128], eyeb[:, :])
                return ins
            T(tp2)

            def tp2c(v):
                return v.tensor_copy(xTlo[:, :].rearrange("p (kc u) -> p kc u", kc=KC),
                                     ps_tpb[:, :].rearrange("p (c2 u) -> p c2 u", c2=KC))
            V(tp2c)

            for gq in range(6):
                def qg(t, gq=gq):
                    for kc in range(KC):
                        ins = t.matmul(ps_q[:, (gq % 2) * 512:(gq % 2) * 512 + 512],
                                       xThi[:, kc * 128:(kc + 1) * 128],
                                       wqsb[:, kc * 3072 + gq * 512:kc * 3072 + (gq + 1) * 512],
                                       start=(kc == 0), stop=(kc == KC - 1))
                    return ins
                T(qg)

                def qe(s, gq=gq):
                    return s.activation(qkvf[:, gq * 512:(gq + 1) * 512],
                                        ps_q[:, (gq % 2) * 512:(gq % 2) * 512 + 512],
                                        AF.Identity, bias=0.0, scale=1.0)
                S(qe)

            def abg(t):
                for kc in range(KC):
                    t.matmul(ps_q[:, 0:32], xThi[:, kc * 128:(kc + 1) * 128],
                             wab[:, kc * 64:kc * 64 + 32],
                             start=(kc == 0), stop=False)
                    ins = t.matmul(ps_q[:, 0:32], xTlo[:, kc * 128:(kc + 1) * 128],
                                   wab[:, kc * 64:kc * 64 + 32],
                                   start=False, stop=(kc == KC - 1))
                return ins
            T(abg)

            def abe1(v):
                return v.tensor_tensor(sjunk[:, 0:32], ps_q[:, 0:32], bgrep[:, :], ALU.add)
            V(abe1)

            def abe2(s):
                s.activation(ab_all[:, :], sjunk[:, 0:32], AF.Sigmoid, bias=0.0, scale=1.0)
                return s.activation(la_all[:, :], ab_all[:, 0:16], AF.Ln, bias=0.0, scale=1.0)
            S(abe2)

            for h in range(H):
                def nrmS(s, h=h):
                    s.activation(sjunk[:, 0:64], qkvf[:, h * 64:(h + 1) * 64], AF.Square,
                                 accum_out=stat[:, 8:9])
                    s.activation(stat[:, 9:10], stat[:, 8:9], AF.Sqrt, bias=0.0, scale=1.0)
                    s.activation(sjunk[:, 0:64], qkvf[:, 1024 + h * 64:1024 + (h + 1) * 64],
                                 AF.Square, accum_out=stat[:, 10:11])
                    return s.activation(stat[:, 11:12], stat[:, 10:11], AF.Sqrt,
                                        bias=0.0, scale=1.0)
                S(nrmS)

                def nrmV(v, h=h):
                    v.tensor_scalar(stat[:, 12:13], stat[:, 9:10], 1e-12, None, ALU.max)
                    v.reciprocal(stat[:, 13:14], stat[:, 12:13])
                    v.tensor_scalar(qf[:, h * 64:(h + 1) * 64], qkvf[:, h * 64:(h + 1) * 64],
                                    stat[:, 13:14], None, ALU.mult)
                    v.tensor_copy(qb[:, h * 64:(h + 1) * 64], qf[:, h * 64:(h + 1) * 64])
                    v.tensor_scalar(stat[:, 14:15], stat[:, 11:12], 1e-12, None, ALU.max)
                    v.reciprocal(stat[:, 15:16], stat[:, 14:15])
                    v.tensor_scalar(kf[:, h * 64:(h + 1) * 64],
                                    qkvf[:, 1024 + h * 64:1024 + (h + 1) * 64],
                                    stat[:, 15:16], None, ALU.mult)
                    return v.tensor_copy(kb[:, h * 64:(h + 1) * 64], kf[:, h * 64:(h + 1) * 64])
                V(nrmV)

            for h in range(H):
                qsec = h * 64
                vsec = 2048 + h * 64

                def s1(t, h=h, qsec=qsec):
                    t.transpose(ps_tpb[0:64, 0:128], kb[:, qsec:qsec + 64], eyeb[:, :])
                    t.transpose(ps_tpb[0:64, 128:256], qb[:, qsec:qsec + 64], eyeb[:, :])
                    t.transpose(ps_tpf[0:64, 0:128], qf[:, qsec:qsec + 64], eyef[:, :])
                    t.transpose(ps_tpf[0:64, 128:256], kf[:, qsec:qsec + 64], eyef[:, :])
                    return t.transpose(ps_tpf[0:1, 256:384], ab_all[:, 16 + h:17 + h], eyef[:, :])
                T(s1)

                def s2(v):
                    v.tensor_copy(ktb[:, :], ps_tpb[0:64, 0:128])
                    v.tensor_copy(qtb[:, :], ps_tpb[0:64, 128:256])
                    v.tensor_copy(qtf[:, :], ps_tpf[0:64, 0:128])
                    v.tensor_copy(ktf[:, :], ps_tpf[0:64, 128:256])
                    return v.tensor_copy(btar[0:1, :], ps_tpf[0:1, 256:384])
                V(s2)

                def s3(t, h=h):
                    t.matmul(ps_r[:, 0:1], triuI[:, :], la_all[:, h:h + 1],
                             start=True, stop=True)
                    return t.matmul(ps_r[0:1, 2:3], ocol[:, :], la_all[:, h:h + 1],
                                    start=True, stop=True)
                T(s3)

                def s4(v):
                    return v.tensor_copy(lamc[:, :], ps_r[:, 0:1])
                V(s4)

                def s4b(s):
                    s.activation(gamc1[0:1, :], ps_r[0:1, 2:3], AF.Exp, bias=0.0, scale=1.0)
                    return s.activation(gamcol[:, :], ps_r[:, 0:1], AF.Exp, bias=0.0, scale=1.0)
                S(s4b)

                def s5(t):
                    t.transpose(ps_tpf[0:1, 256:384], lamc[:, :], eyef[:, :])
                    return t.matmul(ps_r[0:64, 4:5], o1x64[0:1, :], gamc1[0:1, :],
                                    start=True, stop=True)
                T(s5)

                def s6(v):
                    v.tensor_copy(lamr[0:1, :], ps_tpf[0:1, 256:384])
                    return v.tensor_copy(gcc[:, :], ps_r[0:64, 4:5])
                V(s6)

                def s7(t):
                    t.matmul(ps_r[:, 128:256], o1x128[0:1, :], lamr[0:1, :],
                             start=True, stop=True)
                    return t.matmul(ps_k[:, 256:384], o1x128[0:1, :], btar[0:1, :],
                                    start=True, stop=True)
                T(s7)

                def s8(v):
                    v.tensor_scalar(dmt[:, :], ps_r[:, 128:256], lamc[:, 0:1], None,
                                    ALU.subtract)
                    v.tensor_copy(brep[:, :], ps_k[:, 256:384])
                    v.tensor_scalar(brepn[:, :], ps_k[:, 256:384], -1.0, None, ALU.mult)
                    v.tensor_tensor(tmpE[:, 0:128], dmt[:, :], mbI[:, :], ALU.add)
                    return v.tensor_tensor(tmpE[:, 128:256], dmt[:, :], mbS[:, :], ALU.add)
                V(s8)

                def s9(s):
                    s.activation(git[:, :], tmpE[:, 0:128], AF.Exp, bias=0.0, scale=1.0)
                    return s.activation(gst[:, :], tmpE[:, 128:256], AF.Exp,
                                        bias=0.0, scale=1.0)
                S(s9)

                def s11(t):
                    t.matmul(ps_k[:, 0:128], ktb[:, :], ktb[:, :], start=True, stop=True)
                    return t.matmul(ps_k[:, 128:256], ktb[:, :], qtb[:, :],
                                    start=True, stop=True)
                T(s11)

                def s12(v, h=h, vsec=vsec):
                    v.tensor_tensor(nwt[:, :], ps_k[:, 0:128], brepn[:, :], ALU.mult)
                    v.tensor_tensor(nwt[:, :], nwt[:, :], triuS[:, :], ALU.mult)
                    v.tensor_tensor(nut[:, :], nwt[:, :], gst[:, :], ALU.mult)
                    v.tensor_tensor(qkmt[:, :], ps_k[:, 128:256], triuI[:, :], ALU.mult)
                    v.tensor_tensor(qk3[:, :], qkmt[:, :], git[:, :], ALU.mult)
                    v.tensor_tensor(bkt[:, :], ktf[:, :], brep[0:64, :], ALU.mult)
                    v.tensor_scalar(bv[:, :], qkvf[:, vsec:vsec + 64],
                                    ab_all[:, 16 + h:17 + h], None, ALU.mult)
                    v.tensor_copy(ptw[:, 0:128], nwt[:, :])
                    return v.tensor_copy(ubuf[:, 0:64], bv[:, :])
                V(s12)

                def s13(t, h=h):
                    return t.matmul(ps_a[:, 128:192], bkt[:, :],
                                    st_buf[:, rd + h * 64:rd + (h + 1) * 64],
                                    start=True, stop=True)
                T(s13)

                def s14(v):
                    return v.tensor_copy(ybuf[:, 0:64], ps_a[:, 128:192])
                V(s14)

                def series(vecbuf):
                    for kk in range(7):
                        cs = kk % 2
                        ns = 1 - cs

                        def sA(t, cs=cs, kk=kk, vecbuf=vecbuf):
                            t.matmul(ps_a[:, 0:64], eyef[:, :],
                                     vecbuf[:, cs * 64:cs * 64 + 64], start=True, stop=False)
                            return t.matmul(ps_a[:, 0:64],
                                            ptw[:, (kk % 2) * 128:(kk % 2) * 128 + 128],
                                            vecbuf[:, cs * 64:cs * 64 + 64],
                                            start=False, stop=True)
                        T(sA)

                        def sB(v, ns=ns, vecbuf=vecbuf):
                            return v.tensor_copy(vecbuf[:, ns * 64:ns * 64 + 64],
                                                 ps_a[:, 0:64])
                        V(sB)

                        if kk < 6:
                            def sC(t, kk=kk):
                                return t.transpose(ps_tpf[:, 0:128],
                                                   ptw[:, (kk % 2) * 128:(kk % 2) * 128 + 128],
                                                   eyef[:, :])
                            T(sC)

                            def sD(v):
                                return v.tensor_copy(pnat[:, :], ps_tpf[:, 0:128])
                            V(sD)

                            def sE(t, kk=kk):
                                return t.matmul(ps_a[:, 256:384], pnat[:, :],
                                                ptw[:, (kk % 2) * 128:(kk % 2) * 128 + 128],
                                                start=True, stop=True)
                            T(sE)

                            def sF(v, kk=kk):
                                return v.tensor_copy(
                                    ptw[:, ((kk + 1) % 2) * 128:((kk + 1) % 2) * 128 + 128],
                                    ps_a[:, 256:384])
                            V(sF)

                series(ybuf)

                def sw(v):
                    return v.tensor_copy(ptw[:, 0:128], nut[:, :])
                V(sw)

                series(ubuf)
                yf = 64
                uf = 64

                def s17(v, yf=yf, uf=uf):
                    v.tensor_scalar(utb[:, :], ubuf[:, uf:uf + 64], git[:, 127:128],
                                    None, ALU.mult)
                    v.tensor_tensor(ggc[:, :], git[:, 127:128], gamcol[:, :], ALU.mult)
                    v.tensor_scalar(ygb[:, :], ybuf[:, yf:yf + 64], ggc[:, 0:1],
                                    None, ALU.mult)
                    v.tensor_tensor(zbuf[:, :], utb[:, :], ygb[:, :], ALU.subtract)
                    return v.tensor_scalar(ynb[:, :], ybuf[:, yf:yf + 64], -1.0,
                                           None, ALU.mult)
                V(s17)

                def s18(t, h=h, qsec=qsec, uf=uf):
                    t.matmul(ps_a[:, 192:256], qtf[:, :],
                             st_buf[:, rd + h * 64:rd + (h + 1) * 64],
                             start=True, stop=False)
                    t.matmul(ps_a[:, 192:256], qkmt[:, :], ynb[:, :],
                             start=False, stop=True)
                    t.matmul(ps_a[:, 256:320], qk3[:, :], ubuf[:, uf:uf + 64],
                             start=True, stop=True)
                    return t.matmul(ps_a[0:64, 320:384], kf[:, qsec:qsec + 64],
                                    zbuf[:, :], start=True, stop=True)
                T(s18)

                def s19(v, h=h):
                    v.tensor_copy(t3b[:, :], ps_a[:, 256:320])
                    v.scalar_tensor_tensor(otile[:, h * 64:(h + 1) * 64],
                                           ps_a[:, 192:256], gamcol[:, 0:1], t3b[:, :],
                                           ALU.mult, ALU.add)
                    return v.scalar_tensor_tensor(st_buf[:, wr + h * 64:wr + (h + 1) * 64],
                                                  st_buf[:, rd + h * 64:rd + (h + 1) * 64],
                                                  gcc[:, 0:1], ps_a[0:64, 320:384],
                                                  ALU.mult, ALU.add)
                V(s19)

            def pr1(t):
                for kc in range(KC):
                    ins = t.transpose(ps_tpb[:, kc * 128:(kc + 1) * 128],
                                      otile[:, kc * 128:(kc + 1) * 128], eyeb[:, :])
                return ins
            T(pr1)

            def pr2(v):
                return v.tensor_copy(oT[:, :].rearrange("p (c2 u) -> p c2 u", c2=KC),
                                     ps_tpb[:, :].rearrange("p (c2 u) -> p c2 u", c2=KC))
            V(pr2)

            for half in range(2):
                def pr3(t, half=half):
                    for kc in range(KC):
                        ins = t.matmul(ps_q[:, half * 512:(half + 1) * 512],
                                       oT[:, kc * 128:(kc + 1) * 128],
                                       wosb[:, kc * 1024 + half * 512:kc * 1024 + (half + 1) * 512],
                                       start=(kc == 0), stop=(kc == KC - 1))
                    return ins
                T(pr3)

                def pr4(v, half=half):
                    if half == 0 and k >= 2:
                        v.wait_ge(sm["yd%d" % (k % 2)], 16 * (k // 2))
                    ins = v.tensor_tensor(
                        ytile[:, (c % 2) * 1024 + half * 512:(c % 2) * 1024 + (half + 1) * 512],
                        ps_q[:, half * 512:(half + 1) * 512],
                        xtile[:, (c % 2) * 1024 + half * 512:(c % 2) * 1024 + (half + 1) * 512],
                        ALU.add)
                    if half == 1:
                        v.sig(sm["yo"], k + 1)
                    return ins
                V(pr4)

        for b in range(BLOC):
            for c in range(NTT):
                chunk_steps(b, c)

        with nc.Block() as blk:
            @blk.sync
            def _(s):
                nrc = [0]

                def cdma(dst, src):
                    s.dma_start(out=dst, in_=src).then_inc(sm["rc"], 16)
                    nrc[0] += 16
                cdma(eyeb[:, :], eyeb_in[:, :])
                cdma(eyef[:, :], eyef_in[:, :])
                cdma(triuI[:, :], triuI_in[:, :])
                cdma(triuS[:, :], triuS_in[:, :])
                cdma(mbI[:, :], mbI_in[:, :])
                cdma(mbS[:, :], mbS_in[:, :])
                cdma(o1x128[0:1, :], o1x128_in[:, :])
                cdma(ocol[:, :], ocol_in[:, :])
                cdma(o1x64[0:1, :], o1x64_in[:, :])
                cdma(bgrow[0:1, 0:16], bg.rearrange("(a j) -> a j", a=1))
                cdma(bgrow[0:1, 16:32], bb.rearrange("(a j) -> a j", a=1))
                cdma(wabf.rearrange("q (kc j) -> q kc j", kc=KC)[:, :, 0:16],
                     WgT.rearrange("(kc q) j -> q kc j", q=128))
                cdma(wabf.rearrange("q (kc j) -> q kc j", kc=KC)[:, :, 16:32],
                     WbT.rearrange("(kc q) j -> q kc j", q=128))
                for b in range(BLOC):
                    for c in range(NTT):
                        k = b * NTT + c
                        if k >= 2:
                            s.wait_ge(sm["yo"], k - 1)
                        s.dma_start(out=xtile[:, (c % 2) * 1024:(c % 2 + 1) * 1024],
                                    in_=xin[k * 128:(k + 1) * 128, :]).then_inc(sm["x%d" % (c % 2)], 16)
                        s.wait_ge(sm["yo"], k + 1)
                        s.dma_start(out=xout[k * 128:(k + 1) * 128, :],
                                    in_=ytile[:, (c % 2) * 1024:(c % 2 + 1) * 1024]).then_inc(sm["yd%d" % (c % 2)], 16)
                s.wait_ge(sm["yd0"], 16 * (BLOC * NTT // 2))
                s.wait_ge(sm["yd1"], 16 * (BLOC * NTT // 2))

            @blk.gpsimd
            def _(g):
                g.dma_start(out=wqsb.rearrange("q (kc c2) -> q kc c2", kc=KC),
                            in_=WqkvT.rearrange("(kc q) c2 -> q kc c2", q=128)).then_inc(sm["w"], 16)
                g.dma_start(out=wosb.rearrange("q (kc c2) -> q kc c2", kc=KC),
                            in_=WoutT.rearrange("(kc q) c2 -> q kc c2", q=128)).then_inc(sm["w"], 16)
                g.wait_ge(sm["w"], 32)

            @blk.tensor
            def _(t):
                t.wait_ge(sm["rc"], 208)
                t.wait_ge(sm["w"], 32)
                he = 0
                for (eng, fn) in STEPS:
                    if eng == "T":
                        t.wait_ge(sm["he"], he)
                        ins = fn(t)
                        ins.then_inc(sm["hm"], 1)
                    else:
                        he += 1

            @blk.scalar
            def _(sc_raw):
                sc = SelfSync(sc_raw, sm["sch"])
                hm = 0
                he = 0
                for (eng, fn) in STEPS:
                    if eng == "T":
                        hm += 1
                    elif eng == "S":
                        sc.wait_ge(sm["hm"], hm)
                        sc.wait_ge(sm["he"], he)
                        fn(sc)
                        sc.nop()
                        sc.sig(sm["he"], he + 1)
                        he += 1
                    else:
                        he += 1

            @blk.vector
            def _(v_raw):
                v = SelfSync(v_raw, sm["vch"])
                v.wait_ge(sm["rc"], 208)
                hm = 0
                he = 0
                for (eng, fn) in STEPS:
                    if eng == "T":
                        hm += 1
                    elif eng == "V":
                        v.wait_ge(sm["hm"], hm)
                        v.wait_ge(sm["he"], he)
                        fn(v)
                        v.nop()
                        v.sig(sm["he"], he + 1)
                        he += 1
                    else:
                        he += 1


def _rotary_tables_np(t, dim):
    thetas = 10000.0 ** (-np.arange(0, dim, 2, dtype=np.float32) / dim)
    ticks = np.outer(np.arange(t, dtype=np.float32), thetas)
    ticks = np.tile(ticks, (1, 2))
    return np.cos(ticks).astype(np.float32), np.sin(ticks).astype(np.float32)


def _launch(build, x, extra, percore=None):
    p = Prog()
    nc = p.nc
    xin = nc.dram_tensor("xin", [NTOK, D], F32, kind="ExternalInput").ap()
    xout = nc.dram_tensor("xout", [NTOK, D], F32, kind="ExternalOutput").ap()
    aps = {}
    for nmm, (shape, arr) in extra.items():
        dt = F32 if arr.dtype == np.float32 else BF16
        aps[nmm] = nc.dram_tensor(nmm, list(shape), dt, kind="ExternalInput").ap()
    if percore:
        for nmm, (shape, arrs) in percore.items():
            aps[nmm] = nc.dram_tensor(nmm, list(shape), F32, kind="ExternalInput").ap()
    build(p, xin, xout, aps)
    base = {nmm: np.ascontiguousarray(arr) for nmm, (shape, arr) in extra.items()}
    base.update(p.extra_inputs)
    in_maps = []
    for c in range(NCORES):
        m = dict(base)
        m["xin"] = np.ascontiguousarray(x[c * BLOC:(c + 1) * BLOC].reshape(NTOK, D))
        if percore:
            for nmm, (shape, arrs) in percore.items():
                m[nmm] = np.ascontiguousarray(arrs[c])
        in_maps.append(m)
    res = run_bass_kernel_spmd(nc, in_maps, core_ids=list(range(NCORES)))
    return np.concatenate([r["xout"].reshape(BLOC, T, D) for r in res.results], axis=0)


def kernel(**inputs):
    g = lambda k: np.asarray(inputs[k], np.float32)
    x = g("L0_attn_a_x")

    def b1(p, xin, xout, aps):
        emit_l0_attn_a(p, xin, xout, aps["wqkv"], aps["bqkv"], aps["wout"], aps["bout"])
    x = _launch(b1, x, {
        "wqkv": ((D, 3 * D), g("L0_attn_a_w_qkv")),
        "bqkv": ((3 * D,), g("L0_attn_a_b_qkv")),
        "wout": ((D, D), g("L0_attn_a_w_out")),
        "bout": ((D,), g("L0_attn_a_b_out")),
    })

    cos, sin = _rotary_tables_np(T, 64)
    ss = sin.copy()
    ss[:, 0:32] = -ss[:, 0:32]
    cosr = np.ascontiguousarray(np.tile(cos, (1, 16)).astype(BFDT))
    sinr = np.ascontiguousarray(np.tile(ss, (1, 16)).astype(BFDT))
    cms = [np.eye(8, dtype=np.float32)[c:c + 1].copy() for c in range(8)]
    pms = []
    for c in range(8):
        pm = np.zeros((8, 1), np.float32)
        pm[0:c, 0] = 1.0
        pms.append(pm)
    f0_extra = {
        "f0g1": ((D,), g("L0_ffn_g1")), "f0g2": ((D,), g("L0_ffn_g2")),
        "f0wq": ((D, D), g("L0_ffn_wq")), "f0wk": ((D, 256), g("L0_ffn_wk")),
        "f0wv": ((D, 256), g("L0_ffn_wv")), "f0wo": ((D, D), g("L0_ffn_wo")),
        "f0rw": ((D, 8), g("L0_ffn_router_w")), "f0rb": ((8,), g("L0_ffn_router_b")),
        "f0w1": ((8, D, 2048), g("L0_ffn_w1")), "f0w2": ((8, D, 2048), g("L0_ffn_w2")),
        "f0w3": ((8, 2048, D), g("L0_ffn_w3")),
        "f0cos": ((T, D), cosr), "f0sin": ((T, D), sinr),
    }
    f0_percore = {"f0cm": ((1, 8), cms), "f0pm": ((8, 1), pms)}

    def b2(p, xin, xout, aps):
        emit_l0_ffn(p, xin, xout, aps["f0g1"], aps["f0g2"], aps["f0wq"], aps["f0wk"],
                    aps["f0wv"], aps["f0wo"], aps["f0rw"], aps["f0rb"], aps["f0w1"],
                    aps["f0w2"], aps["f0w3"], aps["f0cos"], aps["f0sin"],
                    aps["f0cm"], aps["f0pm"])
    x = _launch(b2, x, f0_extra, f0_percore)

    def b3(p, xin, xout, aps):
        emit_l0_attn_b(p, xin, xout, aps["wjT"], aps["wvT"], aps["woT"])
    ab_extra = {
        "wjT": ((D, D), np.ascontiguousarray(g("L0_attn_b_wj_w").T)),
        "wvT": ((D, D), np.ascontiguousarray(g("L0_attn_b_wjv_w").T)),
        "woT": ((D, D), np.ascontiguousarray(g("L0_attn_b_wo_w").T)),
    }
    x = _launch(b3, x, ab_extra)

    x = _launch(b2, x, f0_extra, f0_percore)

    def b5(p, xin, xout, aps):
        emit_l1_attn_a(p, xin, xout, aps["laq"], aps["lab"], aps["lal"], aps["lalb"],
                       aps["lap"], aps["lapb"])
    x = _launch(b5, x, {
        "laq": ((D, 3 * D), np.ascontiguousarray(g("L1_attn_a_W_qkv").T)),
        "lab": ((128, 24), np.ascontiguousarray(g("L1_attn_a_b_qkv").reshape(24, 128).T)),
        "lal": ((64, 64), np.ascontiguousarray(g("L1_attn_a_W_lat").T)),
        "lalb": ((64,), g("L1_attn_a_b_lat")),
        "lap": ((D, D), np.ascontiguousarray(g("L1_attn_a_W_proj").T)),
        "lapb": ((D,), g("L1_attn_a_b_proj")),
    })

    f1_extra = {
        "f1wg": ((D, 8), np.ascontiguousarray(g("L1_ffn_wg").T)),
        "f1w1T": ((8, D, 2048), np.ascontiguousarray(g("L1_ffn_w1").transpose(0, 2, 1))),
        "f1w3T": ((8, D, 2048), np.ascontiguousarray(g("L1_ffn_w3").transpose(0, 2, 1))),
        "f1w2T": ((8, 2048, D), np.ascontiguousarray(g("L1_ffn_w2").transpose(0, 2, 1))),
    }

    def b6(p, xin, xout, aps):
        emit_l1_ffn(p, xin, xout, aps["f1wg"], aps["f1w1T"], aps["f1w3T"], aps["f1w2T"])
    x = _launch(b6, x, f1_extra)

    def b7(p, xin, xout, aps):
        emit_l1_attn_b(p, xin, xout, aps["dbq"], aps["dbg"], aps["dbbg"], aps["dbb"],
                       aps["dbbb"], aps["dbo"])
    x = _launch(b7, x, {
        "dbq": ((D, 3 * D), np.ascontiguousarray(g("L1_attn_b_Wqkv").T)),
        "dbg": ((D, 16), np.ascontiguousarray(g("L1_attn_b_Wg").T)),
        "dbbg": ((16,), g("L1_attn_b_bg")),
        "dbb": ((D, 16), np.ascontiguousarray(g("L1_attn_b_Wb").T)),
        "dbbb": ((16,), g("L1_attn_b_bb")),
        "dbo": ((D, D), np.ascontiguousarray(g("L1_attn_b_Wout").T)),
    })

    x = _launch(b6, x, f1_extra)
    return np.ascontiguousarray(x, np.float32)



# version 04 (final)
# speedup vs baseline: 1.0044x; 1.0044x over previous
import numpy as np
import ml_dtypes
from contextlib import ExitStack
import concourse.bass as bass
import concourse.mybir as mybir
from concourse.bass_utils import run_bass_kernel_spmd

F32 = mybir.dt.float32
BF16 = mybir.dt.bfloat16
AF = mybir.ActivationFunctionType
ALU = mybir.AluOpType
BFDT = ml_dtypes.bfloat16

NCORES = 8
BLOC = 4
DBG_ROUTE = False
T = 2048
NTOK = BLOC * T
D = 1024
NTT = T // 128
KC = D // 128
ES = 512
NCH = T // ES


class Prog:
    def __init__(self):
        self.nc = bass.Bass("TRN2", target_bir_lowering=False, debug=False)
        self._sems = {}
        self.extra_inputs = {}
        self.extra_aps = {}

    def sem(self, name):
        if name not in self._sems:
            self._sems[name] = self.nc.alloc_semaphore(name)
        return self._sems[name]

    def reg_const(self, tag, vals):
        h = self.nc.alloc_sbuf_tensor(tag + "_cpoolP", [128, 16], F32)
        arr = np.zeros((128, 16), np.float32)
        for i, v in enumerate(vals):
            arr[:, i] = v
            self.nc.const_aps.register(F32, float(v), h.ap()[:, i:i + 1])
        ap_in = self.const_input(tag + "_cpoolin", arr)
        return h.ap(), ap_in

    def const_input(self, name, arr):
        if name in self.extra_inputs:
            return self.extra_aps[name]
        arr = np.ascontiguousarray(arr)
        self.extra_inputs[name] = arr
        dt = {np.dtype(np.float32): F32, np.dtype(BFDT): BF16}[arr.dtype]
        ap = self.nc.dram_tensor(name, list(arr.shape), dt, kind="ExternalInput").ap()
        self.extra_aps[name] = ap
        return ap


class SelfSync:

    def __init__(self, eng, sem):
        self._e = eng
        self._s = sem
        self._n = 0
        self._last = None
        self._edge = None

    def wait_ge(self, sem, val):
        return self._e.wait_ge(sem, val)

    def sig(self, sem, val):
        assert self._last is not None
        self._last.then_inc(sem, 1)
        self._edge = (sem, val)
        self._last = None

    def _advance(self):
        if self._last is not None:
            self._last.then_inc(self._s, 1)
            self._n += 1
            self._edge = (self._s, self._n)
            self._last = None
        if self._edge is not None:
            self._e.wait_ge(*self._edge)
            self._edge = None

    def __getattr__(self, name):
        f = getattr(self._e, name)

        def wrap(*a, **k):
            self._advance()
            ins = f(*a, **k)
            self._last = ins
            return ins
        return wrap


def emit_l0_attn_a(p, xin, xout, w_qkv, b_qkv, w_out, b_out, tag="l0a"):
    nc = p.nc
    M_STAB = 32.0
    SCALE = 0.125

    consts = np.zeros((128, 8), np.float32)
    consts[:, 0] = -M_STAB
    cpool_in = p.const_input(tag + "_cpool", consts)
    ones_row = p.const_input(tag + "_ones", np.ones((64, 128), np.float32))
    eyeb_in = p.const_input(tag + "_eyeb", np.eye(128, dtype=BFDT))
    eyef_in = p.const_input(tag + "_eyef", np.eye(128, dtype=np.float32))
    masks = np.zeros((4, 128, ES), BFDT)
    sl = np.arange(128)[:, None]
    tl = np.arange(ES)[None, :]
    for j in range(4):
        masks[j] = (128 * j + sl >= tl).astype(BFDT)
    mask_in = p.const_input(tag + "_masks", np.ascontiguousarray(masks.transpose(1, 0, 2).reshape(128, 4 * ES)))

    wconv = nc.dram_tensor(tag + "_wconv", [128, KC * 3072], BF16).ap()
    oconv = nc.dram_tensor(tag + "_oconv", [128, KC * 1024], BF16).ap()

    sm = {n: p.sem(tag + n) for n in
          ["w", "wc", "x", "sq", "vb", "sd", "ln", "tp", "xT", "qk", "qe",
           "vn", "ve", "sc", "pf", "pm", "av", "li", "rp", "rpc", "oe",
           "pj", "yd", "st", "pro", "rc", "wm", "cvt", "ye",
           "x0", "x1", "wm0", "wm1", "st0", "st1", "va", "vch", "sch"]}

    fl_tiles = [list(range(4 * tc, NTT)) for tc in range(NCH)]
    fl_off = [sum(len(fl_tiles[c]) for c in range(tc)) for tc in range(NCH)]
    TPB = sum(len(x) for x in fl_tiles)

    NBLK = 20

    with ExitStack() as _es:
        def A(nm, shape, dt):
            return _es.enter_context(nc.sbuf_tensor(tag + nm, shape, dt))

        cpool = A("cpool", [128, 8], F32)
        bcol = A("bcol", [128, 24], F32)
        bvrep = A("bvrep", [128, 1024], F32)
        borep = A("borep", [128, 1024], F32)
        onesr = A("onesr", [64, 128], F32)
        eyeb = A("eyeb", [128, 128], BF16)
        eyef = A("eyef", [128, 128], F32)
        rows = A("rows", [64, 1024], F32)
        maskt = A("maskt", [128, 4 * ES], BF16)
        btmp = A("btmp", [24, 128], F32)
        stg = A("stg", [128, 3072], BF16)
        wstream = A("wstream", [128, 2 * 4096], BF16)
        xtile = A("xtile", [128, 2 * 1024], F32)
        stat = A("stat", [128, 16], F32)
        sjunk = A("sjunk", [128, 1024], F32)
        vjunk = A("vjunk", [128, 1024], F32)
        xln = A("xln", [128, 2 * 1024], BF16)
        xT = A("xT", [128, KC * T], BF16)
        qkT = A("qkT", [128, 16 * T], BF16)
        vb = A("vb", [128, NTT * 1040], BF16)
        ptile = A("pt", [128, 2 * ES], BF16)
        repsb = A("repsb", [64, 2 * ES], F32)
        ytile = A("ytile", [128, 2 * 1024], F32)
        ofm = xT
        linv = rows
        ps_s = _es.enter_context(nc.psum_tensor(tag + "ps_s", [128, 2 * ES], F32))
        ps_o = _es.enter_context(nc.psum_tensor(tag + "ps_o", [128, 2 * ES], F32))
        ps_q = _es.enter_context(nc.psum_tensor(tag + "ps_q", [128, 2 * ES], F32))
        ps_r = _es.enter_context(nc.psum_tensor(tag + "ps_r", [128, ES], F32))
        ps_tp = _es.enter_context(nc.psum_tensor(tag + "ps_tp", [128, 1024], BF16))
        ps_b = ps_r

        nc.const_aps.register(F32, -M_STAB, cpool[:, 0:1])
        wcv = wconv.rearrange("p (kc j) -> p kc j", kc=KC)
        ocv = oconv.rearrange("p (kc j) -> p kc j", kc=KC)

        def wblk_src(bi):
            if bi < 16:
                return wcv[:, :, bi * 128:(bi + 1) * 128]
            if bi < 18:
                n = bi - 16
                return wcv[:, :, 2048 + n * ES:2048 + (n + 1) * ES]
            n = bi - 18
            return ocv[:, :, n * ES:(n + 1) * ES]

        def wslot(bi, parity):
            cols = 1024 if bi < 16 else 4096
            return wstream[:, parity * 4096:parity * 4096 + cols].rearrange(
                "p (kc j) -> p kc j", kc=KC)

        with nc.Block(tag) as blk:
            @blk.sync
            def _(s):
                s.dma_start(out=btmp[:, :], in_=b_qkv.rearrange("(m q) -> m q", q=128)).then_inc(sm["w"], 16)
                s.dma_start(out=onesr[:, :], in_=ones_row[:, :]).then_inc(sm["w"], 16)
                s.dma_start(out=maskt[:, :], in_=mask_in[:, :]).then_inc(sm["w"], 16)
                s.dma_start(out=rows[0:1, :], in_=b_out.rearrange("(a q) -> a q", a=1)).then_inc(sm["w"], 16)
                s.dma_start(out=rows[32:33, :], in_=b_qkv.rearrange("(a q) -> a q", a=3)[2:3, :]).then_inc(sm["w"], 16)
                s.dma_start(out=cpool[:, :], in_=cpool_in[:, :]).then_inc(sm["w"], 16)
                s.dma_start(out=eyeb[:, :], in_=eyeb_in[:, :]).then_inc(sm["w"], 16)
                s.dma_start(out=eyef[:, :], in_=eyef_in[:, :]).then_inc(sm["w"], 16)
                for b in range(BLOC):
                    t0 = b * T
                    for tt in range(NTT):
                        k = b * NTT + tt
                        if k >= 2:
                            s.wait_ge(sm["ln"], k - 1)
                        if b > 0 and tt < 2:
                            s.wait_ge(sm["yd"], b * NTT)
                        s.dma_start(out=xtile[:, (tt % 2) * 1024:(tt % 2 + 1) * 1024],
                                    in_=xin[t0 + tt * 128:t0 + (tt + 1) * 128, :]).then_inc(sm["x%d" % (tt % 2)], 16)
                    for bi in range(NBLK):
                        g = b * NBLK + bi
                        if g >= 2:
                            pb_, pbi = divmod(g - 2, NBLK)
                            if pbi < 16:
                                s.wait_ge(sm["qk"], pb_ * 64 + (pbi + 1) * NCH)
                            elif pbi < 18:
                                s.wait_ge(sm["vn"], pb_ * 32 + (pbi - 16 + 1) * 16)
                            else:
                                s.wait_ge(sm["pj"], pb_ * 32 + 31 + (pbi - 18))
                        if g == 0:
                            s.wait_ge(sm["cvt"], 16 * 16)
                        s.dma_start(out=wslot(bi, g % 2), in_=wblk_src(bi)).then_inc(sm["wm%d" % (g % 2)], 16)
                    for tt in range(NTT):
                        k = b * NTT + tt
                        if tt < 2:
                            s.wait_ge(sm["ln"], (b + 1) * NTT)
                        else:
                            s.wait_ge(sm["yd"], k - 1)
                        s.dma_start(out=xtile[:, (tt % 2) * 1024:(tt % 2 + 1) * 1024],
                                    in_=xin[t0 + tt * 128:t0 + (tt + 1) * 128, :]).then_inc(sm["x%d" % (tt % 2)], 16)
                        if tt >= 2:
                            st_t = tt - 2
                            s.wait_ge(sm["yd"], b * NTT + st_t + 1)
                            s.dma_start(out=xout[t0 + st_t * 128:t0 + (st_t + 1) * 128, :],
                                        in_=ytile[:, (st_t % 2) * 1024:(st_t % 2 + 1) * 1024]).then_inc(sm["st%d" % (st_t % 2)], 16)
                    for st_t in (NTT - 2, NTT - 1):
                        s.wait_ge(sm["yd"], b * NTT + st_t + 1)
                        s.dma_start(out=xout[t0 + st_t * 128:t0 + (st_t + 1) * 128, :],
                                    in_=ytile[:, (st_t % 2) * 1024:(st_t % 2 + 1) * 1024]).then_inc(sm["st%d" % (st_t % 2)], 16)
                s.wait_ge(sm["st0"], 16 * (BLOC * NTT // 2))
                s.wait_ge(sm["st1"], 16 * (BLOC * NTT // 2))

            @blk.gpsimd
            def _(g):
                cv = 0
                for kc in range(KC):
                    if cv > 0:
                        g.wait_ge(sm["cvt"], cv * 16)
                    g.dma_start(out=stg[:, :], in_=w_qkv[kc * 128:(kc + 1) * 128, :]).then_inc(sm["wc"], 16)
                    cv += 1
                    g.wait_ge(sm["wc"], cv * 16)
                    g.dma_start(out=wcv[:, kc, :], in_=stg[:, :]).then_inc(sm["cvt"], 16)
                for kc in range(KC):
                    g.wait_ge(sm["cvt"], cv * 16)
                    g.dma_start(out=stg[:, 0:1024], in_=w_out[kc * 128:(kc + 1) * 128, :]).then_inc(sm["wc"], 16)
                    cv += 1
                    g.wait_ge(sm["wc"], cv * 16)
                    g.dma_start(out=ocv[:, kc, :], in_=stg[:, 0:1024]).then_inc(sm["cvt"], 16)

            @blk.tensor
            def _(t):
                t.wait_ge(sm["w"], 16 * 8)
                t.transpose(ps_b[0:128, 0:24], btmp[:, :], eyef[0:24, 0:24])
                for half in range(2):
                    t.matmul(ps_q[:, half * ES:(half + 1) * ES], onesr[0:1, :],
                             rows[0:1, half * ES:(half + 1) * ES], start=True, stop=True)
                t.matmul(ps_o[:, 0:ES], onesr[32:33, :],
                         rows[32:33, 0:ES], start=True, stop=True)
                t.matmul(ps_o[:, ES:2 * ES], onesr[32:33, :],
                         rows[32:33, ES:2 * ES], start=True, stop=True).then_inc(sm["pro"], 1)
                t.wait_ge(sm["rc"], 1)
                for b in range(BLOC):
                    for tt in range(NTT):
                        k = b * NTT + tt
                        t.wait_ge(sm["ln"], k + 1)
                        if k >= 1:
                            t.wait_ge(sm["xT"], k)
                        for kc in range(KC):
                            ins = t.transpose(ps_tp[:, kc * 128:(kc + 1) * 128],
                                              xln[:, (tt % 2) * 1024 + kc * 128:(tt % 2) * 1024 + (kc + 1) * 128],
                                              eyeb[:, :])
                        ins.then_inc(sm["tp"], 1)
                    t.wait_ge(sm["xT"], (b + 1) * NTT)
                    for m in range(16):
                        g = b * NBLK + m
                        t.wait_ge(sm["wm%d" % (g % 2)], 16 * (g // 2 + 1))
                        ws = wslot(m, g % 2)
                        for tc in range(NCH):
                            mmcnt = m * NCH + tc
                            if mmcnt >= 2:
                                t.wait_ge(sm["qe"], b * 64 + mmcnt - 1)
                            for kc in range(KC):
                                ins = t.matmul(ps_q[:, (mmcnt % 2) * ES:(mmcnt % 2 + 1) * ES],
                                               ws[:, kc, :],
                                               xT[:, kc * T + tc * ES:kc * T + (tc + 1) * ES],
                                               start=(kc == 0), stop=(kc == KC - 1))
                            ins.then_inc(sm["qk"], 1)
                    for nblk in range(2):
                        g = b * NBLK + 16 + nblk
                        t.wait_ge(sm["wm%d" % (g % 2)], 16 * (g // 2 + 1))
                        ws = wslot(16 + nblk, g % 2)
                        for tt in range(NTT):
                            vcnt = nblk * 16 + tt
                            if vcnt >= 2:
                                t.wait_ge(sm["ve"], b * 32 + vcnt - 1)
                            else:
                                if b > 0:
                                    t.wait_ge(sm["ve"], b * 32)
                                    t.wait_ge(sm["oe"], b * 64)
                            for kc in range(KC):
                                ins = t.matmul(ps_o[:, (vcnt % 2) * ES:(vcnt % 2 + 1) * ES],
                                               xT[:, kc * T + tt * 128:kc * T + (tt + 1) * 128],
                                               ws[:, kc, :],
                                               start=(kc == 0), stop=(kc == KC - 1))
                            ins.then_inc(sm["vn"], 1)
                    t.wait_ge(sm["ve"], (b + 1) * 32)
                    t.wait_ge(sm["qe"], (b + 1) * 64)
                    for h in range(16):
                        pb = (h % 2) * 64
                        mq = h // 2
                        mk = 8 + h // 2
                        for tc in range(NCH):
                            tiles = fl_tiles[tc]
                            c_idx = b * 64 + h * NCH + tc
                            opar = c_idx % 2
                            for i, tt in enumerate(tiles):
                                gi = (b * 16 + h) * TPB + fl_off[tc] + i
                                if gi >= 2:
                                    t.wait_ge(sm["pf"], gi - 1)
                                t.matmul(ps_s[:, (gi % 2) * ES:(gi % 2 + 1) * ES],
                                         qkT[pb:pb + 64, mk * T + tt * 128:mk * T + (tt + 1) * 128],
                                         qkT[pb:pb + 64, mq * T + tc * ES:mq * T + (tc + 1) * ES],
                                         start=True, stop=True).then_inc(sm["sc"], 1)
                                t.wait_ge(sm["pm"], gi + 1)
                                if i == 0 and c_idx >= 2:
                                    t.wait_ge(sm["oe"], c_idx - 1)
                                t.matmul(ps_o[0:65, opar * ES:opar * ES + ES],
                                         vb[:, tt * 1040 + h * 65:tt * 1040 + h * 65 + 65],
                                         ptile[:, (gi % 2) * ES:(gi % 2 + 1) * ES],
                                         start=(i == 0), stop=(i == len(tiles) - 1)).then_inc(sm["av"], 1)
                            t.wait_ge(sm["li"], c_idx + 1)
                            if c_idx >= 1:
                                t.wait_ge(sm["rpc"], c_idx)
                            t.matmul(ps_r[0:64, :], onesr[0:1, 0:64],
                                     linv[0:1, opar * ES:(opar + 1) * ES],
                                     start=True, stop=True).then_inc(sm["rp"], 1)
                    t.wait_ge(sm["oe"], (b + 1) * 64)
                    wsE = []
                    for nblk in range(2):
                        g = b * NBLK + 18 + nblk
                        t.wait_ge(sm["wm%d" % (g % 2)], 16 * (g // 2 + 1))
                        wsE.append(wslot(18 + nblk, g % 2))
                    for tt in range(NTT):
                        for nblk in range(2):
                            pjc = tt * 2 + nblk
                            if pjc >= 2:
                                t.wait_ge(sm["ye"], b * 32 + pjc - 1)
                            for kc in range(KC):
                                ins = t.matmul(ps_q[:, (pjc % 2) * ES:(pjc % 2 + 1) * ES],
                                               ofm[:, kc * T + tt * 128:kc * T + (tt + 1) * 128],
                                               wsE[nblk][:, kc, :],
                                               start=(kc == 0), stop=(kc == KC - 1))
                            ins.then_inc(sm["pj"], 1)

            @blk.scalar
            def _(sc_raw):
                sc = SelfSync(sc_raw, sm["sch"])
                for b in range(BLOC):
                    for tt in range(NTT):
                        k = b * NTT + tt
                        sc.wait_ge(sm["x%d" % (tt % 2)], 16 * (b * 16 + tt // 2 + 1))
                        xa = xtile[:, (tt % 2) * 1024:(tt % 2 + 1) * 1024]
                        st8 = (tt % 2) * 8
                        sc.activation(sjunk[:, :], xa, AF.Square,
                                      accum_out=stat[:, st8 + 1:st8 + 2])
                        sc.wait_ge(sm["vb"], k + 1)
                        sc.activation(stat[:, st8 + 3:st8 + 4],
                                      stat[:, st8 + 1:st8 + 2], AF.Sqrt,
                                      bias=stat[:, st8 + 2:st8 + 3],
                                      scale=1.0 / 1024.0)
                        sc.sig(sm["sd"], k + 1)
                    for m in range(16):
                        for tc in range(NCH):
                            i = m * NCH + tc
                            sc.wait_ge(sm["qk"], b * 64 + i + 1)
                            sc.activation(qkT[:, m * T + tc * ES:m * T + (tc + 1) * ES],
                                          ps_q[:, (i % 2) * ES:(i % 2 + 1) * ES], AF.Identity,
                                          bias=bcol[:, m:m + 1])
                            sc.sig(sm["qe"], b * 64 + i + 1)
                    for h in range(16):
                        for tc in range(NCH):
                            tiles = fl_tiles[tc]
                            for i, tt in enumerate(tiles):
                                gi = (b * 16 + h) * TPB + fl_off[tc] + i
                                if gi >= 2:
                                    sc.wait_ge(sm["av"], gi - 1)
                                sc.wait_ge(sm["sc"], gi + 1)
                                sc.activation(ptile[:, (gi % 2) * ES:(gi % 2 + 1) * ES],
                                              ps_s[:, (gi % 2) * ES:(gi % 2 + 1) * ES], AF.Exp,
                                              bias=-M_STAB, scale=SCALE)
                                sc.sig(sm["pf"], gi + 1)

            @blk.vector
            def _(v_raw):
                v = SelfSync(v_raw, sm["vch"])
                v.wait_ge(sm["pro"], 1)
                v.tensor_copy(bcol[:, :], ps_b[0:128, 0:24])
                v.tensor_copy(borep[:, 0:ES], ps_q[:, 0:ES])
                v.tensor_copy(borep[:, ES:2 * ES], ps_q[:, ES:2 * ES])
                v.tensor_copy(bvrep[:, 0:ES], ps_o[:, 0:ES])
                v.tensor_copy(bvrep[:, ES:2 * ES], ps_o[:, ES:2 * ES])
                v.memset(vb.rearrange("p (u h j) -> p u h j", u=NTT, h=16)[:, :, :, 64:65], 1.0)
                v.sig(sm["rc"], 1)
                for b in range(BLOC):
                    for tt in range(NTT):
                        k = b * NTT + tt
                        xa = xtile[:, (tt % 2) * 1024:(tt % 2 + 1) * 1024]
                        st8 = (tt % 2) * 8
                        v.wait_ge(sm["x%d" % (tt % 2)], 16 * (b * 16 + tt // 2 + 1))
                        v.tensor_scalar(vjunk[:, :], xa, 1.0, None, ALU.mult, ALU.add,
                                        accum_out=stat[:, st8 + 0:st8 + 1])
                        v.tensor_scalar(stat[:, st8 + 4:st8 + 5], stat[:, st8 + 0:st8 + 1],
                                        1.0 / 1024.0, None, ALU.mult)
                        v.tensor_tensor(stat[:, st8 + 5:st8 + 6], stat[:, st8 + 4:st8 + 5],
                                        stat[:, st8 + 4:st8 + 5], ALU.mult)
                        v.tensor_scalar(stat[:, st8 + 2:st8 + 3], stat[:, st8 + 5:st8 + 6],
                                        -1.0, 1e-5, ALU.mult, ALU.add)
                        v.sig(sm["vb"], k + 1)
                        v.wait_ge(sm["sd"], k + 1)
                        v.reciprocal(stat[:, st8 + 6:st8 + 7], stat[:, st8 + 3:st8 + 4])
                        v.tensor_scalar(xln[:, (tt % 2) * 1024:(tt % 2 + 1) * 1024], xa,
                                        stat[:, st8 + 4:st8 + 5], stat[:, st8 + 6:st8 + 7],
                                        ALU.subtract, ALU.mult)
                        v.sig(sm["ln"], k + 1)
                        v.wait_ge(sm["tp"], k + 1)
                        v.tensor_copy(
                            xT.rearrange("p (kc u) -> p kc u", kc=KC)[:, :, tt * 128:(tt + 1) * 128],
                            ps_tp[:, :].rearrange("p (c u) -> p c u", c=KC))
                        v.sig(sm["xT"], k + 1)
                    for nblk in range(2):
                        for tt in range(NTT):
                            i = nblk * 16 + tt
                            v.wait_ge(sm["vn"], b * 32 + i + 1)
                            v.tensor_tensor(
                                vb.rearrange("p (u h j) -> p u h j", u=NTT, h=16)[:, tt, nblk * 8:(nblk + 1) * 8, 0:64],
                                ps_o[:, (i % 2) * ES:(i % 2 + 1) * ES].rearrange("p (h j) -> p h j", h=8),
                                bvrep[:, nblk * ES:(nblk + 1) * ES].rearrange("p (h j) -> p h j", h=8),
                                ALU.add)
                            v.sig(sm["ve"], b * 32 + i + 1)
                    for h in range(16):
                        pb = (h % 2) * 64
                        for tc in range(NCH):
                            tiles = fl_tiles[tc]
                            c_idx = b * 64 + h * NCH + tc
                            opar = c_idx % 2
                            for i, tt in enumerate(tiles):
                                gi = (b * 16 + h) * TPB + fl_off[tc] + i
                                v.wait_ge(sm["pf"], gi + 1)
                                j = tt - 4 * tc
                                if 0 <= j < 4:
                                    v.tensor_tensor(ptile[:, (gi % 2) * ES:(gi % 2 + 1) * ES],
                                                    ptile[:, (gi % 2) * ES:(gi % 2 + 1) * ES],
                                                    maskt[:, j * ES:(j + 1) * ES], ALU.mult)
                                else:
                                    v.nop()
                                v.sig(sm["pm"], gi + 1)
                            gi_last = (b * 16 + h) * TPB + fl_off[tc] + len(tiles) - 1
                            v.wait_ge(sm["av"], gi_last + 1)
                            v.reciprocal(linv[0:1, opar * ES:(opar + 1) * ES],
                                         ps_o[64:65, opar * ES:(opar + 1) * ES])
                            v.sig(sm["li"], c_idx + 1)
                            v.wait_ge(sm["rp"], c_idx + 1)
                            v.tensor_copy(repsb[0:64, opar * ES:(opar + 1) * ES],
                                          ps_r[0:64, :])
                            v.sig(sm["rpc"], c_idx + 1)
                            v.tensor_tensor(ofm[pb:pb + 64, (h // 2) * T + tc * ES:(h // 2) * T + (tc + 1) * ES],
                                            ps_o[0:64, opar * ES:(opar + 1) * ES],
                                            repsb[0:64, opar * ES:(opar + 1) * ES],
                                            ALU.mult)
                            v.sig(sm["oe"], c_idx + 1)
                    for tt in range(NTT):
                        k = b * NTT + tt
                        v.wait_ge(sm["x%d" % (tt % 2)], 16 * (b * 16 + 8 + tt // 2 + 1))
                        if tt >= 2:
                            v.wait_ge(sm["st%d" % (tt % 2)], 16 * (b * 8 + tt // 2))
                        elif b > 0:
                            v.wait_ge(sm["st%d" % (tt % 2)], 16 * (b * 8))
                        for nblk in range(2):
                            i = tt * 2 + nblk
                            v.wait_ge(sm["pj"], b * 32 + i + 1)
                            v.tensor_tensor(vjunk[:, 0:ES],
                                            ps_q[:, (nblk % 2) * ES:(nblk % 2 + 1) * ES],
                                            borep[:, nblk * ES:(nblk + 1) * ES],
                                            ALU.add)
                            v.sig(sm["ye"], b * 32 + i + 1)
                            v.tensor_tensor(ytile[:, (tt % 2) * 1024 + nblk * ES:(tt % 2) * 1024 + (nblk + 1) * ES],
                                            vjunk[:, 0:ES],
                                            xtile[:, (tt % 2) * 1024 + nblk * ES:(tt % 2) * 1024 + (nblk + 1) * ES],
                                            ALU.add)
                            if nblk == 1:
                                v.sig(sm["yd"], k + 1)


def emit_l1_ffn(p, xin, xout, wg, w1T, w3T, w2T, tag="f1a"):
    nc = p.nc
    NT = NTOK // 128
    NT2 = NTOK // ES
    IC = 2048 // 128
    E = 8
    BIG = 30000.0

    iota_in = p.const_input(tag + "_iota", np.broadcast_to(np.arange(8, dtype=np.float32), (128, 8)).copy())
    eyeb_in = p.const_input(tag + "_eyeb", np.eye(128, dtype=BFDT))
    xT_h = nc.dram_tensor(tag + "_xT", [128, KC * NTOK], BF16).ap()
    w1c = nc.dram_tensor(tag + "_w1c", [E, 128, KC * 2048], BF16).ap()
    w3c = nc.dram_tensor(tag + "_w3c", [E, 128, KC * 2048], BF16).ap()
    w2c = nc.dram_tensor(tag + "_w2c", [E, 128, IC * 1024], BF16).ap()

    sm = {n: p.sem(tag + n) for n in
          ["w", "wc", "cvt", "x0", "x1", "vb", "sd", "lo", "tp", "xTc",
           "xw0", "xw1", "rle", "rtm", "rt", "sg", "ce", "xs0", "xs1",
           "wl0", "wl1", "wl2", "h1", "sl", "he", "yp", "ye", "yd",
           "st0", "st1", "vch", "sch"]}

    with ExitStack() as _es:
        def A(nm, shape, dt):
            return _es.enter_context(nc.sbuf_tensor(tag + nm, shape, dt))

        wgsb = A("wgsb", [128, KC * 24], F32)
        wgbf = A("wgbf", [128, KC * 24], BF16)
        iota = A("iota", [128, 8], F32)
        eyeb = A("eyeb", [128, 128], BF16)
        stg = A("stg", [128, 2048], BF16)
        xtile = A("xtile", [128, 2 * 1024], F32)
        stat = A("stat", [128, 16], F32)
        sjunk = A("sjunk", [128, 1024], F32)
        vjunk = A("vjunk", [128, 1024], F32)
        xln = A("xln", [128, 2 * 1024], BF16)
        xlo = A("xlo", [128, 2 * 1024], BF16)
        xTrow = A("xTrow", [128, 2 * 1024], BF16)
        xlorow = A("xlorow", [128, 1024], BF16)
        rtt = A("rtt", [128, 72], F32)
        ce_all = A("ce", [128, NT * 8], F32)
        xs = A("xs", [128, 2 * KC * ES], BF16)
        wb1 = A("wb1", [128, KC * 2048], BF16)
        wb3 = A("wb3", [128, KC * 2048], BF16)
        wb2 = A("wb2", [128, IC * 1024], BF16)
        hti = A("hti", [128, IC * ES], BF16)
        hs = A("hs", [128, 2 * ES], F32)
        yacc = A("yacc", [128, 4 * 1024], F32)
        ytile = A("ytile", [128, 2 * 1024], F32)
        grow2 = ytile
        ps_a = _es.enter_context(nc.psum_tensor(tag + "ps_a", [128, 2 * ES], F32))
        ps_b2 = _es.enter_context(nc.psum_tensor(tag + "ps_b", [128, 2 * ES], F32))
        ps_y = _es.enter_context(nc.psum_tensor(tag + "ps_y", [128, 2 * ES], F32))
        ps_tp = _es.enter_context(nc.psum_tensor(tag + "ps_tp", [128, 1024], BF16))
        ps_r = _es.enter_context(nc.psum_tensor(tag + "ps_r", [128, 512], F32))

        LG, V0, EQ0, INC, F0, L2, F1, V1, DD, W0, W1C, CEA, CEB = (
            0, 8, 9, 17, 25, 33, 41, 49, 50, 51, 52, 53, 61)

        with nc.Block(tag) as blk:
            @blk.sync
            def _(s):
                s.dma_start(out=iota[:, :], in_=iota_in[:, :]).then_inc(sm["w"], 16)
                s.dma_start(out=eyeb[:, :], in_=eyeb_in[:, :]).then_inc(sm["w"], 16)
                s.dma_start(out=wgsb.rearrange("q (kc j) -> q kc j", kc=KC)[:, :, 0:8],
                            in_=wg.rearrange("(kc q) e -> q kc e", q=128)).then_inc(sm["w"], 16)
                for tt in range(NT):
                    if tt >= 2:
                        s.wait_ge(sm["lo"], tt - 1)
                    s.dma_start(out=xtile[:, (tt % 2) * 1024:(tt % 2 + 1) * 1024],
                                in_=xin[tt * 128:(tt + 1) * 128, :]).then_inc(sm["x%d" % (tt % 2)], 16)
                    s.wait_ge(sm["xTc"], 2 * tt + 1)
                    s.dma_start(out=xT_h.rearrange("q (kc u) -> q kc u", kc=KC)[:, :, tt * 128:(tt + 1) * 128],
                                in_=xTrow[:, (tt % 2) * 1024:(tt % 2 + 1) * 1024].rearrange("q (c u) -> q c u", c=KC)
                                ).then_inc(sm["xw%d" % (tt % 2)], 16)
                s.wait_ge(sm["cvt"], 16 * 256)
                for t2 in range(NT2 + 1):
                    if t2 >= 1:
                        c2 = t2 - 1
                        for j in range(4):
                            tt = c2 * 4 + j
                            if tt < 2:
                                s.wait_ge(sm["lo"], NT)
                            else:
                                s.wait_ge(sm["yd"], tt - 1)
                            s.dma_start(out=xtile[:, (tt % 2) * 1024:(tt % 2 + 1) * 1024],
                                        in_=xin[tt * 128:(tt + 1) * 128, :]).then_inc(sm["x%d" % (tt % 2)], 16)
                            s.wait_ge(sm["yd"], tt + 1)
                            s.dma_start(out=xout[tt * 128:(tt + 1) * 128, :],
                                        in_=ytile[:, (tt % 2) * 1024:(tt % 2 + 1) * 1024]).then_inc(sm["st%d" % (tt % 2)], 16)
                    if t2 < NT2:
                        s.wait_ge(sm["xw0"], 16 * (NT // 2))
                        s.wait_ge(sm["xw1"], 16 * (NT // 2))
                        if t2 >= 2:
                            s.wait_ge(sm["he"], ((t2 - 2) * 8 + 8) * IC)
                        s.dma_start(out=xs[:, (t2 % 2) * KC * ES:(t2 % 2 + 1) * KC * ES].rearrange("q (kc j) -> q kc j", kc=KC),
                                    in_=xT_h.rearrange("q (kc u) -> q kc u", kc=KC)[:, :, t2 * ES:(t2 + 1) * ES]
                                    ).then_inc(sm["xs%d" % (t2 % 2)], 16)
                        for e in range(E):
                            g = t2 * 8 + e
                            if g >= 1:
                                s.wait_ge(sm["he"], g * IC)
                            s.dma_start(out=wb1[:, :], in_=w1c[e, :, :]).then_inc(sm["wl0"], 16)
                            s.dma_start(out=wb3[:, :], in_=w3c[e, :, :]).then_inc(sm["wl1"], 16)
                            if g >= 1:
                                s.wait_ge(sm["ye"], g * 8)
                            s.dma_start(out=wb2[:, :], in_=w2c[e, :, :]).then_inc(sm["wl2"], 16)
                s.wait_ge(sm["st0"], 16 * (NT // 2))
                s.wait_ge(sm["st1"], 16 * (NT // 2))

            @blk.gpsimd
            def _(g):
                cv = 0
                for e in range(E):
                    for (dst, srcT) in ((w1c, w1T), (w3c, w3T)):
                        for kc in range(KC):
                            if cv > 0:
                                g.wait_ge(sm["cvt"], cv * 16)
                            g.dma_start(out=stg[:, :],
                                        in_=srcT[e, kc * 128:(kc + 1) * 128, :]).then_inc(sm["wc"], 16)
                            cv += 1
                            g.wait_ge(sm["wc"], cv * 16)
                            g.dma_start(out=dst[e, :, kc * 2048:(kc + 1) * 2048],
                                        in_=stg[:, :]).then_inc(sm["cvt"], 16)
                    for ic in range(IC):
                        g.wait_ge(sm["cvt"], cv * 16)
                        g.dma_start(out=stg[:, 0:1024],
                                    in_=w2T[e, ic * 128:(ic + 1) * 128, :]).then_inc(sm["wc"], 16)
                        cv += 1
                        g.wait_ge(sm["wc"], cv * 16)
                        g.dma_start(out=w2c[e, :, ic * 1024:(ic + 1) * 1024],
                                    in_=stg[:, 0:1024]).then_inc(sm["cvt"], 16)
                g.wait_ge(sm["cvt"], cv * 16)

            @blk.tensor
            def _(t):
                t.wait_ge(sm["w"], 16 * 3)
                t.wait_ge(sm["rt"], 1)
                for tt in range(NT):
                    t.wait_ge(sm["lo"], tt + 1)
                    if tt >= 1:
                        t.wait_ge(sm["xTc"], 2 * tt)
                    for kc in range(KC):
                        ins = t.transpose(ps_tp[:, kc * 128:(kc + 1) * 128],
                                          xln[:, (tt % 2) * 1024 + kc * 128:(tt % 2) * 1024 + (kc + 1) * 128],
                                          eyeb[:, :])
                    ins.then_inc(sm["tp"], 1)
                    t.wait_ge(sm["xTc"], 2 * tt + 1)
                    for kc in range(KC):
                        ins = t.transpose(ps_tp[:, kc * 128:(kc + 1) * 128],
                                          xlo[:, (tt % 2) * 1024 + kc * 128:(tt % 2) * 1024 + (kc + 1) * 128],
                                          eyeb[:, :])
                    ins.then_inc(sm["tp"], 1)
                    t.wait_ge(sm["xTc"], 2 * tt + 2)
                    if tt >= 1:
                        t.wait_ge(sm["rle"], tt)
                    xr = xTrow[:, (tt % 2) * 1024:(tt % 2 + 1) * 1024]
                    for kc in range(KC):
                        t.matmul(ps_r[:, 0:8], xr[:, kc * 128:(kc + 1) * 128],
                                 wgbf[:, kc * 24 + 0:kc * 24 + 8], start=(kc == 0), stop=False)
                        t.matmul(ps_r[:, 0:8], xr[:, kc * 128:(kc + 1) * 128],
                                 wgbf[:, kc * 24 + 8:kc * 24 + 16], start=False, stop=False)
                        ins = t.matmul(ps_r[:, 0:8], xlorow[:, kc * 128:(kc + 1) * 128],
                                       wgbf[:, kc * 24 + 16:kc * 24 + 24], start=False,
                                       stop=(kc == KC - 1))
                    ins.then_inc(sm["rtm"], 1)
                for t2 in range(NT2):
                    t.wait_ge(sm["xs%d" % (t2 % 2)], 16 * (t2 // 2 + 1))
                    xsl = xs[:, (t2 % 2) * KC * ES:(t2 % 2 + 1) * KC * ES].rearrange("q (kc j) -> q kc j", kc=KC)
                    for e in range(E):
                        g = t2 * 8 + e
                        t.wait_ge(sm["wl0"], 16 * (g + 1))
                        t.wait_ge(sm["wl1"], 16 * (g + 1))
                        for it in range(IC):
                            hcnt = g * IC + it
                            if hcnt >= 2:
                                t.wait_ge(sm["he"], hcnt - 1)
                            for kc in range(KC):
                                t.matmul(ps_a[:, (hcnt % 2) * ES:(hcnt % 2 + 1) * ES],
                                         wb1[:, kc * 2048 + it * 128:kc * 2048 + (it + 1) * 128],
                                         xsl[:, kc, :], start=(kc == 0), stop=(kc == KC - 1))
                            for kc in range(KC):
                                ins = t.matmul(ps_b2[:, (hcnt % 2) * ES:(hcnt % 2 + 1) * ES],
                                               wb3[:, kc * 2048 + it * 128:kc * 2048 + (it + 1) * 128],
                                               xsl[:, kc, :], start=(kc == 0), stop=(kc == KC - 1))
                            ins.then_inc(sm["h1"], 1)
                        t.wait_ge(sm["wl2"], 16 * (g + 1))
                        t.wait_ge(sm["he"], (g + 1) * IC)
                        for j in range(4):
                            for hc in range(2):
                                ycnt = g * 8 + j * 2 + hc
                                if ycnt >= 2:
                                    t.wait_ge(sm["ye"], ycnt - 1)
                                for ic in range(IC):
                                    ins = t.matmul(ps_y[:, (ycnt % 2) * ES:(ycnt % 2 + 1) * ES],
                                                   hti[:, ic * ES + j * 128:ic * ES + (j + 1) * 128],
                                                   wb2[:, ic * 1024 + hc * ES:ic * 1024 + (hc + 1) * ES],
                                                   start=(ic == 0), stop=(ic == IC - 1))
                                ins.then_inc(sm["yp"], 1)

            @blk.scalar
            def _(sc_raw):
                sc = SelfSync(sc_raw, sm["sch"])
                for tt in range(NT):
                    sc.wait_ge(sm["x%d" % (tt % 2)], 16 * (tt // 2 + 1))
                    xa = xtile[:, (tt % 2) * 1024:(tt % 2 + 1) * 1024]
                    st8 = (tt % 2) * 8
                    sc.activation(sjunk[:, :], xa, AF.Square,
                                  accum_out=stat[:, st8 + 1:st8 + 2])
                    sc.wait_ge(sm["vb"], tt + 1)
                    sc.activation(stat[:, st8 + 3:st8 + 4],
                                  stat[:, st8 + 1:st8 + 2], AF.Sqrt,
                                  bias=stat[:, st8 + 2:st8 + 3], scale=1.0 / 1024.0)
                    sc.sig(sm["sd"], tt + 1)
                    sc.wait_ge(sm["rt"], tt + 2)
                    sc.activation(rtt[:, W0:W0 + 1], rtt[:, DD:DD + 1], AF.Sigmoid,
                                  bias=0.0, scale=1.0)
                    sc.sig(sm["sg"], tt + 1)
                for t2 in range(NT2):
                    for e in range(E):
                        g = t2 * 8 + e
                        for it in range(IC):
                            hcnt = g * IC + it
                            sc.wait_ge(sm["h1"], hcnt + 1)
                            if hcnt >= 2:
                                sc.wait_ge(sm["he"], hcnt - 1)
                            sc.activation(hs[:, (hcnt % 2) * ES:(hcnt % 2 + 1) * ES],
                                          ps_a[:, (hcnt % 2) * ES:(hcnt % 2 + 1) * ES],
                                          AF.Silu, bias=0.0, scale=1.0)
                            sc.sig(sm["sl"], hcnt + 1)

            @blk.vector
            def _(v_raw):
                v = SelfSync(v_raw, sm["vch"])
                v.wait_ge(sm["w"], 16 * 3)
                wgv = wgsb.rearrange("q (kc j) -> q kc j", kc=KC)
                wgbv = wgbf.rearrange("q (kc j) -> q kc j", kc=KC)
                v.tensor_copy(wgbv[:, :, 0:8], wgv[:, :, 0:8])
                v.tensor_copy(wgv[:, :, 8:16], wgbv[:, :, 0:8])
                v.tensor_tensor(xlo[:, 0:KC * 8].rearrange("q (kc j) -> q kc j", kc=KC),
                                wgv[:, :, 0:8], wgv[:, :, 8:16], ALU.subtract)
                v.tensor_copy(wgbv[:, :, 8:16], xlo[:, 0:KC * 8].rearrange("q (kc j) -> q kc j", kc=KC))
                v.tensor_copy(wgbv[:, :, 16:24], wgbv[:, :, 0:8])
                v.sig(sm["rt"], 1)
                for tt in range(NT):
                    xa = xtile[:, (tt % 2) * 1024:(tt % 2 + 1) * 1024]
                    st8 = (tt % 2) * 8
                    v.wait_ge(sm["x%d" % (tt % 2)], 16 * (tt // 2 + 1))
                    v.tensor_scalar(vjunk[:, :], xa, 1.0, None, ALU.mult, ALU.add,
                                    accum_out=stat[:, st8 + 0:st8 + 1])
                    v.tensor_scalar(stat[:, st8 + 4:st8 + 5], stat[:, st8 + 0:st8 + 1],
                                    1.0 / 1024.0, None, ALU.mult)
                    v.tensor_tensor(stat[:, st8 + 5:st8 + 6], stat[:, st8 + 4:st8 + 5],
                                    stat[:, st8 + 4:st8 + 5], ALU.mult)
                    v.tensor_scalar(stat[:, st8 + 2:st8 + 3], stat[:, st8 + 5:st8 + 6],
                                    -1.0, 1e-5, ALU.mult, ALU.add)
                    v.sig(sm["vb"], tt + 1)
                    v.wait_ge(sm["sd"], tt + 1)
                    v.reciprocal(stat[:, st8 + 6:st8 + 7], stat[:, st8 + 3:st8 + 4])
                    v.tensor_scalar(vjunk[:, :], xa, stat[:, st8 + 4:st8 + 5],
                                    stat[:, st8 + 6:st8 + 7], ALU.subtract, ALU.mult)
                    v.tensor_copy(xln[:, (tt % 2) * 1024:(tt % 2 + 1) * 1024], vjunk[:, :])
                    v.tensor_tensor(xlo[:, (tt % 2) * 1024:(tt % 2 + 1) * 1024], vjunk[:, :],
                                    xln[:, (tt % 2) * 1024:(tt % 2 + 1) * 1024], ALU.subtract)
                    v.sig(sm["lo"], tt + 1)
                    v.wait_ge(sm["tp"], 2 * tt + 1)
                    if tt >= 2:
                        v.wait_ge(sm["xw%d" % (tt % 2)], 16 * (tt // 2))
                    v.tensor_copy(xTrow[:, (tt % 2) * 1024:(tt % 2 + 1) * 1024].rearrange("q (c u) -> q c u", c=KC),
                                  ps_tp[:, :].rearrange("q (c u) -> q c u", c=KC))
                    v.sig(sm["xTc"], 2 * tt + 1)
                    v.wait_ge(sm["tp"], 2 * tt + 2)
                    v.tensor_copy(xlorow[:, :].rearrange("q (c u) -> q c u", c=KC),
                                  ps_tp[:, :].rearrange("q (c u) -> q c u", c=KC))
                    v.sig(sm["xTc"], 2 * tt + 2)
                    v.wait_ge(sm["rtm"], tt + 1)
                    v.tensor_copy(rtt[:, LG:LG + 8], ps_r[:, 0:8])
                    v.sig(sm["rle"], tt + 1)
                    v.reduce_max(rtt[:, V0:V0 + 1], rtt[:, LG:LG + 8], mybir.AxisListType.X)
                    v.tensor_scalar(rtt[:, EQ0:EQ0 + 8], rtt[:, LG:LG + 8],
                                    rtt[:, V0:V0 + 1], None, ALU.is_equal)
                    v.tensor_tensor_scan(rtt[:, INC:INC + 8], rtt[:, EQ0:EQ0 + 8],
                                         rtt[:, EQ0:EQ0 + 8], 0.0, ALU.add, ALU.max)
                    v.tensor_scalar(rtt[:, F0:F0 + 8], rtt[:, INC:INC + 8], 1.0, None, ALU.is_equal)
                    v.tensor_tensor(rtt[:, F0:F0 + 8], rtt[:, F0:F0 + 8],
                                    rtt[:, EQ0:EQ0 + 8], ALU.mult)
                    v.tensor_scalar(rtt[:, L2:L2 + 8], rtt[:, F0:F0 + 8], -BIG, None, ALU.mult)
                    v.tensor_tensor(rtt[:, L2:L2 + 8], rtt[:, L2:L2 + 8],
                                    rtt[:, LG:LG + 8], ALU.add)
                    v.reduce_max(rtt[:, V1:V1 + 1], rtt[:, L2:L2 + 8], mybir.AxisListType.X)
                    v.tensor_scalar(rtt[:, F1:F1 + 8], rtt[:, L2:L2 + 8],
                                    rtt[:, V1:V1 + 1], None, ALU.is_equal)
                    v.tensor_tensor_scan(rtt[:, INC:INC + 8], rtt[:, F1:F1 + 8],
                                         rtt[:, F1:F1 + 8], 0.0, ALU.add, ALU.max)
                    v.tensor_scalar(rtt[:, EQ0:EQ0 + 8], rtt[:, INC:INC + 8], 1.0, None, ALU.is_equal)
                    v.tensor_tensor(rtt[:, F1:F1 + 8], rtt[:, F1:F1 + 8],
                                    rtt[:, EQ0:EQ0 + 8], ALU.mult)
                    v.tensor_tensor(rtt[:, DD:DD + 1], rtt[:, V0:V0 + 1],
                                    rtt[:, V1:V1 + 1], ALU.subtract)
                    v.sig(sm["rt"], tt + 2)
                    v.wait_ge(sm["sg"], tt + 1)
                    v.tensor_scalar(rtt[:, W1C:W1C + 1], rtt[:, W0:W0 + 1], -1.0, 1.0,
                                    ALU.mult, ALU.add)
                    v.tensor_scalar(rtt[:, CEA:CEA + 8], rtt[:, F0:F0 + 8],
                                    rtt[:, W0:W0 + 1], None, ALU.mult)
                    v.tensor_scalar(rtt[:, CEB:CEB + 8], rtt[:, F1:F1 + 8],
                                    rtt[:, W1C:W1C + 1], None, ALU.mult)
                    v.tensor_tensor(ce_all[:, tt * 8:(tt + 1) * 8], rtt[:, CEA:CEA + 8],
                                    rtt[:, CEB:CEB + 8], ALU.add)
                    v.sig(sm["ce"], tt + 1)
                for t2 in range(NT2):
                    for e in range(E):
                        g = t2 * 8 + e
                        for it in range(IC):
                            hcnt = g * IC + it
                            v.wait_ge(sm["sl"], hcnt + 1)
                            if it == 0 and g >= 1:
                                v.wait_ge(sm["yp"], g * 8)
                            v.tensor_tensor(hti[:, it * ES:(it + 1) * ES],
                                            hs[:, (hcnt % 2) * ES:(hcnt % 2 + 1) * ES],
                                            ps_b2[:, (hcnt % 2) * ES:(hcnt % 2 + 1) * ES],
                                            ALU.mult)
                            v.sig(sm["he"], hcnt + 1)
                        for j in range(4):
                            tt = t2 * 4 + j
                            for hc in range(2):
                                ycnt = g * 8 + j * 2 + hc
                                v.wait_ge(sm["yp"], ycnt + 1)
                                v.wait_ge(sm["ce"], tt + 1)
                                cecol = ce_all[:, tt * 8 + e:tt * 8 + e + 1]
                                ya = yacc[:, j * 1024 + hc * ES:j * 1024 + (hc + 1) * ES]
                                if e == 0:
                                    v.tensor_scalar(ya, ps_y[:, (ycnt % 2) * ES:(ycnt % 2 + 1) * ES],
                                                    cecol, None, ALU.mult)
                                else:
                                    v.scalar_tensor_tensor(ya, ps_y[:, (ycnt % 2) * ES:(ycnt % 2 + 1) * ES],
                                                           cecol, ya, ALU.mult, ALU.add)
                                v.sig(sm["ye"], ycnt + 1)
                    for j in range(4):
                        tt = t2 * 4 + j
                        v.wait_ge(sm["x%d" % (tt % 2)], 16 * (NT // 2 + tt // 2 + 1))
                        if tt >= 2:
                            v.wait_ge(sm["st%d" % (tt % 2)], 16 * (tt // 2))
                        for hc in range(2):
                            v.tensor_tensor(ytile[:, (tt % 2) * 1024 + hc * ES:(tt % 2) * 1024 + (hc + 1) * ES],
                                            yacc[:, j * 1024 + hc * ES:j * 1024 + (hc + 1) * ES],
                                            xtile[:, (tt % 2) * 1024 + hc * ES:(tt % 2) * 1024 + (hc + 1) * ES],
                                            ALU.add)
                        v.sig(sm["yd"], tt + 1)


def emit_l0_ffn(p, xin, xout, g1, g2, wq, wk, wv, wo, router_w, router_b,
                w1, w2, w3, cosr, sinr, cmask_row, pmask_col, tag="f0a",
                dbg_h=None, cc_mock=False):
    nc = p.nc
    M_STAB = 32.0
    SCALE = 0.125
    NT = NTOK // 128
    NT2 = NTOK // ES
    IC = 2048 // 128
    E = 8
    BIG = 30000.0
    CAP = 8192.0

    iota_in = p.const_input(tag + "_iota", np.broadcast_to(np.arange(8, dtype=np.float32), (128, 8)).copy())
    eyeb_in = p.const_input(tag + "_eyeb", np.eye(128, dtype=BFDT))
    onesc_in = p.const_input(tag + "_onesc", np.ones((128, 1), np.float32))
    trilm_in = p.const_input(tag + "_trilm", np.tril(np.ones((128, 128), BFDT)))
    masks = np.zeros((4, 128, ES), BFDT)
    sl_ = np.arange(128)[:, None]
    tl_ = np.arange(ES)[None, :]
    for j in range(4):
        masks[j] = (128 * j + sl_ <= tl_).astype(BFDT)
    mask_in = p.const_input(tag + "_masks", np.ascontiguousarray(masks.transpose(1, 0, 2).reshape(128, 4 * ES)))

    ones64_in = p.const_input(tag + "_ones", np.ones((64, 128), np.float32))
    eyef_in = p.const_input(tag + "_eyef", np.eye(128, dtype=np.float32))
    xln_h = nc.dram_tensor(tag + "_xlnh", [NTOK, D], F32).ap()
    h_h = dbg_h if dbg_h is not None else nc.dram_tensor(tag + "_hh", [NTOK, D], F32).ap()
    xf_h = nc.dram_tensor(tag + "_xfh", [NTOK, D], F32).ap()
    xT2_h = nc.dram_tensor(tag + "_xT2", [128, KC * NTOK], BF16).ap()
    w1c = nc.dram_tensor(tag + "_w1c", [E, 128, KC * 2048], BF16).ap()
    w2c = nc.dram_tensor(tag + "_w2c", [E, 128, KC * 2048], BF16).ap()
    w3c = nc.dram_tensor(tag + "_w3c", [E, 128, IC * 1024], BF16).ap()
    cc_in = nc.dram_tensor(tag + "_ccin", [8, 16], F32).ap()
    cc_out = nc.dram_tensor(tag + "_ccout", [8, 16], F32).ap()

    sm = {n: p.sem(tag + n) for n in
          ["w", "wc", "cvt", "x0", "x1", "vb", "sd", "ln", "lnw0", "lnw1",
           "tp", "xT", "qk", "qe", "vn", "ve", "rot", "sc", "pf", "pm",
           "av", "li", "rp", "rpc", "oe", "pj", "yd", "ye", "st0", "st1",
           "hx0", "hx1", "hvb", "hsd", "hlo", "htp", "hxc", "hxw0", "hxw1",
           "rs", "cc", "ccd", "co",
           "xs0", "xs1", "wl0", "wl1", "wl2", "h1", "slh", "he", "yp",
           "fy", "fx0", "fx1", "fh0", "fh1", "ff0", "ff1", "vch", "sch",
           "vch2", "sch2", "rrow", "gh",
           "Msc", "Mpf", "Mpm", "Mav", "Mli", "Mrp", "Moe", "Mce", "Mve",
           "Mhtp", "Mhxc", "Myd", "Mye", "Mrle", "Mrt", "Msg", "Mrtm"]}

    fl_tiles = [list(range(0, 4 * tc + 4)) for tc in range(NCH)]
    fl_off = [sum(len(fl_tiles[c]) for c in range(tc)) for tc in range(NCH)]
    TPB = sum(len(x) for x in fl_tiles)
    cpool_ap, cpool_in = p.reg_const(tag, [-M_STAB, 1e-6])

    with ExitStack() as _es:
        def A(nm, shape, dt):
            return _es.enter_context(nc.sbuf_tensor(tag + nm, shape, dt))

        eyeb = A("eyeb", [128, 128], BF16)
        g1rep = A("g1rep", [128, 1024], F32)
        grow = A("grow", [64, 1024], F32)
        onesr = A("onesr", [64, 128], F32)
        wqkv = A("wqkv", [128, KC * 1536], BF16)
        wosb = A("wosb", [128, KC * 1024], BF16)
        stg = A("stgA", [128, 2048], BF16)
        xtile = A("xtileA", [128, 2 * 1024], F32)
        stat = A("statA", [128, 16], F32)
        sjunk = A("sjunkA", [128, 1024], F32)
        vjunk = A("vjunkA", [128, 1024], F32)
        vrot = A("vrotA", [128, 1024], F32)
        xlnrow = A("xlnrow", [128, 2 * 1024], F32)
        xrm = A("xrm", [128, 2 * 1024], BF16)
        rote = A("rote", [128, 2 * 2048], BF16)
        qn = A("qn", [128, 2 * 1536], BF16)
        qr = A("qr", [128, 2 * 1536], BF16)
        xT = A("xTA", [128, KC * T], BF16)
        qkT = A("qkTA", [128, 10 * T], BF16)
        vb = A("vbA", [128, NTT * 260], BF16)
        ptile = A("ptA", [128, 2 * ES], BF16)
        repsb = A("repsbA", [64, 2 * ES], F32)
        linv = grow
        maskt = A("masktA", [128, 4 * ES], BF16)
        ytile = A("ytileA", [128, 2 * 1024], F32)
        ofm = xT
        ps_s = _es.enter_context(nc.psum_tensor(tag + "ps_s", [128, 2 * ES], F32))
        ps_o = _es.enter_context(nc.psum_tensor(tag + "ps_o", [128, 2 * ES], F32))
        ps_q = _es.enter_context(nc.psum_tensor(tag + "ps_q", [128, 2 * ES], F32))
        ps_r = _es.enter_context(nc.psum_tensor(tag + "ps_r", [128, ES], F32))
        ps_tp = _es.enter_context(nc.psum_tensor(tag + "ps_tp", [128, 1024], BF16))

        with nc.Block(tag + "A") as blk:
            @blk.sync
            def _(s):
                s.dma_start(out=eyeb[:, :], in_=eyeb_in[:, :]).then_inc(sm["w"], 16)
                s.dma_start(out=maskt[:, :], in_=mask_in[:, :]).then_inc(sm["w"], 16)
                s.dma_start(out=grow[0:1, :], in_=g1.rearrange("(a q) -> a q", a=1)).then_inc(sm["w"], 16)
                s.dma_start(out=onesr[:, :], in_=ones64_in[:, :]).then_inc(sm["w"], 16)
                s.dma_start(out=cpool_ap[:, :], in_=cpool_in[:, :]).then_inc(sm["w"], 16)
                for b in range(BLOC):
                    t0 = b * T
                    for tt in range(NTT):
                        k = b * NTT + tt
                        if k >= 2:
                            s.wait_ge(sm["ln"], k - 1)
                        if b > 0 and tt < 2:
                            s.wait_ge(sm["yd"], b * NTT)
                        s.dma_start(out=xtile[:, (tt % 2) * 1024:(tt % 2 + 1) * 1024],
                                    in_=xin[t0 + tt * 128:t0 + (tt + 1) * 128, :]).then_inc(sm["x%d" % (tt % 2)], 16)
                        if k >= 2:
                            s.wait_ge(sm["rot"], k)
                        s.dma_start(out=rote[:, (tt % 2) * 2048:(tt % 2) * 2048 + 1024],
                                    in_=cosr[tt * 128:(tt + 1) * 128, :]).then_inc(sm["x%d" % (tt % 2)], 16)
                        s.dma_start(out=rote[:, (tt % 2) * 2048 + 1024:(tt % 2 + 1) * 2048],
                                    in_=sinr[tt * 128:(tt + 1) * 128, :]).then_inc(sm["x%d" % (tt % 2)], 16)
                        s.wait_ge(sm["ln"], k + 1)
                        s.dma_start(out=xln_h[t0 + tt * 128:t0 + (tt + 1) * 128, :],
                                    in_=xlnrow[:, (tt % 2) * 1024:(tt % 2 + 1) * 1024]).then_inc(sm["lnw%d" % (tt % 2)], 16)
                    for tt in range(NTT):
                        k = b * NTT + tt
                        if tt < 2:
                            s.wait_ge(sm["ln"], (b + 1) * NTT)
                        else:
                            s.wait_ge(sm["yd"], k - 1)
                        s.wait_ge(sm["lnw%d" % (tt % 2)], 16 * (b * 8 + tt // 2 + 1))
                        s.dma_start(out=xtile[:, (tt % 2) * 1024:(tt % 2 + 1) * 1024],
                                    in_=xln_h[t0 + tt * 128:t0 + (tt + 1) * 128, :]).then_inc(sm["x%d" % (tt % 2)], 16)
                        if tt >= 2:
                            st_t = tt - 2
                            s.wait_ge(sm["yd"], b * NTT + st_t + 1)
                            s.dma_start(out=h_h[t0 + st_t * 128:t0 + (st_t + 1) * 128, :],
                                        in_=ytile[:, (st_t % 2) * 1024:(st_t % 2 + 1) * 1024]).then_inc(sm["st%d" % (st_t % 2)], 16)
                    for st_t in (NTT - 2, NTT - 1):
                        s.wait_ge(sm["yd"], b * NTT + st_t + 1)
                        s.dma_start(out=h_h[t0 + st_t * 128:t0 + (st_t + 1) * 128, :],
                                    in_=ytile[:, (st_t % 2) * 1024:(st_t % 2 + 1) * 1024]).then_inc(sm["st%d" % (st_t % 2)], 16)
                s.wait_ge(sm["st0"], 16 * (BLOC * NTT // 2))
                s.wait_ge(sm["st1"], 16 * (BLOC * NTT // 2))

            @blk.gpsimd
            def _(g):
                cv = 0
                for kc in range(KC):
                    if cv:
                        g.wait_ge(sm["cvt"], cv * 16)
                    g.dma_start(out=wqkv[:, kc * 1536:kc * 1536 + 1024],
                                in_=wq[kc * 128:(kc + 1) * 128, :]).then_inc(sm["cvt"], 16)
                    cv += 1
                    g.dma_start(out=wqkv[:, kc * 1536 + 1024:kc * 1536 + 1280],
                                in_=wk[kc * 128:(kc + 1) * 128, :]).then_inc(sm["cvt"], 16)
                    cv += 1
                    g.dma_start(out=wqkv[:, kc * 1536 + 1280:kc * 1536 + 1536],
                                in_=wv[kc * 128:(kc + 1) * 128, :]).then_inc(sm["cvt"], 16)
                    cv += 1
                    g.dma_start(out=wosb[:, kc * 1024:(kc + 1) * 1024],
                                in_=wo[kc * 128:(kc + 1) * 128, :]).then_inc(sm["cvt"], 16)
                    cv += 1
                g.wait_ge(sm["cvt"], cv * 16)

            @blk.tensor
            def _(t):
                t.wait_ge(sm["w"], 16 * 5)
                t.matmul(ps_q[:, 0:ES], onesr[0:1, :], grow[0:1, 0:ES], start=True, stop=True)
                t.matmul(ps_q[:, ES:2 * ES], onesr[0:1, :], grow[0:1, ES:2 * ES], start=True, stop=True).then_inc(sm["rot"], 1)
                t.wait_ge(sm["rp"], 1)
                t.wait_ge(sm["cvt"], 16 * 32)
                for b in range(BLOC):
                    for tt in range(NTT):
                        k = b * NTT + tt
                        t.wait_ge(sm["vb"], 2 * k + 2)
                        if k >= 1:
                            t.wait_ge(sm["hxc"], 2 * k)
                        for kc in range(KC):
                            ins = t.transpose(ps_tp[:, kc * 128:(kc + 1) * 128],
                                              xrm[:, (tt % 2) * 1024 + kc * 128:(tt % 2) * 1024 + (kc + 1) * 128],
                                              eyeb[:, :])
                        ins.then_inc(sm["tp"], 1)
                        t.wait_ge(sm["xT"], k + 1)
                        for nc3 in range(3):
                            i3 = k * 3 + nc3
                            if i3 >= 2:
                                t.wait_ge(sm["qe"], i3 - 1)
                            for kc in range(KC):
                                ins = t.matmul(ps_q[:, (i3 % 2) * ES:(i3 % 2 + 1) * ES],
                                               xT[:, kc * T + tt * 128:kc * T + (tt + 1) * 128],
                                               wqkv[:, kc * 1536 + nc3 * ES:kc * 1536 + (nc3 + 1) * ES],
                                               start=(kc == 0), stop=(kc == KC - 1))
                            ins.then_inc(sm["qk"], 1)
                        t.wait_ge(sm["rot"], k + 2)
                        t.wait_ge(sm["xT"], k + 1)
                        for mi in range(8):
                            ins = t.transpose(ps_tp[:, mi * 128:(mi + 1) * 128],
                                              qr[:, (tt % 2) * 1536 + mi * 128:(tt % 2) * 1536 + (mi + 1) * 128],
                                              eyeb[:, :])
                        ins.then_inc(sm["htp"], 1)
                        t.wait_ge(sm["hxc"], 2 * k + 1)
                        for mi in range(8, 10):
                            ins = t.transpose(ps_tp[:, (mi - 8) * 128:(mi - 7) * 128],
                                              qr[:, (tt % 2) * 1536 + mi * 128:(tt % 2) * 1536 + (mi + 1) * 128],
                                              eyeb[:, :])
                        ins.then_inc(sm["htp"], 1)
                    t.wait_ge(sm["ve"], (b + 1) * NTT)
                    t.wait_ge(sm["hxc"], 2 * (b + 1) * NTT)
                    for h in range(16):
                        pb = (h % 2) * 64
                        mq = h // 2
                        mk = 8 + (h % 4) // 2
                        pk = ((h % 4) % 2) * 64
                        for tc in range(NCH):
                            tiles = fl_tiles[tc]
                            c_idx = b * 64 + h * NCH + tc
                            opar = c_idx % 2
                            for i, tt in enumerate(tiles):
                                gi = (b * 16 + h) * TPB + fl_off[tc] + i
                                if gi >= 2:
                                    t.wait_ge(sm["pf"], gi - 1)
                                t.matmul(ps_s[:, (gi % 2) * ES:(gi % 2 + 1) * ES],
                                         qkT[pk:pk + 64, mk * T + tt * 128:mk * T + (tt + 1) * 128],
                                         qkT[pb:pb + 64, mq * T + tc * ES:mq * T + (tc + 1) * ES],
                                         start=True, stop=True).then_inc(sm["sc"], 1)
                                t.wait_ge(sm["pm"], gi + 1)
                                if i == 0 and c_idx >= 2:
                                    t.wait_ge(sm["oe"], c_idx - 1)
                                t.matmul(ps_o[0:65, opar * ES:opar * ES + ES],
                                         vb[:, tt * 260 + (h % 4) * 65:tt * 260 + (h % 4) * 65 + 65],
                                         ptile[:, (gi % 2) * ES:(gi % 2 + 1) * ES],
                                         start=(i == 0), stop=(i == len(tiles) - 1)).then_inc(sm["av"], 1)
                            t.wait_ge(sm["li"], c_idx + 1)
                            if c_idx >= 1:
                                t.wait_ge(sm["rpc"], c_idx)
                            t.matmul(ps_r[0:64, :], onesr[0:1, 0:64],
                                     linv[0:1, opar * ES:(opar + 1) * ES],
                                     start=True, stop=True).then_inc(sm["rp"], 1)
                    t.wait_ge(sm["oe"], (b + 1) * 64)
                    for tt in range(NTT):
                        for nchunk in range(2):
                            pjc = tt * 2 + nchunk
                            if pjc >= 2:
                                t.wait_ge(sm["ye" if nchunk == 0 else "yd"], b * NTT + tt)
                            for kc in range(KC):
                                ins = t.matmul(ps_q[:, (pjc % 2) * ES:(pjc % 2 + 1) * ES],
                                               ofm[:, kc * T + tt * 128:kc * T + (tt + 1) * 128],
                                               wosb[:, kc * 1024 + nchunk * ES:kc * 1024 + (nchunk + 1) * ES],
                                               start=(kc == 0), stop=(kc == KC - 1))
                            ins.then_inc(sm["pj"], 1)

            @blk.scalar
            def _(sc_raw):
                sc = SelfSync(sc_raw, sm["sch"])
                sc.wait_ge(sm["w"], 16 * 5)
                for b in range(BLOC):
                    for tt in range(NTT):
                        k = b * NTT + tt
                        sc.wait_ge(sm["x%d" % (tt % 2)], 16 * (32 * b + 3 * (tt // 2) + 3))
                        xa = xtile[:, (tt % 2) * 1024:(tt % 2 + 1) * 1024]
                        st8 = (tt % 2) * 8
                        sc.activation(sjunk[:, :], xa, AF.Square,
                                      accum_out=stat[:, st8 + 1:st8 + 2])
                        sc.wait_ge(sm["vb"], 2 * k + 1)
                        sc.activation(stat[:, st8 + 3:st8 + 4],
                                      stat[:, st8 + 1:st8 + 2], AF.Sqrt,
                                      bias=stat[:, st8 + 2:st8 + 3], scale=1.0 / 1024.0)
                        sc.sig(sm["sd"], 2 * k + 1)
                        sc.wait_ge(sm["ln"], k + 1)
                        sc.activation(sjunk[:, :], xlnrow[:, (tt % 2) * 1024:(tt % 2 + 1) * 1024],
                                      AF.Square, accum_out=stat[:, st8 + 5:st8 + 6])
                        sc.activation(stat[:, st8 + 6:st8 + 7],
                                      stat[:, st8 + 5:st8 + 6], AF.Sqrt,
                                      bias=1e-6, scale=1.0 / 1024.0)
                        sc.sig(sm["sd"], 2 * k + 2)
                    for h in range(16):
                        for tc in range(NCH):
                            tiles = fl_tiles[tc]
                            for i, tt in enumerate(tiles):
                                gi = (b * 16 + h) * TPB + fl_off[tc] + i
                                if gi >= 2:
                                    sc.wait_ge(sm["av"], gi - 1)
                                sc.wait_ge(sm["sc"], gi + 1)
                                sc.activation(ptile[:, (gi % 2) * ES:(gi % 2 + 1) * ES],
                                              ps_s[:, (gi % 2) * ES:(gi % 2 + 1) * ES], AF.Exp,
                                              bias=-M_STAB, scale=SCALE)
                                sc.sig(sm["pf"], gi + 1)

            @blk.vector
            def _(v_raw):
                v = SelfSync(v_raw, sm["vch"])
                v.wait_ge(sm["rot"], 1)
                v.tensor_copy(g1rep[:, 0:ES], ps_q[:, 0:ES])
                v.tensor_copy(g1rep[:, ES:2 * ES], ps_q[:, ES:2 * ES])
                v.memset(vb.rearrange("p (u h j) -> p u h j", u=NTT, h=4)[:, :, :, 64:65], 1.0)
                v.sig(sm["rp"], 1)
                for b in range(BLOC):
                    for tt in range(NTT):
                        k = b * NTT + tt
                        xa = xtile[:, (tt % 2) * 1024:(tt % 2 + 1) * 1024]
                        st8 = (tt % 2) * 8
                        if tt == 0 and b >= 1:
                            v.wait_ge(sm["av"], b * 16 * TPB)
                        v.wait_ge(sm["x%d" % (tt % 2)], 16 * (32 * b + 3 * (tt // 2) + 3))
                        if k >= 2:
                            v.wait_ge(sm["lnw%d" % (tt % 2)], 16 * (b * 8 + tt // 2))
                        v.tensor_scalar(vjunk[:, :], xa, 1.0, None, ALU.mult, ALU.add,
                                        accum_out=stat[:, st8 + 0:st8 + 1])
                        v.tensor_scalar(stat[:, st8 + 4:st8 + 5], stat[:, st8 + 0:st8 + 1],
                                        1.0 / 1024.0, None, ALU.mult)
                        v.tensor_tensor(stat[:, st8 + 7:st8 + 8], stat[:, st8 + 4:st8 + 5],
                                        stat[:, st8 + 4:st8 + 5], ALU.mult)
                        v.tensor_scalar(stat[:, st8 + 2:st8 + 3], stat[:, st8 + 7:st8 + 8],
                                        -1.0, 1e-5, ALU.mult, ALU.add)
                        v.sig(sm["vb"], 2 * k + 1)
                        v.wait_ge(sm["sd"], 2 * k + 1)
                        v.reciprocal(stat[:, st8 + 7:st8 + 8], stat[:, st8 + 3:st8 + 4])
                        v.tensor_scalar(xlnrow[:, (tt % 2) * 1024:(tt % 2 + 1) * 1024], xa,
                                        stat[:, st8 + 4:st8 + 5], stat[:, st8 + 7:st8 + 8],
                                        ALU.subtract, ALU.mult)
                        v.sig(sm["ln"], k + 1)
                        v.wait_ge(sm["sd"], 2 * k + 2)
                        v.reciprocal(stat[:, st8 + 5:st8 + 6], stat[:, st8 + 6:st8 + 7])
                        v.tensor_scalar(vjunk[:, :], xlnrow[:, (tt % 2) * 1024:(tt % 2 + 1) * 1024],
                                        stat[:, st8 + 5:st8 + 6], None, ALU.mult)
                        v.tensor_tensor(xrm[:, (tt % 2) * 1024:(tt % 2 + 1) * 1024],
                                        vjunk[:, :], g1rep[:, :], ALU.mult)
                        v.sig(sm["vb"], 2 * k + 2)
                        v.wait_ge(sm["tp"], k + 1)
                        v.tensor_copy(xT.rearrange("p (kc u) -> p kc u", kc=KC)[:, :, tt * 128:(tt + 1) * 128],
                                      ps_tp[:, :].rearrange("p (c u) -> p c u", c=KC))
                        v.sig(sm["xT"], k + 1)
                        for nc3 in range(3):
                            i3 = k * 3 + nc3
                            v.wait_ge(sm["qk"], i3 + 1)
                            v.tensor_copy(qn[:, (tt % 2) * 1536 + nc3 * ES:(tt % 2) * 1536 + (nc3 + 1) * ES],
                                          ps_q[:, (i3 % 2) * ES:(i3 % 2 + 1) * ES])
                            v.sig(sm["qe"], i3 + 1)
                        qnv = qn[:, (tt % 2) * 1536:(tt % 2 + 1) * 1536]
                        qrv = qr[:, (tt % 2) * 1536:(tt % 2 + 1) * 1536]
                        cosv = rote[:, (tt % 2) * 2048:(tt % 2) * 2048 + 1024]
                        sinv = rote[:, (tt % 2) * 2048 + 1024:(tt % 2 + 1) * 2048]
                        if k >= 2:
                            v.wait_ge(sm["htp"], 2 * (k - 1))
                        v.tensor_tensor(vrot[:, :], qnv[:, 0:1024], cosv[:, :], ALU.mult)
                        qsw = qnv[:, 0:1024].rearrange("p (h s j) -> p h s j", h=16, s=2)
                        ssw = sinv.rearrange("p (h s j) -> p h s j", h=16, s=2)
                        v.tensor_tensor(vjunk.rearrange("p (h s j) -> p h s j", h=16, s=2)[:, :, 0, :],
                                        qsw[:, :, 1, :], ssw[:, :, 0, :], ALU.mult)
                        v.tensor_tensor(vjunk.rearrange("p (h s j) -> p h s j", h=16, s=2)[:, :, 1, :],
                                        qsw[:, :, 0, :], ssw[:, :, 1, :], ALU.mult)
                        v.tensor_tensor(qrv[:, 0:1024], vrot[:, :], vjunk[:, :], ALU.add)
                        v.tensor_tensor(vrot[:, 0:256], qnv[:, 1024:1280], cosv[:, 0:256], ALU.mult)
                        ksw = qnv[:, 1024:1280].rearrange("p (h s j) -> p h s j", h=4, s=2)
                        ssw4 = sinv[:, 0:256].rearrange("p (h s j) -> p h s j", h=4, s=2)
                        v.tensor_tensor(vjunk[:, 0:256].rearrange("p (h s j) -> p h s j", h=4, s=2)[:, :, 0, :],
                                        ksw[:, :, 1, :], ssw4[:, :, 0, :], ALU.mult)
                        v.tensor_tensor(vjunk[:, 0:256].rearrange("p (h s j) -> p h s j", h=4, s=2)[:, :, 1, :],
                                        ksw[:, :, 0, :], ssw4[:, :, 1, :], ALU.mult)
                        v.tensor_tensor(qrv[:, 1024:1280], vrot[:, 0:256], vjunk[:, 0:256], ALU.add)
                        v.sig(sm["rot"], k + 2)
                        v.tensor_copy(vb.rearrange("p (u h j) -> p u h j", u=NTT, h=4)[:, tt, :, 0:64],
                                      qnv[:, 1280:1536].rearrange("p (h j) -> p h j", h=4))
                        v.sig(sm["ve"], k + 1)
                        v.wait_ge(sm["htp"], 2 * k + 1)
                        v.tensor_copy(qkT.rearrange("p (m u) -> p m u", m=10)[:, 0:8, tt * 128:(tt + 1) * 128],
                                      ps_tp[:, :].rearrange("p (c u) -> p c u", c=8))
                        v.sig(sm["hxc"], 2 * k + 1)
                        v.wait_ge(sm["htp"], 2 * k + 2)
                        v.tensor_copy(qkT.rearrange("p (m u) -> p m u", m=10)[:, 8:10, tt * 128:(tt + 1) * 128],
                                      ps_tp[:, 0:256].rearrange("p (c u) -> p c u", c=2))
                        v.sig(sm["hxc"], 2 * k + 2)
                    for h in range(16):
                        pb = (h % 2) * 64
                        for tc in range(NCH):
                            tiles = fl_tiles[tc]
                            c_idx = b * 64 + h * NCH + tc
                            opar = c_idx % 2
                            for i, tt in enumerate(tiles):
                                gi = (b * 16 + h) * TPB + fl_off[tc] + i
                                v.wait_ge(sm["pf"], gi + 1)
                                j = tt - 4 * tc
                                if 0 <= j < 4:
                                    v.tensor_tensor(ptile[:, (gi % 2) * ES:(gi % 2 + 1) * ES],
                                                    ptile[:, (gi % 2) * ES:(gi % 2 + 1) * ES],
                                                    maskt[:, j * ES:(j + 1) * ES], ALU.mult)
                                else:
                                    v.nop()
                                v.sig(sm["pm"], gi + 1)
                            gi_last = (b * 16 + h) * TPB + fl_off[tc] + len(tiles) - 1
                            v.wait_ge(sm["av"], gi_last + 1)
                            v.reciprocal(linv[0:1, opar * ES:(opar + 1) * ES],
                                         ps_o[64:65, opar * ES:(opar + 1) * ES])
                            v.sig(sm["li"], c_idx + 1)
                            v.wait_ge(sm["rp"], c_idx + 2)
                            v.tensor_copy(repsb[0:64, opar * ES:(opar + 1) * ES], ps_r[0:64, :])
                            v.sig(sm["rpc"], c_idx + 1)
                            v.tensor_tensor(ofm[pb:pb + 64, (h // 2) * T + tc * ES:(h // 2) * T + (tc + 1) * ES],
                                            ps_o[0:64, opar * ES:(opar + 1) * ES],
                                            repsb[0:64, opar * ES:(opar + 1) * ES],
                                            ALU.mult)
                            v.sig(sm["oe"], c_idx + 1)
                    for tt in range(NTT):
                        k = b * NTT + tt
                        v.wait_ge(sm["x%d" % (tt % 2)], 16 * (32 * b + 24 + tt // 2 + 1))
                        if tt >= 2:
                            v.wait_ge(sm["st%d" % (tt % 2)], 16 * (b * 8 + tt // 2))
                        elif b > 0:
                            v.wait_ge(sm["st%d" % (tt % 2)], 16 * (b * 8))
                        for nchunk in range(2):
                            i = tt * 2 + nchunk
                            v.wait_ge(sm["pj"], b * 32 + i + 1)
                            v.tensor_tensor(ytile[:, (tt % 2) * 1024 + nchunk * ES:(tt % 2) * 1024 + (nchunk + 1) * ES],
                                            ps_q[:, (nchunk % 2) * ES:(nchunk % 2 + 1) * ES],
                                            xtile[:, (tt % 2) * 1024 + nchunk * ES:(tt % 2) * 1024 + (nchunk + 1) * ES],
                                            ALU.add)
                            v.sig(sm["ye" if nchunk == 0 else "yd"], b * NTT + tt + 1)

    with ExitStack() as _es:
        def A(nm, shape, dt):
            return _es.enter_context(nc.sbuf_tensor(tag + "M" + nm, shape, dt))

        NT = NTOK // 128
        eyeb2 = A("eyeb", [128, 128], BF16)
        eyef2 = A("eyef", [128, 128], F32)
        iota = A("iota", [128, 8], F32)
        trilm = A("trilm", [128, 128], BF16)
        onesc = A("onesc", [128, 1], F32)
        onesr2 = A("onesr", [64, 128], F32)
        g2rep = A("g2rep", [128, 1024], F32)
        rwsb = A("rwsb", [128, KC * 24], F32)
        rwbf = A("rwbf", [128, KC * 24], BF16)
        brep = A("brep", [128, 8], F32)
        cmrow = A("cmrow", [1, 8], F32)
        pmcol = A("pmcol", [8, 1], F32)
        stg = A("stg", [128, 1024], BF16)
        htile = A("htile", [128, 2 * 1024], F32)
        xtile2 = A("xtile", [128, 2 * 1024], F32)
        stat = A("stat", [128, 16], F32)
        sjunk = A("sjunk", [128, 1024], F32)
        vjunk = A("vjunk", [128, 1024], F32)
        xhi = A("xhi", [128, 2 * 1024], BF16)
        xlo2 = A("xlo2", [128, 2 * 1024], BF16)
        xT2row = A("xT2row", [128, 2 * 1024], BF16)
        xlorow = A("xlorow", [128, 1024], BF16)
        rtt = A("rtt", [128, 72], F32)
        rst = A("rst", [128, NT * 5], F32)
        dbgl = A("dbgl", [128, NT * 8], F32) if DBG_ROUTE else None
        fT = A("fT", [8, 3 * 128], F32)
        ofs = A("ofs", [1, NT * 16], F32)
        orun = A("orun", [1, 16], F32)
        ccsb = A("ccsb", [8, 16], F32)
        coreoff = A("coreoff", [1, 16], F32)
        rstd2 = A("rstd2", [128, NT], F32)
        cxa = A("cxa", [128, NT], F32)
        ce_all = A("Mce", [128, NT * 8], F32)
        xs = A("xs", [128, KC * ES], BF16)
        wbA = A("wbA", [128, KC * 2048], BF16)
        wbB = A("wbB", [128, KC * 2048], BF16)
        wbC = A("wbC", [128, IC * 1024], BF16)
        hti = A("hti", [128, IC * ES], BF16)
        hs = A("hs", [128, ES], F32)
        ghin = A("ghin", [128, ES], F32)
        yacc = A("yacc", [128, 4 * 1024], F32)
        ytile2 = A("ytile", [128, 2 * 1024], F32)
        grow2 = ytile2
        ps_a = _es.enter_context(nc.psum_tensor(tag + "Mps_a", [128, 2 * ES], F32))
        ps_b2 = _es.enter_context(nc.psum_tensor(tag + "Mps_b", [128, 2 * ES], F32))
        ps_y = _es.enter_context(nc.psum_tensor(tag + "Mps_y", [128, 2 * ES], F32))
        ps_tp = _es.enter_context(nc.psum_tensor(tag + "Mps_tp", [128, 1024], BF16))
        ps_r = _es.enter_context(nc.psum_tensor(tag + "Mps_r", [128, 128], F32))

        LG, V0, EQ0, INC, F0, L2, F1, V1, DD, W0, W1C, WN0, WN1, PS0, PS1 = (
            0, 8, 9, 17, 25, 33, 41, 49, 50, 51, 52, 53, 54, 55, 56)

        with nc.Block(tag + "B") as blk:
            @blk.sync
            def _(s):
                s.dma_start(out=eyeb2[:, :], in_=eyeb_in[:, :]).then_inc(sm["fy"], 16)
                s.dma_start(out=eyef2[:, :], in_=eyef_in[:, :]).then_inc(sm["fy"], 16)
                s.dma_start(out=iota[:, :], in_=iota_in[:, :]).then_inc(sm["fy"], 16)
                s.dma_start(out=trilm[:, :], in_=trilm_in[:, :]).then_inc(sm["fy"], 16)
                s.dma_start(out=onesc[:, :], in_=onesc_in[:, :]).then_inc(sm["fy"], 16)
                s.dma_start(out=onesr2[:, :], in_=p.extra_aps[tag + "_ones"][:, :]).then_inc(sm["fy"], 16)
                s.dma_start(out=grow2[0:1, 0:1024], in_=g2.rearrange("(a q) -> a q", a=1)).then_inc(sm["fy"], 16)
                s.dma_start(out=grow2[32:33, 0:8], in_=router_b.rearrange("(a q) -> a q", a=1)).then_inc(sm["fy"], 16)
                s.dma_start(out=rwsb.rearrange("q (kc j) -> q kc j", kc=KC)[:, :, 0:8],
                            in_=router_w.rearrange("(kc q) e -> q kc e", q=128)).then_inc(sm["fy"], 16)
                s.dma_start(out=cmrow[:, :], in_=cmask_row[:, :]).then_inc(sm["fy"], 16)
                s.dma_start(out=pmcol[:, :], in_=pmask_col[:, :]).then_inc(sm["fy"], 16)
                for tt in range(NT):
                    if tt >= 2:
                        s.wait_ge(sm["hlo"], 2 * tt - 3)
                    s.dma_start(out=htile[:, (tt % 2) * 1024:(tt % 2 + 1) * 1024],
                                in_=h_h[tt * 128:(tt + 1) * 128, :]).then_inc(sm["hx%d" % (tt % 2)], 16)
                    s.wait_ge(sm["Mhxc"], 2 * tt + 1)
                    s.dma_start(out=xT2_h.rearrange("q (kc u) -> q kc u", kc=KC)[:, :, tt * 128:(tt + 1) * 128],
                                in_=xT2row[:, (tt % 2) * 1024:(tt % 2 + 1) * 1024].rearrange("q (c u) -> q c u", c=KC)
                                ).then_inc(sm["hxw%d" % (tt % 2)], 16)
                s.wait_ge(sm["cvt"], 16 * (32 + 384))
                for t2 in range(NT2 + 1):
                    if t2 >= 1:
                        c2 = t2 - 1
                        for j in range(4):
                            tt = c2 * 4 + j
                            if tt < 2:
                                s.wait_ge(sm["hlo"], 2 * NT)
                            else:
                                s.wait_ge(sm["Myd"], tt - 1)
                            s.dma_start(out=xtile2[:, (tt % 2) * 1024:(tt % 2 + 1) * 1024],
                                        in_=xin[tt * 128:(tt + 1) * 128, :]).then_inc(sm["fx%d" % (tt % 2)], 16)
                            s.dma_start(out=htile[:, (tt % 2) * 1024:(tt % 2 + 1) * 1024],
                                        in_=h_h[tt * 128:(tt + 1) * 128, :]).then_inc(sm["fh%d" % (tt % 2)], 16)
                            s.wait_ge(sm["Myd"], tt + 1)
                            s.dma_start(out=xout[tt * 128:(tt + 1) * 128, :],
                                        in_=ytile2[:, (tt % 2) * 1024:(tt % 2 + 1) * 1024]).then_inc(sm["ff%d" % (tt % 2)], 16)
                    if t2 < NT2:
                        s.wait_ge(sm["hxw0"], 16 * (NT // 2))
                        s.wait_ge(sm["hxw1"], 16 * (NT // 2))
                        if t2 >= 1:
                            s.wait_ge(sm["he"], ((t2 - 1) * 8 + 8) * IC)
                        s.dma_start(out=xs[:, :].rearrange("q (kc j) -> q kc j", kc=KC),
                                    in_=xT2_h.rearrange("q (kc u) -> q kc u", kc=KC)[:, :, t2 * ES:(t2 + 1) * ES]
                                    ).then_inc(sm["xs0"], 16)
                        for e in range(E):
                            g = t2 * 8 + e
                            if g >= 1:
                                s.wait_ge(sm["he"], g * IC)
                            s.dma_start(out=wbA[:, :], in_=w1c[e, :, :]).then_inc(sm["wl0"], 16)
                            s.dma_start(out=wbB[:, :], in_=w2c[e, :, :]).then_inc(sm["wl1"], 16)
                            if g >= 1:
                                s.wait_ge(sm["Mye"], g * 8)
                            s.dma_start(out=wbC[:, :], in_=w3c[e, :, :]).then_inc(sm["wl2"], 16)
                s.wait_ge(sm["ff0"], 16 * (NT // 2))
                s.wait_ge(sm["ff1"], 16 * (NT // 2))

            @blk.gpsimd
            def _(g):
                cv = 32
                lv = 0
                for e in range(E):
                    for (dst, src) in ((w1c, w1), (w2c, w2)):
                        for kc in range(KC):
                            for hh in range(2):
                                g.wait_ge(sm["cvt"], cv * 16)
                                g.dma_start(out=stg[:, :],
                                            in_=src[e, kc * 128:(kc + 1) * 128, hh * 1024:(hh + 1) * 1024]).then_inc(sm["wc"], 16)
                                cv += 1
                                lv += 1
                                g.wait_ge(sm["wc"], lv * 16)
                                g.dma_start(out=dst[e, :, kc * 2048 + hh * 1024:kc * 2048 + (hh + 1) * 1024],
                                            in_=stg[:, :]).then_inc(sm["cvt"], 16)
                    for ic in range(IC):
                        g.wait_ge(sm["cvt"], cv * 16)
                        g.dma_start(out=stg[:, 0:1024],
                                    in_=w3[e, ic * 128:(ic + 1) * 128, :]).then_inc(sm["wc"], 16)
                        cv += 1
                        lv += 1
                        g.wait_ge(sm["wc"], lv * 16)
                        g.dma_start(out=w3c[e, :, ic * 1024:(ic + 1) * 1024],
                                    in_=stg[:, 0:1024]).then_inc(sm["cvt"], 16)
                g.wait_ge(sm["rs"], 1)
                g.dma_start(out=cc_in[:, :], in_=ccsb[:, :]).then_inc(sm["cc"], 16)
                g.wait_ge(sm["cc"], 16)
                if cc_mock:
                    g.dma_start(out=cc_out[:, :], in_=cc_in[:, :]).then_inc(sm["cc"], 16)
                else:
                    g.collective_compute(
                        "AllReduce", mybir.AluOpType.add,
                        replica_groups=[list(range(8))],
                        ins=[cc_in.opt()], outs=[cc_out.opt()],
                    ).then_inc(sm["cc"])
                g.wait_ge(sm["cc"], 32 if cc_mock else 17)
                g.dma_start(out=ccsb[:, :], in_=cc_out[:, :]).then_inc(sm["ccd"], 16)

            @blk.tensor
            def _(t):
                t.wait_ge(sm["fy"], 16 * 11)
                t.matmul(ps_y[:, 0:ES], onesr2[0:1, :], grow2[0:1, 0:ES], start=True, stop=True)
                t.matmul(ps_y[:, ES:2 * ES], onesr2[0:1, :], grow2[0:1, ES:2 * ES], start=True, stop=True)
                t.matmul(ps_r[:, 0:8], onesr2[32:33, :], grow2[32:33, 0:8], start=True, stop=True).then_inc(sm["rrow"], 1)
                t.wait_ge(sm["Mrle"], 1)
                for tt in range(NT):
                    t.wait_ge(sm["hlo"], 2 * tt + 1)
                    if tt >= 1:
                        t.wait_ge(sm["Mhxc"], 2 * tt)
                    for kc in range(KC):
                        ins = t.transpose(ps_tp[:, kc * 128:(kc + 1) * 128],
                                          xhi[:, (tt % 2) * 1024 + kc * 128:(tt % 2) * 1024 + (kc + 1) * 128],
                                          eyeb2[:, :])
                    ins.then_inc(sm["Mhtp"], 1)
                    t.wait_ge(sm["Mhxc"], 2 * tt + 1)
                    t.wait_ge(sm["hlo"], 2 * tt + 2)
                    for kc in range(KC):
                        ins = t.transpose(ps_tp[:, kc * 128:(kc + 1) * 128],
                                          xlo2[:, (tt % 2) * 1024 + kc * 128:(tt % 2) * 1024 + (kc + 1) * 128],
                                          eyeb2[:, :])
                    ins.then_inc(sm["Mhtp"], 1)
                    t.wait_ge(sm["Mhxc"], 2 * tt + 2)
                    t.wait_ge(sm["Mrle"], tt + 1)
                    xr = xT2row[:, (tt % 2) * 1024:(tt % 2 + 1) * 1024]
                    for kc in range(KC):
                        t.matmul(ps_r[:, 8:16], xr[:, kc * 128:(kc + 1) * 128],
                                 rwbf[:, kc * 24 + 0:kc * 24 + 8], start=(kc == 0), stop=False)
                        t.matmul(ps_r[:, 8:16], xr[:, kc * 128:(kc + 1) * 128],
                                 rwbf[:, kc * 24 + 8:kc * 24 + 16], start=False, stop=False)
                        t.matmul(ps_r[:, 8:16], xlorow[:, kc * 128:(kc + 1) * 128],
                                 rwbf[:, kc * 24 + 16:kc * 24 + 24], start=False, stop=False)
                        ins = t.matmul(ps_r[:, 8:16], xlorow[:, kc * 128:(kc + 1) * 128],
                                       rwbf[:, kc * 24 + 8:kc * 24 + 16], start=False,
                                       stop=(kc == KC - 1))
                    ins.then_inc(sm["Mrtm"], 1)
                    t.wait_ge(sm["Mrt"], tt + 1)
                    t.matmul(ps_r[0:1, 16:24], onesc[:, :], rtt[:, F0:F0 + 8], start=True, stop=True)
                    t.matmul(ps_r[0:1, 24:32], onesc[:, :], rtt[:, F1:F1 + 8], start=True, stop=True).then_inc(sm["Msc"], 1)
                    t.transpose(ps_a[0:8, 0:128], rtt[:, F0:F0 + 8], eyef2[:, :])
                    t.transpose(ps_a[0:8, 128:256], rtt[:, F1:F1 + 8], eyef2[:, :]).then_inc(sm["Msc"], 1)
                    t.wait_ge(sm["Mpf"], tt + 1)
                    if tt >= 1:
                        t.wait_ge(sm["Moe"], tt)
                    t.matmul(ps_b2[:, 0:128], fT[0:8, 0:128], fT[0:8, 0:128], start=True, stop=True)
                    t.matmul(ps_b2[:, 128:256], fT[0:8, 128:256], fT[0:8, 256:384], start=True, stop=True).then_inc(sm["Mpm"], 1)
                t.wait_ge(sm["co"], 1)
                t.matmul(ps_r[0:8, 32:48], cmrow[:, :], orun[0:1, :], start=True, stop=True).then_inc(sm["Mav"], 1)
                t.wait_ge(sm["ccd"], 16)
                t.wait_ge(sm["Mli"], 1)
                t.matmul(ps_r[0:1, 48:64], pmcol[:, :], ccsb[:, :], start=True, stop=True).then_inc(sm["Mav"], 1)
                for tt in range(NT):
                    t.wait_ge(sm["Moe"], NT + tt + 1)
                    if tt >= 1:
                        t.wait_ge(sm["Mce"], tt)
                    t.matmul(ps_r[:, 64:72], onesr2[0:1, :],
                             ofs[0:1, tt * 16:tt * 16 + 8], start=True, stop=True)
                    t.matmul(ps_r[:, 72:80], onesr2[0:1, :],
                             ofs[0:1, tt * 16 + 8:tt * 16 + 16], start=True, stop=True).then_inc(sm["Mrp"], 1)
                for t2 in range(NT2):
                    t.wait_ge(sm["xs0"], 16 * (t2 + 1))
                    xsl = xs[:, :].rearrange("q (kc j) -> q kc j", kc=KC)
                    for e in range(E):
                        g = t2 * 8 + e
                        t.wait_ge(sm["wl0"], 16 * (g + 1))
                        t.wait_ge(sm["wl1"], 16 * (g + 1))
                        for it in range(IC):
                            hcnt = g * IC + it
                            if hcnt >= 2:
                                t.wait_ge(sm["he"], hcnt - 1)
                            for kc in range(KC):
                                t.matmul(ps_a[:, (hcnt % 2) * ES:(hcnt % 2 + 1) * ES],
                                         wbA[:, kc * 2048 + it * 128:kc * 2048 + (it + 1) * 128],
                                         xsl[:, kc, :], start=(kc == 0), stop=(kc == KC - 1))
                            for kc in range(KC):
                                ins = t.matmul(ps_b2[:, (hcnt % 2) * ES:(hcnt % 2 + 1) * ES],
                                               wbB[:, kc * 2048 + it * 128:kc * 2048 + (it + 1) * 128],
                                               xsl[:, kc, :], start=(kc == 0), stop=(kc == KC - 1))
                            ins.then_inc(sm["h1"], 1)
                        t.wait_ge(sm["wl2"], 16 * (g + 1))
                        t.wait_ge(sm["he"], (g + 1) * IC)
                        for j in range(4):
                            for hc in range(2):
                                ycnt = g * 8 + j * 2 + hc
                                if ycnt >= 2:
                                    t.wait_ge(sm["Mye"], ycnt - 1)
                                elif g == 0:
                                    t.wait_ge(sm["Mye"], 0)
                                for ic in range(IC):
                                    ins = t.matmul(ps_y[:, (ycnt % 2) * ES:(ycnt % 2 + 1) * ES],
                                                   hti[:, ic * ES + j * 128:ic * ES + (j + 1) * 128],
                                                   wbC[:, ic * 1024 + hc * ES:ic * 1024 + (hc + 1) * ES],
                                                   start=(ic == 0), stop=(ic == IC - 1))
                                ins.then_inc(sm["yp"], 1)

            @blk.scalar
            def _(sc_raw):
                sc = SelfSync(sc_raw, sm["sch2"])
                for tt in range(NT):
                    sc.wait_ge(sm["hx%d" % (tt % 2)], 16 * (tt // 2 + 1))
                    ha = htile[:, (tt % 2) * 1024:(tt % 2 + 1) * 1024]
                    st8 = (tt % 2) * 8
                    sc.activation(sjunk[:, :], ha, AF.Square,
                                  accum_out=stat[:, st8 + 1:st8 + 2])
                    sc.activation(stat[:, st8 + 3:st8 + 4],
                                  stat[:, st8 + 1:st8 + 2], AF.Sqrt,
                                  bias=1e-6, scale=1.0 / 1024.0)
                    sc.sig(sm["hsd"], tt + 1)
                    sc.wait_ge(sm["hvb"], tt + 1)
                    sc.activation(rtt[:, W0:W0 + 1], rtt[:, DD:DD + 1], AF.Sigmoid,
                                  bias=0.0, scale=1.0)
                    sc.sig(sm["Msg"], tt + 1)
                for t2 in range(NT2):
                    for e in range(E):
                        g = t2 * 8 + e
                        for it in range(IC):
                            hcnt = g * IC + it
                            sc.wait_ge(sm["h1"], hcnt + 1)
                            if hcnt >= 1:
                                sc.wait_ge(sm["gh"], hcnt)
                            sc.activation(hs[:, 0:ES],
                                          ps_b2[:, (hcnt % 2) * ES:(hcnt % 2 + 1) * ES],
                                          AF.Identity, bias=0.0, scale=1.0)
                            sc.sig(sm["slh"], hcnt + 1)
                            sc.wait_ge(sm["gh"], hcnt + 1)
                            sc.activation(hti[:, it * ES:(it + 1) * ES], ghin[:, :],
                                          AF.Gelu, bias=0.0, scale=1.0)
                            sc.sig(sm["he"], hcnt + 1)

            @blk.vector
            def _(v_raw):
                v = SelfSync(v_raw, sm["vch2"])
                v.wait_ge(sm["rrow"], 1)
                v.tensor_copy(g2rep[:, 0:ES], ps_y[:, 0:ES])
                v.tensor_copy(g2rep[:, ES:2 * ES], ps_y[:, ES:2 * ES])
                v.tensor_copy(brep[:, :], ps_r[:, 0:8])
                v.wait_ge(sm["fy"], 16 * 11)
                rv = rwsb.rearrange("q (kc j) -> q kc j", kc=KC)
                rb = rwbf.rearrange("q (kc j) -> q kc j", kc=KC)
                v.tensor_copy(rb[:, :, 0:8], rv[:, :, 0:8])
                v.tensor_copy(rv[:, :, 8:16], rb[:, :, 0:8])
                v.tensor_tensor(rv[:, :, 16:24], rv[:, :, 0:8], rv[:, :, 8:16], ALU.subtract)
                v.tensor_copy(rb[:, :, 8:16], rv[:, :, 16:24])
                v.tensor_copy(rb[:, :, 16:24], rb[:, :, 0:8])
                v.memset(orun[:, :], 0.0)
                v.sig(sm["Mrle"], 1)
                for tt in range(NT):
                    ha = htile[:, (tt % 2) * 1024:(tt % 2 + 1) * 1024]
                    st8 = (tt % 2) * 8
                    v.wait_ge(sm["hx%d" % (tt % 2)], 16 * (tt // 2 + 1))
                    v.wait_ge(sm["hsd"], tt + 1)
                    v.reciprocal(stat[:, st8 + 5:st8 + 6], stat[:, st8 + 3:st8 + 4])
                    v.tensor_copy(rstd2[:, tt:tt + 1], stat[:, st8 + 5:st8 + 6])
                    v.tensor_scalar(vjunk[:, :], ha, stat[:, st8 + 5:st8 + 6], None, ALU.mult)
                    v.tensor_tensor(vjunk[:, :], vjunk[:, :], g2rep[:, :], ALU.mult)
                    if tt >= 2:
                        v.wait_ge(sm["Mhtp"], 2 * tt - 2)
                    v.tensor_copy(xhi[:, (tt % 2) * 1024:(tt % 2 + 1) * 1024], vjunk[:, :])
                    v.sig(sm["hlo"], 2 * tt + 1)
                    v.tensor_tensor(xlo2[:, (tt % 2) * 1024:(tt % 2 + 1) * 1024], vjunk[:, :],
                                    xhi[:, (tt % 2) * 1024:(tt % 2 + 1) * 1024], ALU.subtract)
                    v.sig(sm["hlo"], 2 * tt + 2)
                    v.wait_ge(sm["Mhtp"], 2 * tt + 1)
                    if tt >= 2:
                        v.wait_ge(sm["hxw%d" % (tt % 2)], 16 * (tt // 2))
                    v.tensor_copy(xT2row[:, (tt % 2) * 1024:(tt % 2 + 1) * 1024].rearrange("q (c u) -> q c u", c=KC),
                                  ps_tp[:, :].rearrange("q (c u) -> q c u", c=KC))
                    v.sig(sm["Mhxc"], 2 * tt + 1)
                    v.wait_ge(sm["Mhtp"], 2 * tt + 2)
                    v.tensor_copy(xlorow[:, :].rearrange("q (c u) -> q c u", c=KC),
                                  ps_tp[:, :].rearrange("q (c u) -> q c u", c=KC))
                    v.sig(sm["Mhxc"], 2 * tt + 2)
                    v.wait_ge(sm["Mrtm"], tt + 1)
                    v.tensor_tensor(rtt[:, LG:LG + 8], ps_r[:, 8:16], brep[:, :], ALU.add)
                    if DBG_ROUTE:
                        v.tensor_copy(dbgl[:, tt * 8:(tt + 1) * 8], rtt[:, LG:LG + 8])
                    v.reduce_max(rtt[:, V0:V0 + 1], rtt[:, LG:LG + 8], mybir.AxisListType.X)
                    v.tensor_scalar(rtt[:, EQ0:EQ0 + 8], rtt[:, LG:LG + 8],
                                    rtt[:, V0:V0 + 1], None, ALU.is_equal)
                    v.tensor_tensor_scan(rtt[:, INC:INC + 8], rtt[:, EQ0:EQ0 + 8],
                                         rtt[:, EQ0:EQ0 + 8], 0.0, ALU.add, ALU.max)
                    v.tensor_scalar(rtt[:, F0:F0 + 8], rtt[:, INC:INC + 8], 1.0, None, ALU.is_equal)
                    v.tensor_tensor(rtt[:, F0:F0 + 8], rtt[:, F0:F0 + 8],
                                    rtt[:, EQ0:EQ0 + 8], ALU.mult)
                    v.tensor_scalar(rtt[:, L2:L2 + 8], rtt[:, F0:F0 + 8], -BIG, None, ALU.mult)
                    v.tensor_tensor(rtt[:, L2:L2 + 8], rtt[:, L2:L2 + 8],
                                    rtt[:, LG:LG + 8], ALU.add)
                    v.reduce_max(rtt[:, V1:V1 + 1], rtt[:, L2:L2 + 8], mybir.AxisListType.X)
                    v.tensor_scalar(rtt[:, F1:F1 + 8], rtt[:, L2:L2 + 8],
                                    rtt[:, V1:V1 + 1], None, ALU.is_equal)
                    v.tensor_tensor_scan(rtt[:, INC:INC + 8], rtt[:, F1:F1 + 8],
                                         rtt[:, F1:F1 + 8], 0.0, ALU.add, ALU.max)
                    v.tensor_scalar(rtt[:, EQ0:EQ0 + 8], rtt[:, INC:INC + 8], 1.0, None, ALU.is_equal)
                    v.tensor_tensor(rtt[:, F1:F1 + 8], rtt[:, F1:F1 + 8],
                                    rtt[:, EQ0:EQ0 + 8], ALU.mult)
                    v.tensor_tensor(rtt[:, DD:DD + 1], rtt[:, V0:V0 + 1],
                                    rtt[:, V1:V1 + 1], ALU.subtract)
                    v.sig(sm["hvb"], tt + 1)
                    v.tensor_tensor(rtt[:, L2:L2 + 8], rtt[:, F0:F0 + 8], iota[:, :], ALU.mult)
                    v.reduce_sum(rst[:, tt * 5 + 0:tt * 5 + 1], rtt[:, L2:L2 + 8], mybir.AxisListType.X)
                    v.tensor_tensor(rtt[:, L2:L2 + 8], rtt[:, F1:F1 + 8], iota[:, :], ALU.mult)
                    v.reduce_sum(rst[:, tt * 5 + 1:tt * 5 + 2], rtt[:, L2:L2 + 8], mybir.AxisListType.X)
                    v.sig(sm["Mrt"], tt + 1)
                    v.wait_ge(sm["Msc"], 2 * tt + 2)
                    v.tensor_copy(fT[0:8, 0:128], ps_a[0:8, 0:128])
                    v.tensor_copy(fT[0:8, 128:256], ps_a[0:8, 128:256])
                    v.tensor_tensor(fT[0:8, 256:384], fT[0:8, 0:128], fT[0:8, 128:256], ALU.add)
                    v.sig(sm["Mpf"], tt + 1)
                    v.tensor_copy(ofs[0:1, tt * 16:(tt + 1) * 16], orun[0:1, :])
                    v.tensor_tensor(orun[0:1, 0:8], orun[0:1, 0:8], ps_r[0:1, 16:24], ALU.add)
                    v.tensor_tensor(orun[0:1, 8:16], orun[0:1, 8:16], ps_r[0:1, 16:24], ALU.add)
                    v.tensor_tensor(orun[0:1, 8:16], orun[0:1, 8:16], ps_r[0:1, 24:32], ALU.add)
                    v.sig(sm["Mrle"], tt + 2)
                    v.wait_ge(sm["Mpm"], tt + 1)
                    v.tensor_tensor(vjunk[:, 0:128], ps_b2[:, 0:128], trilm[:, :], ALU.mult)
                    v.reduce_sum(rtt[:, WN0:WN0 + 1], vjunk[:, 0:128], mybir.AxisListType.X)
                    v.tensor_tensor(vjunk[:, 0:128], ps_b2[:, 128:256], trilm[:, :], ALU.mult)
                    v.reduce_sum(rtt[:, WN1:WN1 + 1], vjunk[:, 0:128], mybir.AxisListType.X)
                    v.tensor_copy(rst[:, tt * 5 + 2:tt * 5 + 3], rtt[:, WN0:WN0 + 1])
                    v.tensor_copy(rst[:, tt * 5 + 3:tt * 5 + 4], rtt[:, WN1:WN1 + 1])
                    v.sig(sm["Moe"], tt + 1)
                    v.wait_ge(sm["Msg"], tt + 1)
                    v.tensor_copy(rst[:, tt * 5 + 4:tt * 5 + 5], rtt[:, W0:W0 + 1])
                v.sig(sm["co"], 1)
                v.wait_ge(sm["Mav"], 1)
                v.tensor_copy(ccsb[:, :], ps_r[0:8, 32:48])
                v.sig(sm["rs"], 1)
                v.wait_ge(sm["ccd"], 16)
                v.nop()
                v.sig(sm["Mli"], 1)
                v.wait_ge(sm["Mav"], 2)
                v.tensor_copy(coreoff[0:1, :], ps_r[0:1, 48:64])
                for tt in range(NT):
                    v.tensor_tensor(ofs[0:1, tt * 16:(tt + 1) * 16], ofs[0:1, tt * 16:(tt + 1) * 16],
                                    coreoff[0:1, :], ALU.add)
                    v.sig(sm["Moe"], NT + tt + 1)
                    v.wait_ge(sm["Mrp"], tt + 1)
                    f0s = rtt[:, F0:F0 + 8]
                    f1s = rtt[:, F1:F1 + 8]
                    v.tensor_scalar(f0s, iota[:, :], rst[:, tt * 5 + 0:tt * 5 + 1], None, ALU.is_equal)
                    v.tensor_scalar(f1s, iota[:, :], rst[:, tt * 5 + 1:tt * 5 + 2], None, ALU.is_equal)
                    v.tensor_tensor(rtt[:, LG:LG + 8], f0s, ps_r[:, 64:72], ALU.mult)
                    v.reduce_sum(rtt[:, PS0:PS0 + 1], rtt[:, LG:LG + 8], mybir.AxisListType.X)
                    v.tensor_tensor(rtt[:, LG:LG + 8], f1s, ps_r[:, 72:80], ALU.mult)
                    v.reduce_sum(rtt[:, PS1:PS1 + 1], rtt[:, LG:LG + 8], mybir.AxisListType.X)
                    v.sig(sm["Mce"], tt + 1)
                    v.tensor_tensor(rtt[:, PS0:PS0 + 1], rtt[:, PS0:PS0 + 1],
                                    rst[:, tt * 5 + 2:tt * 5 + 3], ALU.add)
                    v.tensor_tensor(rtt[:, PS1:PS1 + 1], rtt[:, PS1:PS1 + 1],
                                    rst[:, tt * 5 + 3:tt * 5 + 4], ALU.add)
                    v.tensor_scalar(rtt[:, V0:V0 + 1], rtt[:, PS0:PS0 + 1], CAP, None, ALU.is_lt)
                    v.tensor_scalar(rtt[:, V1:V1 + 1], rtt[:, PS1:PS1 + 1], CAP, None, ALU.is_lt)
                    w0s = rst[:, tt * 5 + 4:tt * 5 + 5]
                    v.tensor_tensor(rtt[:, DD:DD + 1], rtt[:, V0:V0 + 1], w0s, ALU.mult)
                    v.tensor_scalar(rtt[:, W1C:W1C + 1], w0s, -1.0, 1.0, ALU.mult, ALU.add)
                    v.tensor_tensor(rtt[:, W1C:W1C + 1], rtt[:, W1C:W1C + 1],
                                    rtt[:, V1:V1 + 1], ALU.mult)
                    v.tensor_tensor(rtt[:, INC:INC + 1], rtt[:, DD:DD + 1],
                                    rtt[:, W1C:W1C + 1], ALU.add)
                    v.tensor_scalar(cxa[:, tt:tt + 1], rtt[:, INC:INC + 1], -1.0, 1.0,
                                    ALU.mult, ALU.add)
                    v.tensor_tensor(rtt[:, PS0:PS0 + 1], rst[:, tt * 5 + 0:tt * 5 + 1],
                                    rtt[:, V0:V0 + 1], ALU.mult)
                    v.tensor_tensor(rtt[:, PS1:PS1 + 1], rst[:, tt * 5 + 1:tt * 5 + 2],
                                    rtt[:, V1:V1 + 1], ALU.mult)
                    v.tensor_tensor(rtt[:, EQ0:EQ0 + 1], rtt[:, PS0:PS0 + 1],
                                    rtt[:, PS1:PS1 + 1], ALU.max)
                    v.tensor_scalar(rtt[:, LG:LG + 8], iota[:, :], rtt[:, EQ0:EQ0 + 1],
                                    None, ALU.is_equal)
                    v.tensor_scalar(ce_all[:, tt * 8:(tt + 1) * 8], rtt[:, LG:LG + 8],
                                    rtt[:, INC:INC + 1], None, ALU.mult)
                    v.sig(sm["Mve"], tt + 1)
                for t2 in range(NT2):
                    for e in range(E):
                        g = t2 * 8 + e
                        for it in range(IC):
                            hcnt = g * IC + it
                            v.wait_ge(sm["slh"], hcnt + 1)
                            if it == 0 and g >= 1:
                                v.wait_ge(sm["yp"], g * 8)
                            v.tensor_tensor(ghin[:, :],
                                            ps_a[:, (hcnt % 2) * ES:(hcnt % 2 + 1) * ES],
                                            hs[:, 0:ES],
                                            ALU.mult)
                            v.sig(sm["gh"], hcnt + 1)
                        for j in range(4):
                            tt = t2 * 4 + j
                            for hc in range(2):
                                ycnt = g * 8 + j * 2 + hc
                                v.wait_ge(sm["yp"], ycnt + 1)
                                v.wait_ge(sm["Mve"], tt + 1)
                                cecol = ce_all[:, tt * 8 + e:tt * 8 + e + 1]
                                ya = yacc[:, j * 1024 + hc * ES:j * 1024 + (hc + 1) * ES]
                                if e == 0:
                                    v.tensor_scalar(ya, ps_y[:, (ycnt % 2) * ES:(ycnt % 2 + 1) * ES],
                                                    cecol, None, ALU.mult)
                                else:
                                    v.scalar_tensor_tensor(ya, ps_y[:, (ycnt % 2) * ES:(ycnt % 2 + 1) * ES],
                                                           cecol, ya, ALU.mult, ALU.add)
                                v.sig(sm["Mye"], ycnt + 1)
                    for j in range(4):
                        tt = t2 * 4 + j
                        v.wait_ge(sm["fx%d" % (tt % 2)], 16 * (tt // 2 + 1))
                        v.wait_ge(sm["fh%d" % (tt % 2)], 16 * (tt // 2 + 1))
                        if tt >= 2:
                            v.wait_ge(sm["ff%d" % (tt % 2)], 16 * (tt // 2))
                        v.tensor_tensor(rtt[:, DD:DD + 1], cxa[:, tt:tt + 1],
                                        rstd2[:, tt:tt + 1], ALU.mult)
                        for hc in range(2):
                            sl2 = slice((tt % 2) * 1024 + hc * ES, (tt % 2) * 1024 + (hc + 1) * ES)
                            v.scalar_tensor_tensor(vjunk[:, hc * ES:(hc + 1) * ES],
                                                   htile[:, sl2], rtt[:, DD:DD + 1],
                                                   g2rep[:, hc * ES:(hc + 1) * ES],
                                                   ALU.mult, ALU.mult)
                            v.tensor_tensor(vjunk[:, hc * ES:(hc + 1) * ES],
                                            vjunk[:, hc * ES:(hc + 1) * ES],
                                            yacc[:, j * 1024 + hc * ES:j * 1024 + (hc + 1) * ES],
                                            ALU.add)
                            v.tensor_tensor(vjunk[:, hc * ES:(hc + 1) * ES],
                                            vjunk[:, hc * ES:(hc + 1) * ES],
                                            htile[:, sl2], ALU.add)
                            v.tensor_tensor(ytile2[:, sl2], vjunk[:, hc * ES:(hc + 1) * ES],
                                            xtile2[:, sl2], ALU.add)
                        v.sig(sm["Myd"], tt + 1)


def emit_l0_attn_b(p, xin, xout, wjT, wvT, woT, tag="ab"):
    nc = p.nc
    eyeb_in = p.const_input(tag + "_eyeb", np.eye(128, dtype=BFDT))
    eyef_in = p.const_input(tag + "_eyef", np.eye(128, dtype=np.float32))
    sl = np.arange(128)[:, None]
    tl = np.arange(128)[None, :]
    tril_in = p.const_input(tag + "_tril", (sl <= tl).astype(BFDT))

    scrow_h = nc.dram_tensor(tag + "_scrow", [T], F32).ap()
    mrow_h = nc.dram_tensor(tag + "_mrow", [T], F32).ap()

    sm = {n: p.sem(tag + n) for n in
          ["w", "x0", "x1", "vb", "sd", "ln", "tp", "xT", "ye", "yv", "ve",
           "scv", "srt", "src", "srd", "srl", "rmx", "sc", "pf", "pm",
           "dn", "av", "oe", "ot", "oc", "pj", "yo", "yd0", "yd1", "rc", "rc2", "vch", "sch"]}

    with ExitStack() as _es:
        def A(nm, shape, dt):
            return _es.enter_context(nc.sbuf_tensor(tag + nm, shape, dt))

        xtile = A("xtile", [128, 2 * 1024], F32)
        sjunk = A("sjunk", [128, 1024], F32)
        vjunk = A("vjunk", [128, 1024], F32)
        stat = A("stat", [128, 16], F32)
        xlnb = A("xlnb", [128, 2 * 1024], BF16)
        xT = A("xT", [128, KC * T], BF16)
        valc = A("valc", [128, NTT * 1024], BF16)
        wjsb = A("wjsb", [128, KC * 1024], BF16)
        wvsb = A("wvsb", [128, KC * 1024], BF16)
        wosb = A("wosb", [128, KC * 1024], BF16)
        scc = A("scc", [128, 16], F32)
        scr = A("scr", [16, 128], F32)
        lhsb = A("lhsb", [2, T], F32)
        rhs2 = A("rhs2", [2, T], F32)
        rmax = A("rmax", [1, T], F32)
        mrow = A("mrow", [1, T], F32)
        pch = A("pch", [128, NTT * 128], BF16)
        onesb = A("onesb", [128, 1], BF16)
        trilb = A("trilb", [128, 128], BF16)
        eyeb = A("eyeb", [128, 128], BF16)
        eyef = A("eyef", [128, 128], F32)
        ob = A("ob", [128, 1024], BF16)
        oT = A("oT", [128, KC * 128], BF16)
        otile = A("otile", [128, 2 * 1024], F32)
        dcol = A("dcol", [128, 2], F32)
        ps_tp = _es.enter_context(nc.psum_tensor(tag + "ps_tp", [128, 1024], BF16))
        ps_sc = _es.enter_context(nc.psum_tensor(tag + "ps_sc", [128, 1024], F32))
        ps_d = _es.enter_context(nc.psum_tensor(tag + "ps_d", [128, 128], F32))
        ps_av = _es.enter_context(nc.psum_tensor(tag + "ps_av", [128, 1024], F32))
        ps_p = _es.enter_context(nc.psum_tensor(tag + "ps_p", [128, 1024], F32))

        with nc.Block() as blk:
            @blk.sync
            def _(s):
                s.dma_start(out=eyeb[:, :], in_=eyeb_in[:, :]).then_inc(sm["rc"], 16)
                s.dma_start(out=eyef[:, :], in_=eyef_in[:, :]).then_inc(sm["rc"], 16)
                s.dma_start(out=trilb[:, :], in_=tril_in[:, :]).then_inc(sm["rc"], 16)
                for b in range(BLOC):
                    for tt in range(NTT):
                        k = b * NTT + tt
                        if k >= 2:
                            s.wait_ge(sm["ln"], k - 1)
                        s.dma_start(out=xtile[:, (tt % 2) * 1024:(tt % 2 + 1) * 1024],
                                    in_=xin[k * 128:(k + 1) * 128, :]).then_inc(sm["x%d" % (tt % 2)], 16)
                    s.wait_ge(sm["src"], b + 1)
                    s.dma_start(out=scrow_h.rearrange("(u j) -> u j", u=16),
                                in_=scr[:, :]).then_inc(sm["srd"], 16)
                    s.wait_ge(sm["srd"], 48 * b + 16)
                    s.dma_start(out=lhsb[0:1, :],
                                in_=scrow_h.rearrange("(a j) -> a j", a=1)).then_inc(sm["srd"], 16)
                    s.dma_start(out=rhs2[0:1, :],
                                in_=scrow_h.rearrange("(a j) -> a j", a=1)).then_inc(sm["srd"], 16)
                    s.wait_ge(sm["rmx"], b + 1)
                    s.dma_start(out=mrow_h.rearrange("(a j) -> a j", a=1),
                                in_=mrow[0:1, :]).then_inc(sm["srl"], 16)
                    s.wait_ge(sm["srl"], 32 * b + 16)
                    s.dma_start(out=rhs2[1:2, :],
                                in_=mrow_h.rearrange("(a j) -> a j", a=1)).then_inc(sm["srl"], 16)
                    for tt in range(NTT):
                        k = b * NTT + tt
                        if k >= 2:
                            s.wait_ge(sm["yd%d" % (tt % 2)], 16 * (k // 2))
                        s.dma_start(out=xtile[:, (tt % 2) * 1024:(tt % 2 + 1) * 1024],
                                    in_=xin[k * 128:(k + 1) * 128, :]).then_inc(sm["x%d" % (tt % 2)], 16)
                        s.wait_ge(sm["yo"], k + 1)
                        s.dma_start(out=xout[k * 128:(k + 1) * 128, :],
                                    in_=otile[:, (tt % 2) * 1024:(tt % 2 + 1) * 1024]).then_inc(sm["yd%d" % (tt % 2)], 16)
                s.wait_ge(sm["yd0"], 16 * (BLOC * NTT // 2))
                s.wait_ge(sm["yd1"], 16 * (BLOC * NTT // 2))

            @blk.gpsimd
            def _(g):
                for i, (dst, src) in enumerate(((wjsb, wjT), (wvsb, wvT), (wosb, woT))):
                    g.dma_start(out=dst.rearrange("q (kc d) -> q kc d", kc=KC),
                                in_=src.rearrange("(kc q) d -> q kc d", q=128)).then_inc(sm["w"], 16)
                g.wait_ge(sm["w"], 48)

            @blk.tensor
            def _(t):
                t.wait_ge(sm["rc"], 48)
                t.wait_ge(sm["rc2"], 1)
                t.wait_ge(sm["w"], 48)
                for b in range(BLOC):
                    if b >= 1:
                        t.wait_ge(sm["yo"], b * NTT)
                    for tt in range(NTT):
                        k = b * NTT + tt
                        t.wait_ge(sm["ln"], k + 1)
                        if k >= 1:
                            t.wait_ge(sm["xT"], k)
                        for kc in range(KC):
                            ins = t.transpose(ps_tp[:, kc * 128:(kc + 1) * 128],
                                              xlnb[:, (tt % 2) * 1024 + kc * 128:(tt % 2) * 1024 + (kc + 1) * 128],
                                              eyeb[:, :])
                        ins.then_inc(sm["tp"], 1)
                        t.wait_ge(sm["xT"], k + 1)
                        if k >= 1:
                            t.wait_ge(sm["ye"], 2 * k)
                        for half in range(2):
                            for kc in range(KC):
                                ins = t.matmul(ps_av[:, half * ES:(half + 1) * ES],
                                               xT[:, kc * T + tt * 128:kc * T + (tt + 1) * 128],
                                               wjsb[:, kc * 1024 + half * ES:kc * 1024 + (half + 1) * ES],
                                               start=(kc == 0), stop=(kc == KC - 1))
                            ins.then_inc(sm["sc"], 1)
                        if k >= 1:
                            t.wait_ge(sm["ve"], 2 * k)
                        for half in range(2):
                            for kc in range(KC):
                                ins = t.matmul(ps_p[:, half * ES:(half + 1) * ES],
                                               xT[:, kc * T + tt * 128:kc * T + (tt + 1) * 128],
                                               wvsb[:, kc * 1024 + half * ES:kc * 1024 + (half + 1) * ES],
                                               start=(kc == 0), stop=(kc == KC - 1))
                            ins.then_inc(sm["yv"], 1)
                    t.wait_ge(sm["scv"], b * NTT + NTT)
                    if b >= 1:
                        t.wait_ge(sm["dn"], b * NTT)
                    t.transpose(ps_d[0:16, 0:128], scc[:, 0:16], eyef[:, :]).then_inc(sm["srt"], 1)
                    t.wait_ge(sm["srd"], 48 * b + 48)
                    t.wait_ge(sm["srl"], 32 * b + 32)
                    for tt in range(NTT):
                        k = b * NTT + tt
                        for st in range(tt + 1):
                            i2 = k * NTT + st
                            scnt = b * 136 + tt * (tt + 1) // 2 + st
                            if scnt >= 2:
                                t.wait_ge(sm["pf"], scnt - 1)
                            t.matmul(ps_sc[:, (scnt % 2) * 512:(scnt % 2) * 512 + 128],
                                     lhsb[0:2, st * 128:(st + 1) * 128],
                                     rhs2[0:2, tt * 128:(tt + 1) * 128],
                                     start=True, stop=True).then_inc(sm["sc"], 1)
                        if k >= 1:
                            t.wait_ge(sm["oe"], 2 * k)
                        for st in range(tt + 1):
                            scnt = b * 136 + tt * (tt + 1) // 2 + st
                            if st == tt:
                                t.wait_ge(sm["pm"], k + 1)
                            else:
                                t.wait_ge(sm["pf"], scnt + 1)
                            t.matmul(ps_d[:, 0:1], pch[:, st * 128:(st + 1) * 128],
                                     onesb[:, :], start=(st == 0), stop=(st == tt))
                            for half in range(2):
                                ins = t.matmul(ps_av[:, half * ES:(half + 1) * ES],
                                               pch[:, st * 128:(st + 1) * 128],
                                               valc[:, st * 1024 + half * ES:st * 1024 + (half + 1) * ES],
                                               start=(st == 0), stop=(st == tt))
                            if st == tt:
                                ins.then_inc(sm["dn"], 1)
                        t.wait_ge(sm["oe"], 2 * k + 2)
                        for kc in range(KC):
                            ins = t.transpose(ps_tp[:, kc * 128:(kc + 1) * 128],
                                              ob[:, kc * 128:(kc + 1) * 128],
                                              eyeb[:, :])
                        ins.then_inc(sm["ot"], 1)
                        t.wait_ge(sm["oc"], k + 1)
                        if k >= 1:
                            t.wait_ge(sm["yo"], k)
                        for half in range(2):
                            for kc in range(KC):
                                ins = t.matmul(ps_p[:, half * ES:(half + 1) * ES],
                                               oT[:, kc * 128:(kc + 1) * 128],
                                               wosb[:, kc * 1024 + half * ES:kc * 1024 + (half + 1) * ES],
                                               start=(kc == 0), stop=(kc == KC - 1))
                            ins.then_inc(sm["pj"], 1)

            @blk.scalar
            def _(sc_raw):
                sc = SelfSync(sc_raw, sm["sch"])
                for b in range(BLOC):
                    for tt in range(NTT):
                        k = b * NTT + tt
                        sc.wait_ge(sm["x%d" % (tt % 2)], 16 * (b * 16 + tt // 2 + 1))
                        xa = xtile[:, (tt % 2) * 1024:(tt % 2 + 1) * 1024]
                        sc.activation(sjunk[:, :], xa, AF.Square,
                                      accum_out=stat[:, 1:2])
                        sc.wait_ge(sm["vb"], k + 1)
                        sc.activation(stat[:, 3:4], stat[:, 1:2], AF.Sqrt,
                                      bias=stat[:, 2:3], scale=1.0 / 1024.0)
                        sc.sig(sm["sd"], k + 1)
                        for half in range(2):
                            sc.wait_ge(sm["sc"], b * 168 + tt * 2 + half + 1)
                            sc.activation(sjunk[:, half * ES:(half + 1) * ES],
                                          ps_av[:, half * ES:(half + 1) * ES], AF.Square,
                                          accum_out=stat[:, 8 + half:9 + half])
                            sc.sig(sm["ye"], 2 * k + half + 1)
                    for tt in range(NTT):
                        k = b * NTT + tt
                        for st in range(tt + 1):
                            local = tt * (tt + 1) // 2 + st
                            scnt = b * 136 + local
                            sc.wait_ge(sm["sc"], 32 * (b + 1) + b * 136 + local + 1)
                            sc.activation(pch[:, st * 128:(st + 1) * 128],
                                          ps_sc[:, (scnt % 2) * 512:(scnt % 2) * 512 + 128],
                                          AF.Exp, bias=0.0, scale=1.0)
                            sc.sig(sm["pf"], scnt + 1)

            @blk.vector
            def _(v_raw):
                v = SelfSync(v_raw, sm["vch"])
                v.memset(onesb[:, :], 1.0)
                v.memset(lhsb[0:2, :], -1.0)
                v.sig(sm["rc2"], 1)
                for b in range(BLOC):
                    for tt in range(NTT):
                        k = b * NTT + tt
                        xa = xtile[:, (tt % 2) * 1024:(tt % 2 + 1) * 1024]
                        v.wait_ge(sm["x%d" % (tt % 2)], 16 * (b * 16 + tt // 2 + 1))
                        v.tensor_scalar(vjunk[:, :], xa, 1.0, None, ALU.mult, ALU.add,
                                        accum_out=stat[:, 0:1])
                        v.tensor_scalar(stat[:, 4:5], stat[:, 0:1], 1.0 / 1024.0, None, ALU.mult)
                        v.tensor_tensor(stat[:, 5:6], stat[:, 4:5], stat[:, 4:5], ALU.mult)
                        v.tensor_scalar(stat[:, 2:3], stat[:, 5:6], -1.0, 1e-5, ALU.mult, ALU.add)
                        v.sig(sm["vb"], k + 1)
                        v.wait_ge(sm["sd"], k + 1)
                        v.reciprocal(stat[:, 6:7], stat[:, 3:4])
                        v.tensor_scalar(xlnb[:, (tt % 2) * 1024:(tt % 2 + 1) * 1024], xa,
                                        stat[:, 4:5], stat[:, 6:7], ALU.subtract, ALU.mult)
                        v.sig(sm["ln"], k + 1)
                        v.wait_ge(sm["tp"], k + 1)
                        v.tensor_copy(
                            xT.rearrange("p (kc u) -> p kc u", kc=KC)[:, :, tt * 128:(tt + 1) * 128],
                            ps_tp[:, :].rearrange("p (c u) -> p c u", c=KC))
                        v.sig(sm["xT"], k + 1)
                        v.wait_ge(sm["ye"], 2 * k + 2)
                        v.tensor_tensor(stat[:, 10:11], stat[:, 8:9], stat[:, 9:10], ALU.add)
                        v.tensor_scalar(scc[:, tt:tt + 1], stat[:, 10:11], 1.0 / 32.0, None, ALU.mult)
                        v.sig(sm["scv"], k + 1)
                        for half in range(2):
                            v.wait_ge(sm["yv"], 2 * k + half + 1)
                            v.tensor_copy(valc[:, tt * 1024 + half * ES:tt * 1024 + (half + 1) * ES],
                                          ps_p[:, half * ES:(half + 1) * ES])
                            v.sig(sm["ve"], 2 * k + half + 1)
                    v.wait_ge(sm["srt"], b + 1)
                    v.tensor_copy(scr[:, :], ps_d[0:16, 0:128])
                    v.sig(sm["src"], b + 1)
                    v.wait_ge(sm["srd"], 48 * b + 48)
                    v.tensor_tensor_scan(rmax[0:1, :], rhs2[0:1, :], rhs2[0:1, :], 0.0,
                                         ALU.max, ALU.max)
                    v.tensor_tensor(mrow[0:1, :], rmax[0:1, :], rhs2[0:1, :], ALU.mult)
                    v.sig(sm["rmx"], b + 1)
                    for tt in range(NTT):
                        k = b * NTT + tt
                        scnt_d = b * 136 + tt * (tt + 1) // 2 + tt
                        v.wait_ge(sm["pf"], scnt_d + 1)
                        v.tensor_tensor(pch[:, tt * 128:(tt + 1) * 128],
                                        pch[:, tt * 128:(tt + 1) * 128], trilb[:, :], ALU.mult)
                        v.sig(sm["pm"], k + 1)
                        v.wait_ge(sm["dn"], k + 1)
                        v.reciprocal(dcol[:, 0:1], ps_d[:, 0:1])
                        for half in range(2):
                            v.tensor_scalar(ob[:, half * ES:(half + 1) * ES],
                                            ps_av[:, half * ES:(half + 1) * ES],
                                            dcol[:, 0:1], None, ALU.mult)
                            v.sig(sm["oe"], 2 * k + half + 1)
                        v.wait_ge(sm["ot"], k + 1)
                        v.tensor_copy(oT[:, :].rearrange("p (c u) -> p c u", c=KC),
                                      ps_tp[:, :].rearrange("p (c u) -> p c u", c=KC))
                        v.sig(sm["oc"], k + 1)
                        if k >= 2:
                            v.wait_ge(sm["yd%d" % (tt % 2)], 16 * (k // 2))
                        v.wait_ge(sm["x%d" % (tt % 2)], 16 * (b * 16 + 8 + tt // 2 + 1))
                        for half in range(2):
                            v.wait_ge(sm["pj"], 2 * k + half + 1)
                            v.tensor_tensor(otile[:, (tt % 2) * 1024 + half * ES:(tt % 2) * 1024 + (half + 1) * ES],
                                            ps_p[:, half * ES:(half + 1) * ES],
                                            xtile[:, (tt % 2) * 1024 + half * ES:(tt % 2) * 1024 + (half + 1) * ES],
                                            ALU.add)
                        v.sig(sm["yo"], k + 1)


def _window_indices(seq_len, window=7):
    half = window // 2
    idx = np.zeros((seq_len, window), dtype=np.int32)
    for i in range(seq_len):
        start = max(0, i - half)
        end = min(seq_len, i + half + 1)
        win = np.arange(start, end, dtype=np.int32)
        pad = window - len(win)
        if pad > 0:
            if start == 0:
                win = np.concatenate([win, np.full(pad, seq_len - 1, dtype=np.int32)])
            else:
                win = np.concatenate([np.zeros(pad, dtype=np.int32), win])
        idx[i] = win
    return idx


def emit_l1_attn_a(p, xin, xout, WqkvT, b_qkv, WlatT, b_lat, WprojT, b_proj, tag="la"):
    nc = p.nc
    M_STAB = 8.0
    SCALE = 0.125
    H = 16

    eyeb_in = p.const_input(tag + "_eyeb", np.eye(128, dtype=BFDT))
    onesr_in = p.const_input(tag + "_onesr", np.ones((1, 128), np.float32))
    cpool_ap, cpool_in = p.reg_const(tag, [-M_STAB])

    widx = _window_indices(T)
    mlist = []
    mslot = {}
    for tt in range(NTT):
        for dlt in (-1, 0, 1):
            jt = tt + dlt
            if jt < 0 or jt >= NTT:
                continue
            M = np.zeros((128, 128), np.float32)
            for sl in range(128):
                for j in widx[tt * 128 + sl]:
                    jl = j - jt * 128
                    if 0 <= jl < 128:
                        M[jl, sl] += 1.0
            mslot[(tt, dlt)] = len(mlist)
            mlist.append(M.astype(BFDT))
    NM = len(mlist)
    masks_in = p.const_input(tag + "_masks", np.concatenate(mlist, axis=1))
    ex0 = np.array([[float((widx[sl] == T - 1).sum()) for sl in range(128)]], np.float32)
    ex15 = np.array([[float((widx[15 * 128 + sl] == 0).sum()) for sl in range(128)]], np.float32)
    ex0_in = p.const_input(tag + "_ex0", ex0.astype(BFDT))
    ex15_in = p.const_input(tag + "_ex15", ex15.astype(BFDT))

    def slot_of(t):
        return 3 if t == 0 else (4 if t == 1 else t % 3)

    def qslot_of(t):
        return 2 if t == 0 else t % 2

    attn_order = list(range(1, NTT)) + [0]

    sm = {n: p.sem(tag + n) for n in
          ["w", "x0", "x1", "vb", "sd", "ln", "tp", "xT", "qm", "qe",
           "at", "ae", "am", "dn", "oe", "ot", "oc",
           "pj", "yo", "yd0", "yd1", "rx0", "rx1", "vx0", "vx1", "pro",
           "rc", "rc2", "vch", "sch"]}

    with ExitStack() as _es:
        def A(nm, shape, dt):
            return _es.enter_context(nc.sbuf_tensor(tag + nm, shape, dt))

        xtile = A("xtile", [128, 2 * 1024], F32)
        xres = A("xres", [128, 2 * 1024], F32)
        sjunk = A("sjunk", [128, 1024], F32)
        vjunk = A("vjunk", [128, 1024], F32)
        stat = A("stat", [128, 16], F32)
        xlnb = A("xlnb", [128, 2 * 1024], BF16)
        xTt = A("xTt", [128, 2 * KC * 128], BF16)
        wqsb = A("wqsb", [128, KC * 3072], BF16)
        wlsb = A("wlsb", [128, 64], BF16)
        wpsb = A("wpsb", [128, KC * 1024], BF16)
        bqcol = A("bqcol", [128, 24], F32)
        blcol2 = A("blcol2", [128, 1], F32)
        blrow = A("blrow", [1, 64], F32)
        bprow = A("bprow", [1, 1024], F32)
        blrep = A("blrep", [128, 64], F32)
        bprep = A("bprep", [128, 1024], F32)
        onesr = A("onesr", [1, 128], F32)
        onescol = A("onescol", [128, 1], BF16)
        kTt = A("kTt", [128, KC * 128], BF16)
        vTt = A("vTt", [128, KC * 128], BF16)
        qTs = A("qTs", [128, KC * 3 * 128], BF16)
        klr = A("klr", [128, KC * 5 * 128], BF16)
        vlr = A("vlr", [128, 5 * 1024], BF16)
        vex0 = A("vex0", [1, 1024], BF16)
        vex1 = A("vex1", [1, 1024], BF16)
        masks = A("masks", [128, NM * 128], BF16)
        exm0 = A("exm0", [1, 128], BF16)
        exm15 = A("exm15", [1, 128], BF16)
        pt = A("pt", [128, 2 * 512], BF16)
        ptx = A("ptx", [1, 2 * 128], BF16)
        dcol = A("dcol", [128, 4], F32)
        otile = A("otile", [128, 1024], BF16)
        oT = A("oT", [128, KC * 128], BF16)
        ytile = A("ytile", [128, 2 * 1024], F32)
        eyeb = A("eyeb", [128, 128], BF16)
        ps_q = _es.enter_context(nc.psum_tensor(tag + "ps_q", [128, 1024], F32))
        ps_at = _es.enter_context(nc.psum_tensor(tag + "ps_at", [128, 1024], F32))
        ps_o = _es.enter_context(nc.psum_tensor(tag + "ps_o", [128, 512], F32))
        ps_dn = _es.enter_context(nc.psum_tensor(tag + "ps_dn", [128, 256], F32))
        ps_tp = _es.enter_context(nc.psum_tensor(tag + "ps_tp", [128, 1024], BF16))

        def attn_tiles(ta):
            jts = [jt for jt in (ta - 1, ta, ta + 1) if 0 <= jt < NTT]
            return jts, ta == 0, ta == NTT - 1

        with nc.Block() as blk:
            @blk.sync
            def _(s):
                nrc = [0]

                def cdma(dst, src):
                    s.dma_start(out=dst, in_=src).then_inc(sm["rc"], 16)
                    nrc[0] += 16
                cdma(eyeb[:, :], eyeb_in[:, :])
                cdma(masks[:, :], masks_in[:, :])
                cdma(exm0[0:1, :], ex0_in[:, :])
                cdma(exm15[0:1, :], ex15_in[:, :])
                cdma(bqcol[:, :], b_qkv[:, :])
                cdma(blcol2[0:64, 0], b_lat[:])
                cdma(blcol2[64:128, 0], b_lat[:])
                cdma(blrow[0:1, :], b_lat.rearrange("(a j) -> a j", a=1))
                cdma(bprow[0:1, :], b_proj.rearrange("(a j) -> a j", a=1))
                cdma(onesr[0:1, :], onesr_in[:, :])
                cdma(cpool_ap[:, :], cpool_in[:, :])
                RC_TOTAL = nrc[0]
                for b in range(BLOC):
                    for step in range(NTT + 2):
                        if step < NTT:
                            tt = step
                            k = b * NTT + tt
                            if k >= 2:
                                s.wait_ge(sm["ln"], k - 1)
                            s.dma_start(out=xtile[:, (tt % 2) * 1024:(tt % 2 + 1) * 1024],
                                        in_=xin[k * 128:(k + 1) * 128, :]).then_inc(sm["x%d" % (tt % 2)], 16)
                        if step >= 2:
                            ai = step - 2
                            ta = attn_order[ai]
                            if ai == 13:
                                if b >= 1:
                                    s.wait_ge(sm["dn"], b * NTT * H)
                                s.wait_ge(sm["qe"], (b * NTT + 1) * 56)
                                s.dma_start(out=vex1[0:1, :],
                                            in_=vlr[0:1, 3 * 1024:4 * 1024]).then_inc(sm["vx1"], 16)
                            if ai == 15:
                                s.wait_ge(sm["qe"], (b * NTT + NTT) * 56)
                                s.dma_start(out=vex0[0:1, :],
                                            in_=vlr[127:128, slot_of(NTT - 1) * 1024:(slot_of(NTT - 1) + 1) * 1024]).then_inc(sm["vx0"], 16)
                            kat = b * NTT + ai
                            kr = b * NTT + ta
                            if ai >= 2:
                                s.wait_ge(sm["yo"], kat - 1)
                            s.dma_start(out=xres[:, (ai % 2) * 1024:(ai % 2 + 1) * 1024],
                                        in_=xin[kr * 128:(kr + 1) * 128, :]).then_inc(sm["rx%d" % (ai % 2)], 16)
                            s.wait_ge(sm["yo"], kat + 1)
                            s.dma_start(out=xout[kr * 128:(kr + 1) * 128, :],
                                        in_=ytile[:, (ai % 2) * 1024:(ai % 2 + 1) * 1024]).then_inc(sm["yd%d" % (ai % 2)], 16)
                s.wait_ge(sm["yd0"], 16 * (BLOC * NTT // 2))
                s.wait_ge(sm["yd1"], 16 * (BLOC * NTT // 2))

            @blk.gpsimd
            def _(g):
                g.dma_start(out=wqsb.rearrange("q (kc c) -> q kc c", kc=KC),
                            in_=WqkvT.rearrange("(kc q) c -> q kc c", q=128)).then_inc(sm["w"], 16)
                g.dma_start(out=wlsb[0:64, :], in_=WlatT[:, :]).then_inc(sm["w"], 16)
                g.dma_start(out=wlsb[64:128, :], in_=WlatT[:, :]).then_inc(sm["w"], 16)
                g.dma_start(out=wpsb.rearrange("q (kc c) -> q kc c", kc=KC),
                            in_=WprojT.rearrange("(kc q) c -> q kc c", q=128)).then_inc(sm["w"], 16)
                g.wait_ge(sm["w"], 64)

            @blk.tensor
            def _(t):
                t.wait_ge(sm["rc"], 176)
                t.wait_ge(sm["rc2"], 1)
                t.wait_ge(sm["w"], 64)
                t.matmul(ps_q[:, 0:64], onesr[0:1, :], blrow[0:1, :], start=True, stop=True)
                t.matmul(ps_q[:, 512:1024], onesr[0:1, :], bprow[0:1, 0:512], start=True, stop=True)
                t.matmul(ps_at[:, 0:512], onesr[0:1, :], bprow[0:1, 512:1024], start=True, stop=True).then_inc(sm["pro"], 1)
                t.wait_ge(sm["pro"], 2)
                for b in range(BLOC):
                    for step in range(NTT + 2):
                        if step < NTT:
                            tt = step
                            k = b * NTT + tt
                            t.wait_ge(sm["ln"], k + 1)
                            if k >= 2:
                                t.wait_ge(sm["xT"], k - 1)
                            for kc in range(KC):
                                ins = t.transpose(ps_tp[:, kc * 128:(kc + 1) * 128],
                                                  xlnb[:, (tt % 2) * 1024 + kc * 128:(tt % 2) * 1024 + (kc + 1) * 128],
                                                  eyeb[:, :])
                            ins.then_inc(sm["tp"], 1)
                            t.wait_ge(sm["xT"], k + 1)
                            for cch in range(24):
                                it56 = k * 56 + cch
                                if it56 >= 2:
                                    t.wait_ge(sm["qe"], it56 - 1)
                                for kc in range(KC):
                                    ins = t.matmul(ps_q[:, (it56 % 2) * 512:(it56 % 2) * 512 + 128],
                                                   wqsb[:, kc * 3072 + cch * 128:kc * 3072 + (cch + 1) * 128],
                                                   xTt[:, (tt % 2) * KC * 128 + kc * 128:(tt % 2) * KC * 128 + (kc + 1) * 128],
                                                   start=(kc == 0), stop=(kc == KC - 1))
                                ins.then_inc(sm["qm"], 1)
                            t.wait_ge(sm["qe"], k * 56 + 16)
                            for h in range(H):
                                it56 = k * 56 + 24 + h
                                t.wait_ge(sm["qe"], it56 - 1)
                                pb = (h % 2) * 64
                                t.matmul(ps_q[0:64, (it56 % 2) * 512:(it56 % 2) * 512 + 128],
                                         wlsb[pb:pb + 64, :],
                                         kTt[pb:pb + 64, (h // 2) * 128:(h // 2 + 1) * 128],
                                         start=True, stop=True).then_inc(sm["qm"], 1)
                            t.wait_ge(sm["qe"], k * 56 + 24)
                            for h in range(H):
                                it56 = k * 56 + 40 + h
                                t.wait_ge(sm["qe"], it56 - 1)
                                pb = (h % 2) * 64
                                t.matmul(ps_q[:, (it56 % 2) * 512:(it56 % 2) * 512 + 64],
                                         vTt[pb:pb + 64, (h // 2) * 128:(h // 2 + 1) * 128],
                                         wlsb[pb:pb + 64, :],
                                         start=True, stop=True).then_inc(sm["qm"], 1)
                        if step >= 2:
                            ai = step - 2
                            ta = attn_order[ai]
                            kat = b * NTT + ai
                            jts, hx0, hx15 = attn_tiles(ta)
                            qsl = qslot_of(ta)
                            need = NTT if ta == 0 else max(jts) + 1
                            t.wait_ge(sm["qe"], (b * NTT + need) * 56)
                            if hx0:
                                t.wait_ge(sm["vx0"], 16 * (b + 1))
                            if hx15:
                                t.wait_ge(sm["vx1"], 16 * (b + 1))
                            if kat >= 1:
                                t.wait_ge(sm["yo"], kat)
                            for h in range(H):
                                hg = kat * H + h
                                pb = (h % 2) * 64
                                if hg >= 2:
                                    t.wait_ge(sm["ae"], hg - 1)
                                for ji, jt in enumerate(jts):
                                    sl = slot_of(jt)
                                    ins = t.matmul(ps_at[:, (hg % 2) * 512 + ji * 128:(hg % 2) * 512 + (ji + 1) * 128],
                                                   klr[pb:pb + 64, (h // 2) * 640 + sl * 128:(h // 2) * 640 + (sl + 1) * 128],
                                                   qTs[pb:pb + 64, (h // 2) * 384 + qsl * 128:(h // 2) * 384 + (qsl + 1) * 128],
                                                   start=True, stop=True)
                                    ins.then_inc(sm["at"], 1)
                                if hx0 or hx15:
                                    exsl = slot_of(NTT - 1) if hx0 else slot_of(0)
                                    excol = 127 if hx0 else 0
                                    c0 = (h // 2) * 640 + exsl * 128 + excol
                                    t.matmul(ps_at[0:1, (hg % 2) * 512 + 384:(hg % 2) * 512 + 512],
                                             klr[pb:pb + 64, c0:c0 + 1],
                                             qTs[pb:pb + 64, (h // 2) * 384 + qsl * 128:(h // 2) * 384 + (qsl + 1) * 128],
                                             start=True, stop=True).then_inc(sm["at"], 1)
                                t.wait_ge(sm["am"], hg + 1)
                                if hg >= 2:
                                    t.wait_ge(sm["oe"], hg - 1)
                                nj = len(jts)
                                hasex = hx0 or hx15
                                for ji, jt in enumerate(jts):
                                    sl = slot_of(jt)
                                    last = (ji == nj - 1) and not hasex
                                    t.matmul(ps_o[:, (hg % 2) * 256:(hg % 2) * 256 + 64],
                                             pt[:, (hg % 2) * 512 + ji * 128:(hg % 2) * 512 + (ji + 1) * 128],
                                             vlr[:, sl * 1024 + h * 64:sl * 1024 + (h + 1) * 64],
                                             start=(ji == 0), stop=last)
                                    ins = t.matmul(ps_dn[:, (hg % 2) * 128:(hg % 2) * 128 + 1],
                                                   pt[:, (hg % 2) * 512 + ji * 128:(hg % 2) * 512 + (ji + 1) * 128],
                                                   onescol[:, :],
                                                   start=(ji == 0), stop=last)
                                if hasex:
                                    vexb = vex0 if hx0 else vex1
                                    t.matmul(ps_o[:, (hg % 2) * 256:(hg % 2) * 256 + 64],
                                             ptx[0:1, (hg % 2) * 128:(hg % 2 + 1) * 128],
                                             vexb[0:1, h * 64:(h + 1) * 64],
                                             start=False, stop=True)
                                    ins = t.matmul(ps_dn[:, (hg % 2) * 128:(hg % 2) * 128 + 1],
                                                   ptx[0:1, (hg % 2) * 128:(hg % 2 + 1) * 128],
                                                   onescol[0:1, :],
                                                   start=False, stop=True)
                                ins.then_inc(sm["dn"], 1)
                            t.wait_ge(sm["oe"], kat * H + H)
                            for kc in range(KC):
                                ins = t.transpose(ps_tp[:, kc * 128:(kc + 1) * 128],
                                                  otile[:, kc * 128:(kc + 1) * 128],
                                                  eyeb[:, :])
                            ins.then_inc(sm["ot"], 1)
                            t.wait_ge(sm["oc"], kat + 1)
                            for half in range(2):
                                for kc in range(KC):
                                    ins = t.matmul(ps_at[:, half * 512:(half + 1) * 512],
                                                   oT[:, kc * 128:(kc + 1) * 128],
                                                   wpsb[:, kc * 1024 + half * 512:kc * 1024 + (half + 1) * 512],
                                                   start=(kc == 0), stop=(kc == KC - 1))
                                ins.then_inc(sm["pj"], 1)

            @blk.scalar
            def _(sc_raw):
                sc = SelfSync(sc_raw, sm["sch"])
                atcum = [0]
                for b in range(BLOC):
                    for step in range(NTT + 2):
                        if step < NTT:
                            tt = step
                            k = b * NTT + tt
                            sc.wait_ge(sm["x%d" % (tt % 2)], 16 * (b * 8 + tt // 2 + 1))
                            xa = xtile[:, (tt % 2) * 1024:(tt % 2 + 1) * 1024]
                            sc.activation(sjunk[:, :], xa, AF.Square,
                                          accum_out=stat[:, 1:2])
                            sc.wait_ge(sm["vb"], k + 1)
                            sc.activation(stat[:, 3:4], stat[:, 1:2], AF.Sqrt,
                                          bias=stat[:, 2:3], scale=1.0 / 1024.0)
                            sc.sig(sm["sd"], k + 1)
                            qsl = qslot_of(tt)
                            sc.wait_ge(sm["qe"], k * 56)
                            for cch in range(24):
                                it56 = k * 56 + cch
                                sc.wait_ge(sm["qm"], it56 + 1)
                                src56 = ps_q[:, (it56 % 2) * 512:(it56 % 2) * 512 + 128]
                                if cch < KC:
                                    dst = qTs[:, cch * 384 + qsl * 128:cch * 384 + (qsl + 1) * 128]
                                elif cch < 16:
                                    dst = kTt[:, (cch - 8) * 128:(cch - 7) * 128]
                                else:
                                    dst = vTt[:, (cch - 16) * 128:(cch - 15) * 128]
                                sc.activation(dst, src56, AF.Identity, bias=bqcol[:, cch:cch + 1])
                                sc.sig(sm["qe"], it56 + 1)
                            sl = slot_of(tt)
                            for h in range(H):
                                it56 = k * 56 + 24 + h
                                sc.wait_ge(sm["qm"], it56 + 1)
                                pb = (h % 2) * 64
                                sc.activation(klr[pb:pb + 64, (h // 2) * 640 + sl * 128:(h // 2) * 640 + (sl + 1) * 128],
                                              ps_q[0:64, (it56 % 2) * 512:(it56 % 2) * 512 + 128],
                                              AF.Identity, bias=blcol2[pb:pb + 64, 0:1])
                                sc.sig(sm["qe"], it56 + 1)
                        if step >= 2:
                            ai = step - 2
                            ta = attn_order[ai]
                            kat = b * NTT + ai
                            jts, hx0, hx15 = attn_tiles(ta)
                            cnt_h = len(jts) + (1 if (hx0 or hx15) else 0)
                            for h in range(H):
                                hg = kat * H + h
                                atcum[0] += cnt_h
                                sc.wait_ge(sm["at"], atcum[0])
                                nev = len(jts) + (1 if (hx0 or hx15) else 0)
                                for ji in range(len(jts)):
                                    sc.activation(pt[:, (hg % 2) * 512 + ji * 128:(hg % 2) * 512 + (ji + 1) * 128],
                                                  ps_at[:, (hg % 2) * 512 + ji * 128:(hg % 2) * 512 + (ji + 1) * 128],
                                                  AF.Exp, bias=-M_STAB, scale=SCALE)
                                    if ji == len(jts) - 1 and not (hx0 or hx15):
                                        sc.sig(sm["ae"], hg + 1)
                                if hx0 or hx15:
                                    sc.activation(ptx[0:1, (hg % 2) * 128:(hg % 2 + 1) * 128],
                                                  ps_at[0:1, (hg % 2) * 512 + 384:(hg % 2) * 512 + 512],
                                                  AF.Exp, bias=-M_STAB, scale=SCALE)
                                    sc.sig(sm["ae"], hg + 1)

            @blk.vector
            def _(v_raw):
                v = SelfSync(v_raw, sm["vch"])
                v.memset(onescol[:, :], 1.0)
                v.sig(sm["rc2"], 1)
                v.wait_ge(sm["pro"], 1)
                v.tensor_copy(blrep[:, :], ps_q[:, 0:64])
                v.tensor_copy(bprep[:, 0:512], ps_q[:, 512:1024])
                v.tensor_copy(bprep[:, 512:1024], ps_at[:, 0:512])
                v.sig(sm["pro"], 2)
                for b in range(BLOC):
                    for step in range(NTT + 2):
                        if step < NTT:
                            tt = step
                            k = b * NTT + tt
                            xa = xtile[:, (tt % 2) * 1024:(tt % 2 + 1) * 1024]
                            v.wait_ge(sm["x%d" % (tt % 2)], 16 * (b * 8 + tt // 2 + 1))
                            v.tensor_scalar(vjunk[:, :], xa, 1.0, None, ALU.mult, ALU.add,
                                            accum_out=stat[:, 0:1])
                            v.tensor_scalar(stat[:, 4:5], stat[:, 0:1], 1.0 / 1024.0, None, ALU.mult)
                            v.tensor_tensor(stat[:, 5:6], stat[:, 4:5], stat[:, 4:5], ALU.mult)
                            v.tensor_scalar(stat[:, 2:3], stat[:, 5:6], -1.0, 1e-5, ALU.mult, ALU.add)
                            v.sig(sm["vb"], k + 1)
                            v.wait_ge(sm["sd"], k + 1)
                            v.reciprocal(stat[:, 6:7], stat[:, 3:4])
                            v.tensor_scalar(xlnb[:, (tt % 2) * 1024:(tt % 2 + 1) * 1024], xa,
                                            stat[:, 4:5], stat[:, 6:7], ALU.subtract, ALU.mult)
                            v.sig(sm["ln"], k + 1)
                            v.wait_ge(sm["tp"], k + 1)
                            v.tensor_copy(
                                xTt[:, (tt % 2) * KC * 128:(tt % 2 + 1) * KC * 128].rearrange("p (kc u) -> p kc u", kc=KC),
                                ps_tp[:, :].rearrange("p (c u) -> p c u", c=KC))
                            v.sig(sm["xT"], k + 1)
                            sl = slot_of(tt)
                            v.wait_ge(sm["qe"], k * 56 + 40)
                            for h in range(H):
                                it56 = k * 56 + 40 + h
                                v.wait_ge(sm["qm"], it56 + 1)
                                v.tensor_tensor(vlr[:, sl * 1024 + h * 64:sl * 1024 + (h + 1) * 64],
                                                ps_q[:, (it56 % 2) * 512:(it56 % 2) * 512 + 64],
                                                blrep[:, :], ALU.add)
                                v.sig(sm["qe"], it56 + 1)
                        if step >= 2:
                            ai = step - 2
                            ta = attn_order[ai]
                            kat = b * NTT + ai
                            jts, hx0, hx15 = attn_tiles(ta)
                            for h in range(H):
                                hg = kat * H + h
                                v.wait_ge(sm["ae"], hg + 1)
                                for ji, jt in enumerate(jts):
                                    dlt = jt - ta
                                    msl = mslot[(ta, dlt)]
                                    v.tensor_tensor(pt[:, (hg % 2) * 512 + ji * 128:(hg % 2) * 512 + (ji + 1) * 128],
                                                    pt[:, (hg % 2) * 512 + ji * 128:(hg % 2) * 512 + (ji + 1) * 128],
                                                    masks[:, msl * 128:(msl + 1) * 128], ALU.mult)
                                if hx0 or hx15:
                                    exb = exm0 if hx0 else exm15
                                    v.tensor_tensor(ptx[0:1, (hg % 2) * 128:(hg % 2 + 1) * 128],
                                                    ptx[0:1, (hg % 2) * 128:(hg % 2 + 1) * 128],
                                                    exb[0:1, :], ALU.mult)
                                v.sig(sm["am"], hg + 1)
                                v.wait_ge(sm["dn"], hg + 1)
                                v.reciprocal(dcol[:, (hg % 2):(hg % 2) + 1],
                                             ps_dn[:, (hg % 2) * 128:(hg % 2) * 128 + 1])
                                v.tensor_scalar(otile[:, h * 64:(h + 1) * 64],
                                                ps_o[:, (hg % 2) * 256:(hg % 2) * 256 + 64],
                                                dcol[:, (hg % 2):(hg % 2) + 1], None, ALU.mult)
                                v.sig(sm["oe"], hg + 1)
                            v.wait_ge(sm["ot"], kat + 1)
                            v.tensor_copy(oT[:, :].rearrange("p (c u) -> p c u", c=KC),
                                          ps_tp[:, :].rearrange("p (c u) -> p c u", c=KC))
                            v.sig(sm["oc"], kat + 1)
                            v.wait_ge(sm["rx%d" % (ai % 2)], 16 * (b * 8 + ai // 2 + 1))
                            if kat >= 2:
                                v.wait_ge(sm["yd%d" % (ai % 2)], 16 * (b * 8 + ai // 2))
                            for half in range(2):
                                v.wait_ge(sm["pj"], 2 * kat + half + 1)
                                v.tensor_tensor(vjunk[:, half * 512:(half + 1) * 512],
                                                ps_at[:, half * 512:(half + 1) * 512],
                                                bprep[:, half * 512:(half + 1) * 512], ALU.add)
                                v.tensor_tensor(ytile[:, (ai % 2) * 1024 + half * 512:(ai % 2) * 1024 + (half + 1) * 512],
                                                vjunk[:, half * 512:(half + 1) * 512],
                                                xres[:, (ai % 2) * 1024 + half * 512:(ai % 2) * 1024 + (half + 1) * 512],
                                                ALU.add)
                            v.sig(sm["yo"], kat + 1)


def emit_l1_attn_b(p, xin, xout, WqkvT, WgT, bg, WbT, bb, WoutT, tag="db"):
    nc = p.nc
    H = 16
    BIG = 30000.0

    eyeb_in = p.const_input(tag + "_eyeb", np.eye(128, dtype=BFDT))
    eyef_in = p.const_input(tag + "_eyef", np.eye(128, dtype=np.float32))
    sl_ = np.arange(128)[:, None]
    tl_ = np.arange(128)[None, :]
    triuI_in = p.const_input(tag + "_triuI", (sl_ <= tl_).astype(np.float32))
    triuS_in = p.const_input(tag + "_triuS", (sl_ < tl_).astype(np.float32))
    mbI_in = p.const_input(tag + "_mbI", np.where(sl_ <= tl_, 0.0, -BIG).astype(np.float32))
    mbS_in = p.const_input(tag + "_mbS", np.where(sl_ < tl_, 0.0, -BIG).astype(np.float32))
    o1x128_in = p.const_input(tag + "_o1x128", np.ones((1, 128), np.float32))
    ocol_in = p.const_input(tag + "_ocol", np.ones((128, 1), np.float32))
    o1x64_in = p.const_input(tag + "_o1x64", np.ones((1, 64), np.float32))

    sm = {n: p.sem(tag + n) for n in
          ["w", "x0", "x1", "hm", "he", "yo", "yd0", "yd1", "rc", "vch", "sch"]}

    with ExitStack() as _es:
        def A(nm, shape, dt):
            return _es.enter_context(nc.sbuf_tensor(tag + nm, shape, dt))

        xtile = A("xtile", [128, 2 * 1024], F32)
        sjunk = A("sjunk", [128, 1024], F32)
        vjunk = A("vjunk", [128, 1024], F32)
        stat = A("stat", [128, 48], F32)
        xlnb = A("xlnb", [128, 1024], BF16)
        xlob = A("xlob", [128, 1024], BF16)
        xThi = A("xThi", [128, KC * 128], BF16)
        xTlo = A("xTlo", [128, KC * 128], BF16)
        wqsb = A("wqsb", [128, KC * 3072], BF16)
        wosb = A("wosb", [128, KC * 1024], BF16)
        wab = A("wab", [128, KC * 64], BF16)
        wabf = A("wabf", [128, KC * 32], F32)
        bgrep = A("bgrep", [128, 32], F32)
        bgrow = A("bgrow", [1, 32], F32)
        qkvf = A("qkvf", [128, 3072], F32)
        qf = A("qf", [128, 1024], F32)
        kf = A("kf", [128, 1024], F32)
        qb = A("qb", [128, 1024], BF16)
        kb = A("kb", [128, 1024], BF16)
        ab_all = A("ab_all", [128, 32], F32)
        la_all = A("la_all", [128, 16], F32)
        st_buf = A("st_buf", [64, 2 * 1024], F32)
        ktb = A("ktb", [64, 128], BF16)
        qtb = A("qtb", [64, 128], BF16)
        qtf = A("qtf", [64, 128], F32)
        ktf = A("ktf", [64, 128], F32)
        bkt = A("bkt", [64, 128], F32)
        btar = A("btar", [1, 128], F32)
        lamc = A("lamc", [128, 1], F32)
        lamr = A("lamr", [1, 128], F32)
        gamc1 = A("gamc1", [1, 1], F32)
        gcc = A("gcc", [64, 1], F32)
        gamcol = A("gamcol", [128, 1], F32)
        ggc = A("ggc", [128, 1], F32)
        dmt = A("dmt", [128, 128], F32)
        git = A("git", [128, 128], F32)
        gst = A("gst", [128, 128], F32)
        nwt = A("nwt", [128, 128], F32)
        nut = A("nut", [128, 128], F32)
        qkmt = A("qkmt", [128, 128], F32)
        qk3 = A("qk3", [128, 128], F32)
        brep = A("brep", [128, 128], F32)
        brepn = A("brepn", [128, 128], F32)
        tmpE = A("tmpE", [128, 256], F32)
        ptw = A("ptw", [128, 2 * 128], F32)
        pnat = A("pnat", [128, 128], F32)
        ybuf = A("ybuf", [128, 2 * 64], F32)
        ubuf = A("ubuf", [128, 2 * 64], F32)
        bv = A("bv", [128, 64], F32)
        utb = A("utb", [128, 64], F32)
        ygb = A("ygb", [128, 64], F32)
        zbuf = A("zbuf", [128, 64], F32)
        ynb = A("ynb", [128, 64], F32)
        t3b = A("t3b", [128, 64], F32)
        otile = A("otile", [128, 1024], BF16)
        oT = A("oT", [128, KC * 128], BF16)
        ytile = A("ytile", [128, 2 * 1024], F32)
        eyeb = A("eyeb", [128, 128], BF16)
        eyef = A("eyef", [128, 128], F32)
        triuI = A("triuI", [128, 128], F32)
        triuS = A("triuS", [128, 128], F32)
        mbI = A("mbI", [128, 128], F32)
        mbS = A("mbS", [128, 128], F32)
        o1x128 = A("o1x128", [1, 128], F32)
        ocol = A("ocol", [128, 1], F32)
        o1x64 = A("o1x64", [1, 64], F32)

        ps_tpb = _es.enter_context(nc.psum_tensor(tag + "ps_tpb", [128, 1024], BF16))
        ps_tpf = _es.enter_context(nc.psum_tensor(tag + "ps_tpf", [128, 512], F32))
        ps_r = _es.enter_context(nc.psum_tensor(tag + "ps_r", [128, 512], F32))
        ps_k = _es.enter_context(nc.psum_tensor(tag + "ps_k", [128, 512], F32))
        ps_a = _es.enter_context(nc.psum_tensor(tag + "ps_a", [128, 512], F32))
        ps_q = _es.enter_context(nc.psum_tensor(tag + "ps_q", [128, 1024], F32))

        STEPS = []

        def T(fn):
            STEPS.append(("T", fn))

        def V(fn):
            STEPS.append(("V", fn))

        def S(fn):
            STEPS.append(("S", fn))

        def g1(t):
            return t.matmul(ps_q[:, 0:32], o1x128[0:1, :], bgrow[0:1, :],
                            start=True, stop=True)
        T(g1)

        def g2(v):
            v.tensor_copy(bgrep[:, :], ps_q[:, 0:32])
            for kc in range(KC):
                fs = wabf[:, kc * 32:(kc + 1) * 32]
                v.tensor_copy(wab[:, kc * 64:kc * 64 + 32], fs)
                v.tensor_copy(sjunk[:, 0:32], wab[:, kc * 64:kc * 64 + 32])
                v.tensor_tensor(sjunk[:, 32:64], fs, sjunk[:, 0:32], ALU.subtract)
                v.tensor_copy(wab[:, kc * 64 + 32:kc * 64 + 64], sjunk[:, 32:64])
            return v.tensor_copy(bgrep[:, :], ps_q[:, 0:32])
        V(g2)

        def chunk_steps(b, c):
            k = b * NTT + c
            rd = (c % 2) * 1024
            wr = ((c + 1) % 2) * 1024

            def lnV1(v):
                if c == 0:
                    v.memset(st_buf[:, rd:rd + 1024], 0.0)
                v.wait_ge(sm["x%d" % (c % 2)], 16 * (k // 2 + 1))
                xa = xtile[:, (c % 2) * 1024:(c % 2 + 1) * 1024]
                v.tensor_scalar(vjunk[:, :], xa, 1.0, None, ALU.mult, ALU.add,
                                accum_out=stat[:, 0:1])
                v.tensor_scalar(stat[:, 4:5], stat[:, 0:1], 1.0 / 1024.0, None, ALU.mult)
                v.tensor_tensor(stat[:, 5:6], stat[:, 4:5], stat[:, 4:5], ALU.mult)
                return v.tensor_scalar(stat[:, 2:3], stat[:, 5:6], -1.0, 1e-5,
                                       ALU.mult, ALU.add)
            V(lnV1)

            def lnS1(s):
                s.wait_ge(sm["x%d" % (c % 2)], 16 * (k // 2 + 1))
                xa = xtile[:, (c % 2) * 1024:(c % 2 + 1) * 1024]
                s.activation(sjunk[:, :], xa, AF.Square, accum_out=stat[:, 1:2])
                return s.activation(stat[:, 3:4], stat[:, 1:2], AF.Sqrt,
                                    bias=stat[:, 2:3], scale=1.0 / 1024.0)
            S(lnS1)

            def lnV2(v):
                xa = xtile[:, (c % 2) * 1024:(c % 2 + 1) * 1024]
                v.reciprocal(stat[:, 6:7], stat[:, 3:4])
                v.tensor_scalar(vjunk[:, :], xa, stat[:, 4:5], stat[:, 6:7],
                                ALU.subtract, ALU.mult)
                v.tensor_copy(xlnb[:, :], vjunk[:, :])
                return v.tensor_tensor(xlob[:, :], vjunk[:, :], xlnb[:, :], ALU.subtract)
            V(lnV2)

            def tp1(t):
                for kc in range(KC):
                    ins = t.transpose(ps_tpb[:, kc * 128:(kc + 1) * 128],
                                      xlnb[:, kc * 128:(kc + 1) * 128], eyeb[:, :])
                return ins
            T(tp1)

            def tp1c(v):
                return v.tensor_copy(xThi[:, :].rearrange("p (kc u) -> p kc u", kc=KC),
                                     ps_tpb[:, :].rearrange("p (c2 u) -> p c2 u", c2=KC))
            V(tp1c)

            def tp2(t):
                for kc in range(KC):
                    ins = t.transpose(ps_tpb[:, kc * 128:(kc + 1) * 128],
                                      xlob[:, kc * 128:(kc + 1) * 128], eyeb[:, :])
                return ins
            T(tp2)

            def tp2c(v):
                return v.tensor_copy(xTlo[:, :].rearrange("p (kc u) -> p kc u", kc=KC),
                                     ps_tpb[:, :].rearrange("p (c2 u) -> p c2 u", c2=KC))
            V(tp2c)

            for gq in range(6):
                def qg(t, gq=gq):
                    for kc in range(KC):
                        ins = t.matmul(ps_q[:, (gq % 2) * 512:(gq % 2) * 512 + 512],
                                       xThi[:, kc * 128:(kc + 1) * 128],
                                       wqsb[:, kc * 3072 + gq * 512:kc * 3072 + (gq + 1) * 512],
                                       start=(kc == 0), stop=(kc == KC - 1))
                    return ins
                T(qg)

                def qe(s, gq=gq):
                    return s.activation(qkvf[:, gq * 512:(gq + 1) * 512],
                                        ps_q[:, (gq % 2) * 512:(gq % 2) * 512 + 512],
                                        AF.Identity, bias=0.0, scale=1.0)
                S(qe)

            def abg(t):
                for kc in range(KC):
                    t.matmul(ps_q[:, 0:32], xThi[:, kc * 128:(kc + 1) * 128],
                             wab[:, kc * 64:kc * 64 + 32],
                             start=(kc == 0), stop=False)
                    ins = t.matmul(ps_q[:, 0:32], xTlo[:, kc * 128:(kc + 1) * 128],
                                   wab[:, kc * 64:kc * 64 + 32],
                                   start=False, stop=(kc == KC - 1))
                return ins
            T(abg)

            def abe1(v):
                return v.tensor_tensor(sjunk[:, 0:32], ps_q[:, 0:32], bgrep[:, :], ALU.add)
            V(abe1)

            def abe2(s):
                s.activation(ab_all[:, :], sjunk[:, 0:32], AF.Sigmoid, bias=0.0, scale=1.0)
                return s.activation(la_all[:, :], ab_all[:, 0:16], AF.Ln, bias=0.0, scale=1.0)
            S(abe2)

            for h in range(H):
                def nrmS(s, h=h):
                    s.activation(sjunk[:, 0:64], qkvf[:, h * 64:(h + 1) * 64], AF.Square,
                                 accum_out=stat[:, 8:9])
                    s.activation(stat[:, 9:10], stat[:, 8:9], AF.Sqrt, bias=0.0, scale=1.0)
                    s.activation(sjunk[:, 0:64], qkvf[:, 1024 + h * 64:1024 + (h + 1) * 64],
                                 AF.Square, accum_out=stat[:, 10:11])
                    return s.activation(stat[:, 11:12], stat[:, 10:11], AF.Sqrt,
                                        bias=0.0, scale=1.0)
                S(nrmS)

                def nrmV(v, h=h):
                    v.tensor_scalar(stat[:, 12:13], stat[:, 9:10], 1e-12, None, ALU.max)
                    v.reciprocal(stat[:, 13:14], stat[:, 12:13])
                    v.tensor_scalar(qf[:, h * 64:(h + 1) * 64], qkvf[:, h * 64:(h + 1) * 64],
                                    stat[:, 13:14], None, ALU.mult)
                    v.tensor_copy(qb[:, h * 64:(h + 1) * 64], qf[:, h * 64:(h + 1) * 64])
                    v.tensor_scalar(stat[:, 14:15], stat[:, 11:12], 1e-12, None, ALU.max)
                    v.reciprocal(stat[:, 15:16], stat[:, 14:15])
                    v.tensor_scalar(kf[:, h * 64:(h + 1) * 64],
                                    qkvf[:, 1024 + h * 64:1024 + (h + 1) * 64],
                                    stat[:, 15:16], None, ALU.mult)
                    return v.tensor_copy(kb[:, h * 64:(h + 1) * 64], kf[:, h * 64:(h + 1) * 64])
                V(nrmV)

            for h in range(H):
                qsec = h * 64
                vsec = 2048 + h * 64

                def s1(t, h=h, qsec=qsec):
                    t.transpose(ps_tpb[0:64, 0:128], kb[:, qsec:qsec + 64], eyeb[:, :])
                    t.transpose(ps_tpb[0:64, 128:256], qb[:, qsec:qsec + 64], eyeb[:, :])
                    t.transpose(ps_tpf[0:64, 0:128], qf[:, qsec:qsec + 64], eyef[:, :])
                    t.transpose(ps_tpf[0:64, 128:256], kf[:, qsec:qsec + 64], eyef[:, :])
                    return t.transpose(ps_tpf[0:1, 256:384], ab_all[:, 16 + h:17 + h], eyef[:, :])
                T(s1)

                def s2(v):
                    v.tensor_copy(ktb[:, :], ps_tpb[0:64, 0:128])
                    v.tensor_copy(qtb[:, :], ps_tpb[0:64, 128:256])
                    v.tensor_copy(qtf[:, :], ps_tpf[0:64, 0:128])
                    v.tensor_copy(ktf[:, :], ps_tpf[0:64, 128:256])
                    return v.tensor_copy(btar[0:1, :], ps_tpf[0:1, 256:384])
                V(s2)

                def s3(t, h=h):
                    t.matmul(ps_r[:, 0:1], triuI[:, :], la_all[:, h:h + 1],
                             start=True, stop=True)
                    return t.matmul(ps_r[0:1, 2:3], ocol[:, :], la_all[:, h:h + 1],
                                    start=True, stop=True)
                T(s3)

                def s4(v):
                    return v.tensor_copy(lamc[:, :], ps_r[:, 0:1])
                V(s4)

                def s4b(s):
                    s.activation(gamc1[0:1, :], ps_r[0:1, 2:3], AF.Exp, bias=0.0, scale=1.0)
                    return s.activation(gamcol[:, :], ps_r[:, 0:1], AF.Exp, bias=0.0, scale=1.0)
                S(s4b)

                def s5(t):
                    t.transpose(ps_tpf[0:1, 256:384], lamc[:, :], eyef[:, :])
                    return t.matmul(ps_r[0:64, 4:5], o1x64[0:1, :], gamc1[0:1, :],
                                    start=True, stop=True)
                T(s5)

                def s6(v):
                    v.tensor_copy(lamr[0:1, :], ps_tpf[0:1, 256:384])
                    return v.tensor_copy(gcc[:, :], ps_r[0:64, 4:5])
                V(s6)

                def s7(t):
                    t.matmul(ps_r[:, 128:256], o1x128[0:1, :], lamr[0:1, :],
                             start=True, stop=True)
                    return t.matmul(ps_k[:, 256:384], o1x128[0:1, :], btar[0:1, :],
                                    start=True, stop=True)
                T(s7)

                def s8(v):
                    v.tensor_scalar(dmt[:, :], ps_r[:, 128:256], lamc[:, 0:1], None,
                                    ALU.subtract)
                    v.tensor_copy(brep[:, :], ps_k[:, 256:384])
                    v.tensor_scalar(brepn[:, :], ps_k[:, 256:384], -1.0, None, ALU.mult)
                    v.tensor_tensor(tmpE[:, 0:128], dmt[:, :], mbI[:, :], ALU.add)
                    return v.tensor_tensor(tmpE[:, 128:256], dmt[:, :], mbS[:, :], ALU.add)
                V(s8)

                def s9(s):
                    s.activation(git[:, :], tmpE[:, 0:128], AF.Exp, bias=0.0, scale=1.0)
                    return s.activation(gst[:, :], tmpE[:, 128:256], AF.Exp,
                                        bias=0.0, scale=1.0)
                S(s9)

                def s11(t):
                    t.matmul(ps_k[:, 0:128], ktb[:, :], ktb[:, :], start=True, stop=True)
                    return t.matmul(ps_k[:, 128:256], ktb[:, :], qtb[:, :],
                                    start=True, stop=True)
                T(s11)

                def s12(v, h=h, vsec=vsec):
                    v.tensor_tensor(nwt[:, :], ps_k[:, 0:128], brepn[:, :], ALU.mult)
                    v.tensor_tensor(nwt[:, :], nwt[:, :], triuS[:, :], ALU.mult)
                    v.tensor_tensor(nut[:, :], nwt[:, :], gst[:, :], ALU.mult)
                    v.tensor_tensor(qkmt[:, :], ps_k[:, 128:256], triuI[:, :], ALU.mult)
                    v.tensor_tensor(qk3[:, :], qkmt[:, :], git[:, :], ALU.mult)
                    v.tensor_tensor(bkt[:, :], ktf[:, :], brep[0:64, :], ALU.mult)
                    v.tensor_scalar(bv[:, :], qkvf[:, vsec:vsec + 64],
                                    ab_all[:, 16 + h:17 + h], None, ALU.mult)
                    v.tensor_copy(ptw[:, 0:128], nwt[:, :])
                    return v.tensor_copy(ubuf[:, 0:64], bv[:, :])
                V(s12)

                def s13(t, h=h):
                    return t.matmul(ps_a[:, 128:192], bkt[:, :],
                                    st_buf[:, rd + h * 64:rd + (h + 1) * 64],
                                    start=True, stop=True)
                T(s13)

                def s14(v):
                    return v.tensor_copy(ybuf[:, 0:64], ps_a[:, 128:192])
                V(s14)

                def series(vecbuf):
                    for kk in range(7):
                        cs = kk % 2
                        ns = 1 - cs

                        def sA(t, cs=cs, kk=kk, vecbuf=vecbuf):
                            t.matmul(ps_a[:, 0:64], eyef[:, :],
                                     vecbuf[:, cs * 64:cs * 64 + 64], start=True, stop=False)
                            return t.matmul(ps_a[:, 0:64],
                                            ptw[:, (kk % 2) * 128:(kk % 2) * 128 + 128],
                                            vecbuf[:, cs * 64:cs * 64 + 64],
                                            start=False, stop=True)
                        T(sA)

                        def sB(v, ns=ns, vecbuf=vecbuf):
                            return v.tensor_copy(vecbuf[:, ns * 64:ns * 64 + 64],
                                                 ps_a[:, 0:64])
                        V(sB)

                        if kk < 6:
                            def sC(t, kk=kk):
                                return t.transpose(ps_tpf[:, 0:128],
                                                   ptw[:, (kk % 2) * 128:(kk % 2) * 128 + 128],
                                                   eyef[:, :])
                            T(sC)

                            def sD(v):
                                return v.tensor_copy(pnat[:, :], ps_tpf[:, 0:128])
                            V(sD)

                            def sE(t, kk=kk):
                                return t.matmul(ps_a[:, 256:384], pnat[:, :],
                                                ptw[:, (kk % 2) * 128:(kk % 2) * 128 + 128],
                                                start=True, stop=True)
                            T(sE)

                            def sF(v, kk=kk):
                                return v.tensor_copy(
                                    ptw[:, ((kk + 1) % 2) * 128:((kk + 1) % 2) * 128 + 128],
                                    ps_a[:, 256:384])
                            V(sF)

                series(ybuf)

                def sw(v):
                    return v.tensor_copy(ptw[:, 0:128], nut[:, :])
                V(sw)

                series(ubuf)
                yf = 64
                uf = 64

                def s17(v, yf=yf, uf=uf):
                    v.tensor_scalar(utb[:, :], ubuf[:, uf:uf + 64], git[:, 127:128],
                                    None, ALU.mult)
                    v.tensor_tensor(ggc[:, :], git[:, 127:128], gamcol[:, :], ALU.mult)
                    v.tensor_scalar(ygb[:, :], ybuf[:, yf:yf + 64], ggc[:, 0:1],
                                    None, ALU.mult)
                    v.tensor_tensor(zbuf[:, :], utb[:, :], ygb[:, :], ALU.subtract)
                    return v.tensor_scalar(ynb[:, :], ybuf[:, yf:yf + 64], -1.0,
                                           None, ALU.mult)
                V(s17)

                def s18(t, h=h, qsec=qsec, uf=uf):
                    t.matmul(ps_a[:, 192:256], qtf[:, :],
                             st_buf[:, rd + h * 64:rd + (h + 1) * 64],
                             start=True, stop=False)
                    t.matmul(ps_a[:, 192:256], qkmt[:, :], ynb[:, :],
                             start=False, stop=True)
                    t.matmul(ps_a[:, 256:320], qk3[:, :], ubuf[:, uf:uf + 64],
                             start=True, stop=True)
                    return t.matmul(ps_a[0:64, 320:384], kf[:, qsec:qsec + 64],
                                    zbuf[:, :], start=True, stop=True)
                T(s18)

                def s19(v, h=h):
                    v.tensor_copy(t3b[:, :], ps_a[:, 256:320])
                    v.scalar_tensor_tensor(otile[:, h * 64:(h + 1) * 64],
                                           ps_a[:, 192:256], gamcol[:, 0:1], t3b[:, :],
                                           ALU.mult, ALU.add)
                    return v.scalar_tensor_tensor(st_buf[:, wr + h * 64:wr + (h + 1) * 64],
                                                  st_buf[:, rd + h * 64:rd + (h + 1) * 64],
                                                  gcc[:, 0:1], ps_a[0:64, 320:384],
                                                  ALU.mult, ALU.add)
                V(s19)

            def pr1(t):
                for kc in range(KC):
                    ins = t.transpose(ps_tpb[:, kc * 128:(kc + 1) * 128],
                                      otile[:, kc * 128:(kc + 1) * 128], eyeb[:, :])
                return ins
            T(pr1)

            def pr2(v):
                return v.tensor_copy(oT[:, :].rearrange("p (c2 u) -> p c2 u", c2=KC),
                                     ps_tpb[:, :].rearrange("p (c2 u) -> p c2 u", c2=KC))
            V(pr2)

            for half in range(2):
                def pr3(t, half=half):
                    for kc in range(KC):
                        ins = t.matmul(ps_q[:, half * 512:(half + 1) * 512],
                                       oT[:, kc * 128:(kc + 1) * 128],
                                       wosb[:, kc * 1024 + half * 512:kc * 1024 + (half + 1) * 512],
                                       start=(kc == 0), stop=(kc == KC - 1))
                    return ins
                T(pr3)

                def pr4(v, half=half):
                    if half == 0 and k >= 2:
                        v.wait_ge(sm["yd%d" % (k % 2)], 16 * (k // 2))
                    ins = v.tensor_tensor(
                        ytile[:, (c % 2) * 1024 + half * 512:(c % 2) * 1024 + (half + 1) * 512],
                        ps_q[:, half * 512:(half + 1) * 512],
                        xtile[:, (c % 2) * 1024 + half * 512:(c % 2) * 1024 + (half + 1) * 512],
                        ALU.add)
                    if half == 1:
                        v.sig(sm["yo"], k + 1)
                    return ins
                V(pr4)

        for b in range(BLOC):
            for c in range(NTT):
                chunk_steps(b, c)

        with nc.Block() as blk:
            @blk.sync
            def _(s):
                nrc = [0]

                def cdma(dst, src):
                    s.dma_start(out=dst, in_=src).then_inc(sm["rc"], 16)
                    nrc[0] += 16
                cdma(eyeb[:, :], eyeb_in[:, :])
                cdma(eyef[:, :], eyef_in[:, :])
                cdma(triuI[:, :], triuI_in[:, :])
                cdma(triuS[:, :], triuS_in[:, :])
                cdma(mbI[:, :], mbI_in[:, :])
                cdma(mbS[:, :], mbS_in[:, :])
                cdma(o1x128[0:1, :], o1x128_in[:, :])
                cdma(ocol[:, :], ocol_in[:, :])
                cdma(o1x64[0:1, :], o1x64_in[:, :])
                cdma(bgrow[0:1, 0:16], bg.rearrange("(a j) -> a j", a=1))
                cdma(bgrow[0:1, 16:32], bb.rearrange("(a j) -> a j", a=1))
                cdma(wabf.rearrange("q (kc j) -> q kc j", kc=KC)[:, :, 0:16],
                     WgT.rearrange("(kc q) j -> q kc j", q=128))
                cdma(wabf.rearrange("q (kc j) -> q kc j", kc=KC)[:, :, 16:32],
                     WbT.rearrange("(kc q) j -> q kc j", q=128))
                for b in range(BLOC):
                    for c in range(NTT):
                        k = b * NTT + c
                        if k >= 2:
                            s.wait_ge(sm["yo"], k - 1)
                        s.dma_start(out=xtile[:, (c % 2) * 1024:(c % 2 + 1) * 1024],
                                    in_=xin[k * 128:(k + 1) * 128, :]).then_inc(sm["x%d" % (c % 2)], 16)
                        s.wait_ge(sm["yo"], k + 1)
                        s.dma_start(out=xout[k * 128:(k + 1) * 128, :],
                                    in_=ytile[:, (c % 2) * 1024:(c % 2 + 1) * 1024]).then_inc(sm["yd%d" % (c % 2)], 16)
                s.wait_ge(sm["yd0"], 16 * (BLOC * NTT // 2))
                s.wait_ge(sm["yd1"], 16 * (BLOC * NTT // 2))

            @blk.gpsimd
            def _(g):
                g.dma_start(out=wqsb.rearrange("q (kc c2) -> q kc c2", kc=KC),
                            in_=WqkvT.rearrange("(kc q) c2 -> q kc c2", q=128)).then_inc(sm["w"], 16)
                g.dma_start(out=wosb.rearrange("q (kc c2) -> q kc c2", kc=KC),
                            in_=WoutT.rearrange("(kc q) c2 -> q kc c2", q=128)).then_inc(sm["w"], 16)
                g.wait_ge(sm["w"], 32)

            @blk.tensor
            def _(t):
                t.wait_ge(sm["rc"], 208)
                t.wait_ge(sm["w"], 32)
                he = 0
                for (eng, fn) in STEPS:
                    if eng == "T":
                        t.wait_ge(sm["he"], he)
                        ins = fn(t)
                        ins.then_inc(sm["hm"], 1)
                    else:
                        he += 1

            @blk.scalar
            def _(sc_raw):
                sc = SelfSync(sc_raw, sm["sch"])
                hm = 0
                he = 0
                for (eng, fn) in STEPS:
                    if eng == "T":
                        hm += 1
                    elif eng == "S":
                        sc.wait_ge(sm["hm"], hm)
                        sc.wait_ge(sm["he"], he)
                        fn(sc)
                        sc.nop()
                        sc.sig(sm["he"], he + 1)
                        he += 1
                    else:
                        he += 1

            @blk.vector
            def _(v_raw):
                v = SelfSync(v_raw, sm["vch"])
                v.wait_ge(sm["rc"], 208)
                hm = 0
                he = 0
                for (eng, fn) in STEPS:
                    if eng == "T":
                        hm += 1
                    elif eng == "V":
                        v.wait_ge(sm["hm"], hm)
                        v.wait_ge(sm["he"], he)
                        fn(v)
                        v.nop()
                        v.sig(sm["he"], he + 1)
                        he += 1
                    else:
                        he += 1


def _rotary_tables_np(t, dim):
    thetas = 10000.0 ** (-np.arange(0, dim, 2, dtype=np.float32) / dim)
    ticks = np.outer(np.arange(t, dtype=np.float32), thetas)
    ticks = np.tile(ticks, (1, 2))
    return np.cos(ticks).astype(np.float32), np.sin(ticks).astype(np.float32)


def _launch(build, x, extra, percore=None):
    p = Prog()
    nc = p.nc
    xin = nc.dram_tensor("xin", [NTOK, D], F32, kind="ExternalInput").ap()
    xout = nc.dram_tensor("xout", [NTOK, D], F32, kind="ExternalOutput").ap()
    aps = {}
    for nmm, (shape, arr) in extra.items():
        dt = F32 if arr.dtype == np.float32 else BF16
        aps[nmm] = nc.dram_tensor(nmm, list(shape), dt, kind="ExternalInput").ap()
    if percore:
        for nmm, (shape, arrs) in percore.items():
            aps[nmm] = nc.dram_tensor(nmm, list(shape), F32, kind="ExternalInput").ap()
    build(p, xin, xout, aps)
    base = {nmm: np.ascontiguousarray(arr) for nmm, (shape, arr) in extra.items()}
    base.update(p.extra_inputs)
    in_maps = []
    for c in range(NCORES):
        m = dict(base)
        m["xin"] = np.ascontiguousarray(x[c * BLOC:(c + 1) * BLOC].reshape(NTOK, D))
        if percore:
            for nmm, (shape, arrs) in percore.items():
                m[nmm] = np.ascontiguousarray(arrs[c])
        in_maps.append(m)
    try:
        res = run_bass_kernel_spmd(nc, in_maps, core_ids=list(range(NCORES)))
    except Exception:
        res = run_bass_kernel_spmd(nc, in_maps, core_ids=list(range(NCORES)))
    return np.concatenate([r["xout"].reshape(BLOC, T, D) for r in res.results], axis=0)


def kernel(**inputs):
    g = lambda k: np.asarray(inputs[k], np.float32)
    x = g("L0_attn_a_x")

    def b1(p, xin, xout, aps):
        emit_l0_attn_a(p, xin, xout, aps["wqkv"], aps["bqkv"], aps["wout"], aps["bout"])
    x = _launch(b1, x, {
        "wqkv": ((D, 3 * D), g("L0_attn_a_w_qkv")),
        "bqkv": ((3 * D,), g("L0_attn_a_b_qkv")),
        "wout": ((D, D), g("L0_attn_a_w_out")),
        "bout": ((D,), g("L0_attn_a_b_out")),
    })

    cos, sin = _rotary_tables_np(T, 64)
    ss = sin.copy()
    ss[:, 0:32] = -ss[:, 0:32]
    cosr = np.ascontiguousarray(np.tile(cos, (1, 16)).astype(BFDT))
    sinr = np.ascontiguousarray(np.tile(ss, (1, 16)).astype(BFDT))
    cms = [np.eye(8, dtype=np.float32)[c:c + 1].copy() for c in range(8)]
    pms = []
    for c in range(8):
        pm = np.zeros((8, 1), np.float32)
        pm[0:c, 0] = 1.0
        pms.append(pm)
    f0_extra = {
        "f0g1": ((D,), g("L0_ffn_g1")), "f0g2": ((D,), g("L0_ffn_g2")),
        "f0wq": ((D, D), g("L0_ffn_wq")), "f0wk": ((D, 256), g("L0_ffn_wk")),
        "f0wv": ((D, 256), g("L0_ffn_wv")), "f0wo": ((D, D), g("L0_ffn_wo")),
        "f0rw": ((D, 8), g("L0_ffn_router_w")), "f0rb": ((8,), g("L0_ffn_router_b")),
        "f0w1": ((8, D, 2048), g("L0_ffn_w1")), "f0w2": ((8, D, 2048), g("L0_ffn_w2")),
        "f0w3": ((8, 2048, D), g("L0_ffn_w3")),
        "f0cos": ((T, D), cosr), "f0sin": ((T, D), sinr),
    }
    f0_percore = {"f0cm": ((1, 8), cms), "f0pm": ((8, 1), pms)}

    def b2(p, xin, xout, aps):
        emit_l0_ffn(p, xin, xout, aps["f0g1"], aps["f0g2"], aps["f0wq"], aps["f0wk"],
                    aps["f0wv"], aps["f0wo"], aps["f0rw"], aps["f0rb"], aps["f0w1"],
                    aps["f0w2"], aps["f0w3"], aps["f0cos"], aps["f0sin"],
                    aps["f0cm"], aps["f0pm"])
    x = _launch(b2, x, f0_extra, f0_percore)

    def b3(p, xin, xout, aps):
        emit_l0_attn_b(p, xin, xout, aps["wjT"], aps["wvT"], aps["woT"])
    ab_extra = {
        "wjT": ((D, D), np.ascontiguousarray(g("L0_attn_b_wj_w").T)),
        "wvT": ((D, D), np.ascontiguousarray(g("L0_attn_b_wjv_w").T)),
        "woT": ((D, D), np.ascontiguousarray(g("L0_attn_b_wo_w").T)),
    }
    x = _launch(b3, x, ab_extra)

    x = _launch(b2, x, f0_extra, f0_percore)

    def b5(p, xin, xout, aps):
        emit_l1_attn_a(p, xin, xout, aps["laq"], aps["lab"], aps["lal"], aps["lalb"],
                       aps["lap"], aps["lapb"])
    x = _launch(b5, x, {
        "laq": ((D, 3 * D), np.ascontiguousarray(g("L1_attn_a_W_qkv").T)),
        "lab": ((128, 24), np.ascontiguousarray(g("L1_attn_a_b_qkv").reshape(24, 128).T)),
        "lal": ((64, 64), np.ascontiguousarray(g("L1_attn_a_W_lat").T)),
        "lalb": ((64,), g("L1_attn_a_b_lat")),
        "lap": ((D, D), np.ascontiguousarray(g("L1_attn_a_W_proj").T)),
        "lapb": ((D,), g("L1_attn_a_b_proj")),
    })

    f1_extra = {
        "f1wg": ((D, 8), np.ascontiguousarray(g("L1_ffn_wg").T)),
        "f1w1T": ((8, D, 2048), np.ascontiguousarray(g("L1_ffn_w1").transpose(0, 2, 1))),
        "f1w3T": ((8, D, 2048), np.ascontiguousarray(g("L1_ffn_w3").transpose(0, 2, 1))),
        "f1w2T": ((8, 2048, D), np.ascontiguousarray(g("L1_ffn_w2").transpose(0, 2, 1))),
    }

    db_extra = {
        "dbq": ((D, 3 * D), np.ascontiguousarray(g("L1_attn_b_Wqkv").T)),
        "dbg": ((D, 16), np.ascontiguousarray(g("L1_attn_b_Wg").T)),
        "dbbg": ((16,), g("L1_attn_b_bg")),
        "dbb": ((D, 16), np.ascontiguousarray(g("L1_attn_b_Wb").T)),
        "dbbb": ((16,), g("L1_attn_b_bb")),
        "dbo": ((D, D), np.ascontiguousarray(g("L1_attn_b_Wout").T)),
    }
    merged = dict(f1_extra)
    merged.update(db_extra)

    def b678(p, xin, xout, aps):
        ncl = p.nc
        m1 = ncl.dram_tensor("fuse_m1", [NTOK, D], F32).ap()
        m2 = ncl.dram_tensor("fuse_m2", [NTOK, D], F32).ap()
        emit_l1_ffn(p, xin, m1, aps["f1wg"], aps["f1w1T"], aps["f1w3T"],
                    aps["f1w2T"], tag="f1a")
        ncl.all_engine_barrier()
        emit_l1_attn_b(p, m1, m2, aps["dbq"], aps["dbg"], aps["dbbg"], aps["dbb"],
                       aps["dbbb"], aps["dbo"])
        ncl.all_engine_barrier()
        emit_l1_ffn(p, m2, xout, aps["f1wg"], aps["f1w1T"], aps["f1w3T"],
                    aps["f1w2T"], tag="f1b")
    x = _launch(b678, x, merged)
    return np.ascontiguousarray(x, np.float32)

